# version 16
# baseline (speedup 1.0000x reference)
import numpy as np
from contextlib import ExitStack

import concourse.bass as bass
import concourse.mybir as mybir
from concourse.bass_utils import run_bass_kernel_spmd

F32 = mybir.dt.float32
BF16 = mybir.dt.bfloat16
I8 = mybir.dt.int8
U16 = mybir.dt.uint16
AF = mybir.ActivationFunctionType
ALU = mybir.AluOpType
AX = mybir.AxisListType.X

P = 128
D = 2048
DC = D // P
T = 1024
NCH = T // P
H = 32
HD = 64
PAIR = 16
IO = 5504
IOC = IO // P
NTILE = 4
LN_EPS = 1e-5
STATE_CHUNKS = 8
SKIP_Y = False
SKIP_ST = False
SKIP_MASK = False
ST_HEADS = 3
DEBUG_TAPS = False
ATT_EPS = 1e-12

ENGINES = ("sync", "scalar", "vector", "tensor", "gpsimd")


class SemCtr:
    __slots__ = ("h", "n")

    def __init__(self, h):
        self.h = h
        self.n = 0


class Prog:
    def __init__(self):
        self.ops = {e: [] for e in ENGINES}

    def emit(self, eng, fn, incs=()):
        lowered = []
        for s, a in incs:
            s.n += a
            lowered.append((s.h, a))
        self.ops[eng].append((fn, tuple(lowered)))

    def wait(self, eng, s, val=None):
        v = s.n if val is None else val
        if v <= 0:
            return
        self.ops[eng].append((lambda e, h=s.h, vv=v: e.wait_ge(h, vv), ()))

    def install(self, block):
        def runner(ops):
            def body(e):
                for fn, incs in ops:
                    ins = fn(e)
                    for h, a in incs:
                        ins = ins.then_inc(h, a)
            return body
        block.sync(runner(self.ops["sync"]))
        block.scalar(runner(self.ops["scalar"]))
        block.vector(runner(self.ops["vector"]))
        block.tensor(runner(self.ops["tensor"]))
        block.gpsimd(runner(self.ops["gpsimd"]))


def build_program(limit=99, skip_gather=False):
    nc = bass.Bass("TRN2", target_bir_lowering=False)

    x_own = nc.dram_tensor("x_own", [T, D], F32, kind="ExternalInput").ap()
    x_pre = nc.dram_tensor("x_pre", [3 * T, D], F32, kind="ExternalInput").ap()
    wqT = nc.dram_tensor("wqT", [D, D], F32, kind="ExternalInput").ap()
    wkT = nc.dram_tensor("wkT", [D, D], F32, kind="ExternalInput").ap()
    wvT = nc.dram_tensor("wvT", [D, D], F32, kind="ExternalInput").ap()
    woT = nc.dram_tensor("woT", [D, D], F32, kind="ExternalInput").ap()
    wfq_in = nc.dram_tensor("wfq_bd", [P, PAIR * P], F32, kind="ExternalInput").ap()
    wfk_in = nc.dram_tensor("wfk_bd", [P, PAIR * P], F32, kind="ExternalInput").ap()
    lut_in = nc.dram_tensor("lutbc", [P, 16], F32, kind="ExternalInput").ap()
    g8 = nc.dram_tensor("g8", [D, IO], I8, kind="ExternalInput").ap()
    u8 = nc.dram_tensor("u8", [D, IO], I8, kind="ExternalInput").ap()
    d8 = nc.dram_tensor("d8", [IO, D], I8, kind="ExternalInput").ap()
    utri_in = nc.dram_tensor("utri", [P, 4 * P], F32, kind="ExternalInput").ap()
    ones_in = nc.dram_tensor("ones_in", [P, 1], F32, kind="ExternalInput").ap()
    cvec_in = nc.dram_tensor("cvec_in", [P, 2], F32, kind="ExternalInput").ap()
    ident_in = nc.dram_tensor("ident_in", [P, P], F32, kind="ExternalInput").ap()
    pmask_in = nc.dram_tensor("pmask_in", [P, NTILE], F32, kind="ExternalInput").ap()
    yout = nc.dram_tensor("yout", [T, D], F32, kind="ExternalOutput").ap()

    kd = "ExternalOutput" if DEBUG_TAPS else "Internal"
    wg_s = nc.dram_tensor("wg_s", [DC, P, IO], BF16, kind=kd).ap()
    wu_s = nc.dram_tensor("wu_s", [DC, P, IO], BF16, kind=kd).ap()
    wd_s = nc.dram_tensor("wd_s", [IOC, P, D], BF16, kind=kd).ap()
    xh_d = nc.dram_tensor("xh_d", [NTILE, P, DC * T], BF16, kind=kd).ap()
    fk_d = nc.dram_tensor("fk_d", [NTILE, NCH, P, D], BF16, kind=kd).ap()
    hm_d = nc.dram_tensor("hm_d", [2, IOC, P, 512], BF16, kind="Internal").ap()

    pr = Prog()

    with ExitStack() as ctx:
        def sb(name, shape, dtype, st=None):
            return (st or ctx).enter_context(nc.sbuf_tensor(name, shape, dtype))

        s_att_psum = ctx.enter_context(ExitStack())

        def psum(name, shape, dtype=F32):
            return s_att_psum.enter_context(nc.psum_tensor(name, shape, dtype))

        def sem(name):
            return SemCtr(ctx.enter_context(nc.semaphore(name)))

        eP = sem("eP")
        eA = sem("eA")
        eV = sem("eV")
        sC = sem("sC")
        sC2 = sem("sC2")
        sXW = [sem("sXWa")]
        sXH = sem("sXH")
        sWR = sem("sWR")
        sFK = [sem("sFK0"), sem("sFK1")]
        sGB = [sem(f"sGB{i}") for i in range(4)]
        sGW = [sem("sGW0"), sem("sGW1")]
        sWT = [sem(f"sWT{i}") for i in range(8)]
        sOT = [sem("sOT0"), sem("sOT1")]
        sHM = sem("sHM")
        eG = sem("eG")
        sPL = [sem("sPL0"), sem("sPL1")]
        sPS = [sem("sPS0"), sem("sPS1")]
        eSt = sem("eSt")

        c_utri = sb("c_utri", [P, 4 * P], BF16)
        c_wfq = sb("c_wfq", [P, PAIR * P], BF16)
        c_wfk = sb("c_wfk", [P, PAIR * P], BF16)
        c_ones = sb("c_ones", [P, 1], BF16)
        c_id = sb("c_id", [P, P], BF16)
        c_pmask = sb("c_pmask", [P, NTILE], F32)
        c_cvec = sb("c_cvec", [P, 2], F32)
        c_lutb = sb("c_lutb", [P, 16], BF16)
        c_lutf = sb("c_lutf", [P, 16], F32)

        st_KV = sb("st_KV", [P, PAIR * HD], F32)
        st_KVb = sb("st_KVb", [P, PAIR * HD], BF16)
        st_k = sb("st_k", [P, PAIR], F32)
        st_ke = sb("st_ke", [P, PAIR], BF16)

        x_win = [sb("x_win0", [P, D], F32)]
        GW = 1024
        gw_bf = [sb(f"gw_bf{i}", [P, GW], BF16) for i in range(4)]
        gw_t = sb("gw_t", [P, GW], BF16)
        gw_w = [sb(f"gw_w{i}", [P, GW], BF16) for i in range(2)]
        PW = 512
        A_xh = sb("A_xh", [P, DC * T], BF16)
        A_k = sb("A_k", [P, DC * T], BF16)
        fk_scr = [sb(f"fk_scr{i}", [P, D], BF16) for i in range(2)]

        s_sum = sb("s_sum", [P, 1], F32)
        s_sq = sb("s_sq", [P, 1], F32)
        s_m = sb("s_m", [P, 1], F32)
        s_v = sb("s_v", [P, 1], F32)
        s_t1 = sb("s_t1", [P, 1], F32)
        s_a = sb("s_a", [P, 1], F32)
        s_b = sb("s_b", [P, 1], F32)
        s_sk = sb("s_sk", [P, H], F32)
        s_rsk = sb("s_rsk", [P, H], F32)
        s_rd = sb("s_rd", [P, PAIR], F32)

        pa = [psum(f"pa{i}", [P, 512], F32) for i in range(4)]
        pb = [psum(f"pb{i}", [P, 512], F32) for i in range(2)]
        pt = psum("pt", [P, PAIR * P], BF16)

        def dma(q, s, out, in_):
            pr.emit(q, lambda e, o=out, i=in_: e.dma_start(out=o, in_=i), [(s, 16)])
            return s.n

        def mm(lhsT, rhs, out, start, stop, mark=0):
            incs = [(eP, mark)] if mark else ()
            pr.emit("tensor",
                    lambda e, a=lhsT, b=rhs, o=out, s1=start, s2=stop:
                    e.matmul(o, a, b, start=s1, stop=s2), incs)

        def tr(in_, out, mark=0):
            incs = [(eP, mark)] if mark else ()
            pr.emit("tensor",
                    lambda e, i=in_, o=out: e.transpose(o, i, c_id[:, :]), incs)

        def act(out, in_, func, bias=0.0, scale=1.0, accum=None, mark=0):
            incs = [(eA, mark)] if mark else ()
            pr.emit("scalar",
                    lambda e, o=out, i=in_, f=func, b=bias, s=scale, ao=accum:
                    e.activation(o, i, f, bias=b, scale=s, accum_out=ao), incs)

        def v_op(fn, mark=0):
            incs = [(eV, mark)] if mark else ()
            pr.emit("vector", fn, incs)

        def wait_tok(eng, tok):
            pr.wait(eng, tok[0], tok[1])

        pa_free = [(eV, 0)] * 4
        pb_free = [(eV, 0)] * 2
        pt_free = (eV, 0)

        def state_add_ops(slot, masked_tile=None):
            for i in range(4):
                for s in range(2):
                    lo = s * 64
                    src = pa[i][:, :].rearrange("p (a q) -> p a q", q=P)[lo:lo + 64, :, lo:lo + 64]
                    dst = st_KV[lo:lo + 64, 4 * i * HD:(4 * i + 4) * HD].rearrange(
                        "p (a f) -> p a f", f=HD)
                    if masked_tile is None:
                        v_op(lambda e, d=dst, s2=src: e.tensor_tensor(d, d, s2, ALU.add))
                    else:
                        tcol = c_pmask[:, masked_tile:masked_tile + 1][lo:lo + 64, :]
                        v_op(lambda e, d=dst, s2=src, tc=tcol: e.scalar_tensor_tensor(
                            d, s2, tc, d, ALU.mult, ALU.add))
            for i in range(2):
                dst = st_k[:, 8 * i:8 * i + 8]
                src = pb[i][:, 0:8]
                if masked_tile is None:
                    v_op(lambda e, d=dst, s2=src: e.tensor_tensor(d, d, s2, ALU.add),
                         mark=1 if i == 1 else 0)
                else:
                    tcol = c_pmask[:, masked_tile:masked_tile + 1]
                    v_op(lambda e, d=dst, s2=src, tc=tcol: e.scalar_tensor_tensor(
                        d, s2, tc, d, ALU.mult, ALU.add), mark=1 if i == 1 else 0)

        dma("gpsimd", sC, c_utri[:, :], utri_in)
        dma("gpsimd", sC, c_wfq[:, :], wfq_in)
        dma("gpsimd", sC, c_wfk[:, :], wfk_in)
        dma("gpsimd", sC, c_ones[:, :], ones_in)
        dma("gpsimd", sC, c_id[:, :], ident_in)
        dma("gpsimd", sC, c_lutb[:, :], lut_in)
        dma("sync", sC2, c_lutf[:, :], lut_in)
        dma("sync", sC2, c_cvec[:, :], cvec_in)
        dma("sync", sC2, c_pmask[:, :], pmask_in)

        pr.wait("vector", sC)
        pr.wait("vector", sC2)
        v_op(lambda e: e.memset(st_KV[:, :], 0.0))
        v_op(lambda e: e.memset(st_k[:, :], 0.0))
        v_op(lambda e: e.memset(st_KVb[:, :], 0.0))
        v_op(lambda e: e.memset(st_ke[:, :], float(ATT_EPS)), mark=1)
        pr.wait("tensor", sC)
        pr.wait("scalar", sC)
        pr.wait("scalar", sC2)

        POOL_DOWN_CHUNKS = 30
        gjobs = []
        pjobs = []
        for gsrc, gdst, nk, width in (() if skip_gather else ((g8, wg_s, DC, IO), (u8, wu_s, DC, IO), (d8, wd_s, IOC, D))):
            for k in range(nk):
                pool = False
                step = PW if pool else GW
                c0 = 0
                while c0 < width:
                    f = min(step, width - c0)
                    (pjobs if pool else gjobs).append((gsrc, gdst, k, c0, f))
                    c0 += f

        pr.wait("vector", sC)
        pr.wait("vector", sC2)
        pr.wait("gpsimd", sC2)

        def _gather_gen():
            NS = 4
            gw_rd_tok = [(eV, 0)] * NS

            def issue(n):
                gsrc_, gdst_, k_, c0_, f_ = gjobs[n]
                b_ = n % NS
                if n >= NS:
                    wait_tok("gpsimd", gw_rd_tok[b_])
                dma("gpsimd", sGB[b_], gw_bf[b_][:, 0:f_],
                    gsrc_[k_ * P:(k_ + 1) * P, c0_:c0_ + f_])

            for n in range(min(3, len(gjobs))):
                issue(n)
            for n, (gsrc, gdst, k, c0, f) in enumerate(gjobs):
                if n + 3 < len(gjobs):
                    issue(n + 3)
                b = n % NS
                w = n % 2
                pr.wait("vector", sGB[b])
                if n >= 2:
                    pr.wait("vector", sGW[w])
                for j in range(16):
                    if j == 0:
                        v_op(lambda e, b=b, w=w, f=f: e.tensor_scalar(
                            gw_w[w][:, 0:f], gw_bf[b][:, 0:f], 0.0, c_lutf[:, 0:1],
                            ALU.is_equal, ALU.mult))
                    else:
                        v_op(lambda e, b=b, f=f, j=j: e.tensor_scalar(
                            gw_t[:, 0:f], gw_bf[b][:, 0:f], float(j), c_lutf[:, j:j + 1],
                            ALU.is_equal, ALU.mult))
                        mk = 1 if j == 15 else 0
                        v_op(lambda e, w=w, f=f: e.tensor_tensor(
                            gw_w[w][:, 0:f], gw_w[w][:, 0:f], gw_t[:, 0:f], ALU.add),
                             mark=mk)
                    yield
                gw_rd_tok[b] = (eV, eV.n)
                pr.wait("sync", eV, eV.n)
                dma("sync", sGW[w], gdst[k, :, c0:c0 + f], gw_w[w][:, 0:f])
                if n == len(gjobs) - 1 or gjobs[n + 1][1] is not gdst:
                    gather_mat_tok[id(gdst)] = (sGW[0].n, sGW[1].n)
                if gdst is wd_s and (n == len(gjobs) - 1 or gjobs[n + 1][2] != k):
                    gather_tile_tok[k] = (sGW[0].n, sGW[1].n)
                yield

        def _pool_gen():
            for n, (gsrc, gdst, k, c0, f) in enumerate(pjobs):
                b = n % 2
                pr.wait("gpsimd", sPS[b])
                dma("gpsimd", sPL[b], pw_bf[b][:, 0:f], gsrc[k * P:(k + 1) * P, c0:c0 + f])
                pr.wait("gpsimd", sPL[b])
                for j in range(16):
                    if j == 0:
                        pr.emit("gpsimd", lambda e, b=b, f=f: e.tensor_scalar(
                            pw_w[b][:, 0:f], pw_bf[b][:, 0:f], 0.0, c_lutf[:, 0:1],
                            ALU.is_equal, ALU.mult))
                    else:
                        pr.emit("gpsimd", lambda e, b=b, f=f, j=j: e.tensor_scalar(
                            pw_t[:, 0:f], pw_bf[b][:, 0:f], float(j), c_lutf[:, j:j + 1],
                            ALU.is_equal, ALU.mult))
                        pr.emit("gpsimd", lambda e, b=b, f=f: e.tensor_tensor(
                            pw_w[b][:, 0:f], pw_w[b][:, 0:f], pw_t[:, 0:f], ALU.add))
                    yield
                dma("gpsimd", sPS[b], gdst[k, :, c0:c0 + f], pw_w[b][:, 0:f])
                yield

        _pg = _pool_gen()
        _pg_done = [False]

        def pump_pool(n):
            if _pg_done[0]:
                return
            try:
                for _ in range(n):
                    next(_pg)
            except StopIteration:
                _pg_done[0] = True

        def pump_pool_all():
            while not _pg_done[0]:
                pump_pool(1000)

        _gg = _gather_gen()
        _gg_done = [False]
        gather_mat_tok = {}
        gather_tile_tok = {}

        def pump(n):
            pump_pool(max(1, n // 3))
            if _gg_done[0]:
                return
            try:
                for _ in range(n):
                    next(_gg)
            except StopIteration:
                _gg_done[0] = True

        def pump_all():
            while not _gg_done[0]:
                pump(1000)
            pump_pool_all()



        def _early_out():
            pump_all()
            pump_pool_all()
            pr.wait("sync", eV, eV.n)
            pr.wait("sync", eA, eA.n)
            dma("sync", sOT[0], yout[0:P, :], x_win[0][:, :])
            for s_ in (sOT[0], sGW[0], sGW[1], sXH, sFK[0], sFK[1]):
                pr.wait("sync", s_)
            block = ctx.enter_context(nc.Block())
            pr.install(block)

        if limit < 1:
            _early_out()
            return nc

        def x_src(t, c):
            if t == 3:
                return x_own[c * P:(c + 1) * P, :]
            return x_pre[t * T + c * P:t * T + (c + 1) * P, :]

        fk_used = [(eP, 0), (eP, 0)]
        x_user = [(eA, 0)]
        tok_kproj = [0] * NTILE
        tok_zk_last = 0

        with ExitStack() as s_p1:
            ln_dump = sb("ln_dump", [P, D], F32, s_p1)
            tm_scr = sb("tm_scr", [P, D], BF16, s_p1)
            W_res = sb("W_res", [P, DC * D], BF16, s_p1)
            for dc in range(DC):
                for sc in range(4):
                    dma("gpsimd", sWR, W_res[:, dc * D + sc * 512:dc * D + (sc + 1) * 512],
                        wkT[dc * P:(dc + 1) * P, sc * 512:(sc + 1) * 512])

            for t in range(NTILE):
                for c in range(NCH):
                    xw = x_win[c % 1]
                    sxw = sXW[c % 1]
                    wait_tok("sync", x_user[c % 1])
                    if t > 0 and c == 0:
                        pr.wait("sync", eP, tok_kproj[t - 1])
                        pr.wait("sync", sXH)
                    dma("sync", sxw, xw[:, :], x_src(t, c))
                    pr.wait("scalar", sxw)
                    act(ln_dump[:, :], xw[:, :], AF.Identity, accum=s_sum[:, :], mark=1)
                    act(ln_dump[:, :], xw[:, :], AF.Square, accum=s_sq[:, :], mark=1)
                    pr.wait("vector", eA, eA.n)
                    v_op(lambda e: e.tensor_scalar(s_m[:, :], s_sum[:, :], 1.0 / D, None, ALU.mult),
                         mark=1)
                    v_op(lambda e: e.tensor_scalar(s_t1[:, :], s_sq[:, :], 1.0 / D, None, ALU.mult),
                         mark=1)
                    pr.wait("vector", eV, eV.n)
                    v_op(lambda e: e.tensor_tensor(s_v[:, :], s_m[:, :], s_m[:, :], ALU.mult), mark=1)
                    pr.wait("vector", eV, eV.n)
                    v_op(lambda e: e.tensor_tensor(s_v[:, :], s_t1[:, :], s_v[:, :], ALU.subtract),
                         mark=1)
                    pr.wait("scalar", eV, eV.n)
                    act(s_t1[:, :], s_v[:, :], AF.Ln, bias=c_cvec[:, 0:1], mark=1)
                    pr.wait("scalar", eA, eA.n)
                    act(s_a[:, :], s_t1[:, :], AF.Exp, scale=-0.5, mark=1)
                    pr.wait("vector", eA, eA.n)
                    v_op(lambda e: e.scalar_tensor_tensor(
                        s_b[:, :], s_m[:, :], -1.0, s_a[:, :], ALU.mult, ALU.mult), mark=1)
                    pr.wait("scalar", eV, eV.n)
                    pr.wait("scalar", eA, eA.n)
                    act(tm_scr[:, :], xw[:, :], AF.Identity, bias=s_b[:, :], scale=s_a[:, :],
                        mark=1)
                    x_user[c % 1] = (eA, eA.n)
                    tok_apply = eA.n
                    wait_tok("tensor", pt_free)
                    pr.wait("tensor", eA, tok_apply)
                    for dc in range(DC):
                        tr(tm_scr[:, dc * P:(dc + 1) * P], pt[:, dc * P:(dc + 1) * P],
                           mark=1 if dc == DC - 1 else 0)
                    pr.wait("scalar", eP, eP.n)
                    act(A_xh[:, :].rearrange("p (d t) -> p d t", t=T)[:, :, c * P:(c + 1) * P],
                        pt[:, :].rearrange("p (d q) -> p d q", q=P), AF.Identity, mark=1)
                    pt_free = (eA, eA.n)
                    pump(12)
                tok_xh = eA.n

                pr.wait("sync", eA, tok_xh)
                dma("sync", sXH, xh_d[t, :, :], A_xh[:, :])

                pr.wait("tensor", sWR)
                pr.wait("tensor", eA, tok_xh)
                if t > 0:
                    pr.wait("scalar", eP, tok_zk_last)
                pp = 0
                TH = T // 2
                for oc in range(DC):
                    for th in range(2):
                        bank = pa[pp]
                        wait_tok("tensor", pa_free[pp])
                        for dc in range(DC):
                            mm(W_res[:, dc * D + oc * P: dc * D + (oc + 1) * P],
                               A_xh[:, dc * T + th * TH: dc * T + (th + 1) * TH],
                               bank[:, 0:TH], start=(dc == 0), stop=(dc == DC - 1),
                               mark=1 if dc == DC - 1 else 0)
                        tok_mm = eP.n
                        pr.wait("scalar", eP, tok_mm)
                        act(A_k[:, oc * T + th * TH: oc * T + (th + 1) * TH], bank[:, 0:TH],
                            AF.Identity, mark=1)
                        pa_free[pp] = (eA, eA.n)
                        pp = (pp + 1) % 2
                        pump(6)
                tok_kproj[t] = eP.n
                tok_kfm = eA.n

                for c in range(NCH):
                    slot = c % 2
                    pr.wait("tensor", eA, tok_kfm)
                    for i in range(4):
                        wait_tok("tensor", pa_free[i])
                    for p in range(PAIR):
                        mm(A_k[:, p * T + c * P:p * T + (c + 1) * P],
                           c_wfk[:, p * P:(p + 1) * P],
                           pa[p // 4][:, (p % 4) * P:(p % 4 + 1) * P],
                           start=True, stop=True, mark=1 if p == PAIR - 1 else 0)
                    tok_zk = eP.n
                    tok_zk_last = tok_zk
                    pr.wait("scalar", eP, tok_zk)
                    pr.wait("scalar", sFK[slot])
                    wait_tok("scalar", fk_used[slot])
                    for i in range(4):
                        act(fk_scr[slot][:, i * 512:(i + 1) * 512], pa[i][:, :], AF.Exp,
                            mark=1 if i == 3 else 0)
                    tok_exp = eA.n
                    for i in range(4):
                        pa_free[i] = (eA, tok_exp)
                    pr.wait("vector", eA, tok_exp)
                    v_op(lambda e, s=slot: e.tensor_reduce(
                        s_sk[:, :], fk_scr[s][:, :].rearrange("p (h f) -> p h f", f=HD),
                        AX, ALU.add), mark=1)
                    pr.wait("vector", eV, eV.n)
                    v_op(lambda e: e.reciprocal(s_rsk[:, :], s_sk[:, :]), mark=1)
                    pr.wait("vector", eV, eV.n)
                    v_op(lambda e, t=t: e.tensor_scalar(
                        s_rsk[:, :], s_rsk[:, :], c_pmask[:, t:t + 1], None, ALU.mult), mark=1)
                    pr.wait("vector", eV, eV.n)
                    v_op(lambda e, s=slot: e.tensor_tensor(
                        fk_scr[s][:, :].rearrange("p (h f) -> p h f", f=HD),
                        fk_scr[s][:, :].rearrange("p (h f) -> p h f", f=HD),
                        s_rsk[:, :].rearrange("p (h o) -> p h o", o=1).broadcast_to([P, H, HD]),
                        ALU.mult), mark=1)
                    pr.wait("sync", eV, eV.n)
                    dma("sync", sFK[slot], fk_d[t, c, :, :], fk_scr[slot][:, :])
                    pump(12)

        if limit < 2:
            _early_out()
            return nc
        tok_vproj_last = 0
        with ExitStack() as s_p2:
            W_res = sb("W_res2", [P, DC * D], BF16, s_p2)
            pr.wait("gpsimd", eP, eP.n)
            for dc in range(DC):
                for sc in range(4):
                    dma("gpsimd", sWR, W_res[:, dc * D + sc * 512:dc * D + (sc + 1) * 512],
                        wvT[dc * P:(dc + 1) * P, sc * 512:(sc + 1) * 512])

            tok_zk_guard = tok_zk_last
            for t in range(NTILE):
                pr.wait("sync", eP, eP.n)
                pr.wait("sync", sXH)
                dma("sync", sXH, A_xh[:, :], xh_d[t, :, :])
                pr.wait("tensor", sXH)
                pr.wait("tensor", sWR)
                if t == 0:
                    pr.wait("scalar", eP, tok_zk_guard)
                pp = 0
                for tb in range(NCH):
                    for og in range(4):
                        bank = pa[pp]
                        wait_tok("tensor", pa_free[pp])
                        for dc in range(DC):
                            mm(A_xh[:, dc * T + tb * P: dc * T + (tb + 1) * P],
                               W_res[:, dc * D + og * 512: dc * D + (og + 1) * 512],
                               bank[:, :], start=(dc == 0), stop=(dc == DC - 1),
                               mark=1 if dc == DC - 1 else 0)
                        tok_mm = eP.n
                        pr.wait("scalar", eP, tok_mm)
                        act(A_k[:, tb * D + og * 512: tb * D + (og + 1) * 512], bank[:, :],
                            AF.Identity, mark=1)
                        pa_free[pp] = (eA, eA.n)
                        pp = (pp + 1) % 2
                        pump(4)
                tok_vproj_last = eP.n
                tok_vtm = eA.n

                if t == 3:
                    break

                for c in range(NCH):
                    slot = c % 2
                    wait_tok("sync", fk_used[slot])
                    pr.wait("sync", eV, eV.n)
                    dma("sync", sFK[slot], fk_scr[slot][:, :], fk_d[t, c, :, :])
                    pr.wait("tensor", sFK[slot])
                    pr.wait("tensor", eA, tok_vtm)
                    for i in range(4):
                        wait_tok("tensor", pa_free[i])
                    for i in range(2):
                        wait_tok("tensor", pb_free[i])
                    for p in range(PAIR):
                        mm(fk_scr[slot][:, p * P:(p + 1) * P],
                           A_k[:, c * D + p * P:c * D + (p + 1) * P],
                           pa[p // 4][:, (p % 4) * P:(p % 4 + 1) * P],
                           start=True, stop=True)
                        mm(fk_scr[slot][:, p * P:(p + 1) * P], c_ones[:, :],
                           pb[p // 8][:, (p % 8):(p % 8) + 1],
                           start=True, stop=True, mark=1 if p == PAIR - 1 else 0)
                    tok_dmm = eP.n
                    fk_used[slot] = (eP, tok_dmm)
                    pr.wait("vector", eP, tok_dmm)
                    state_add_ops(slot, masked_tile=t)
                    st_tok = eV.n
                    for i in range(4):
                        pa_free[i] = (eV, st_tok)
                    for i in range(2):
                        pb_free[i] = (eV, st_tok)
                    pump(12)

            v_op(lambda e: e.tensor_copy(st_KVb[:, :], st_KV[:, :]))
            v_op(lambda e: e.tensor_scalar(st_ke[:, :], st_k[:, :], float(ATT_EPS), None,
                                           ALU.add), mark=1)
            tok_state = eV.n

        if limit < 3:
            _early_out()
            return nc
        tok_scan = 0
        with ExitStack() as s_aq:
            A_q = sb("A_q", [P, DC * T], BF16, s_aq)

            with ExitStack() as s_q:
                W_res = sb("W_res3", [P, DC * D], BF16, s_q)
                pr.wait("gpsimd", eP, eP.n)
                for dc in range(DC):
                    for sc in range(4):
                        dma("gpsimd", sWR, W_res[:, dc * D + sc * 512:dc * D + (sc + 1) * 512],
                            wqT[dc * P:(dc + 1) * P, sc * 512:(sc + 1) * 512])
                pr.wait("sync", eP, eP.n)
                dma("sync", sXH, A_xh[:, :], xh_d[3, :, :])
                pr.wait("tensor", sXH)
                pr.wait("tensor", sWR)
                pp = 0
                TH = T // 2
                for oc in range(DC):
                    for th in range(2):
                        bank = pa[pp]
                        wait_tok("tensor", pa_free[pp])
                        for dc in range(DC):
                            mm(W_res[:, dc * D + oc * P: dc * D + (oc + 1) * P],
                               A_xh[:, dc * T + th * TH: dc * T + (th + 1) * TH],
                               bank[:, 0:TH], start=(dc == 0), stop=(dc == DC - 1),
                               mark=1 if dc == DC - 1 else 0)
                        tok_mm = eP.n
                        pr.wait("scalar", eP, tok_mm)
                        act(A_q[:, oc * T + th * TH: oc * T + (th + 1) * TH], bank[:, 0:TH],
                            AF.Identity, mark=1)
                        pa_free[pp] = (eA, eA.n)
                        pp = (pp + 1) % 2
                        pump(8)
                tok_qproj = eP.n
                tok_qfm = eA.n
            if limit < 2.4:
                _early_out()
                return nc

            with ExitStack() as s_scan:
                fq_bd = sb("fq_bd", [P, 2 * PAIR * P], BF16, s_scan)
                fkf_scr = sb("fkf_scr", [P, PAIR * P], BF16, s_scan)
                stm_scr = sb("stm_scr", [P, 16 * P], BF16, s_scan)
                ytmp = sb("ytmp", [P, 1024], BF16, s_scan)

                pr.wait("tensor", eA, tok_qfm)
                pr.wait("tensor", eV, tok_state)
                pr.wait("vector", eP, tok_qproj)
                v_op(lambda e: e.memset(fq_bd[:, :], 0.0), mark=1)
                pr.wait("scalar", eV, eV.n)
                kvb_tok = (eV, tok_state)
                ytok = 0
                for c in range(NCH):
                    slot = c % 2
                    wait_tok("sync", fk_used[slot])
                    pr.wait("sync", eV, eV.n)
                    dma("sync", sFK[slot], fk_scr[slot][:, :], fk_d[3, c, :, :])

                    wait_tok("tensor", kvb_tok)
                    for i in range(4):
                        wait_tok("tensor", pa_free[i])
                    for p in range(PAIR):
                        mm(c_wfq[:, p * P:(p + 1) * P],
                           A_q[:, p * T + c * P:p * T + (c + 1) * P],
                           pa[p // 4][:, (p % 4) * P:(p % 4 + 1) * P],
                           start=True, stop=True, mark=1 if p == PAIR - 1 else 0)
                    tok_zq = eP.n
                    pr.wait("scalar", eP, tok_zq)
                    for i in range(4):
                        act(fq_bd[0:64, i * 1024:(i + 1) * 1024].rearrange(
                                "p (a c) -> p a c", c=256)[:, :, 0:128],
                            pa[i][0:64, :].rearrange("p (a c) -> p a c", c=128),
                            AF.Exp)
                        act(fq_bd[64:128, i * 1024:(i + 1) * 1024].rearrange(
                                "p (a c) -> p a c", c=256)[:, :, 128:256],
                            pa[i][64:128, :].rearrange("p (a c) -> p a c", c=128),
                            AF.Exp, mark=1 if i == 3 else 0)
                    tok_fq = eA.n
                    for i in range(4):
                        pa_free[i] = (eA, tok_fq)
                    if limit < 2.5:
                        continue

                    wait_tok("tensor", pt_free)
                    pr.wait("tensor", sFK[slot])
                    for p in range(PAIR):
                        tr(fk_scr[slot][:, p * P:(p + 1) * P], pt[:, p * P:(p + 1) * P],
                           mark=1 if p == PAIR - 1 else 0)
                    tok_fkT = eP.n
                    pr.wait("scalar", eP, tok_fkT)
                    act(fkf_scr[:, 0:1024], pt[:, 0:1024], AF.Identity)
                    act(fkf_scr[:, 1024:2048], pt[:, 1024:2048], AF.Identity, mark=1)
                    tok_fkf = eA.n
                    pt_free = (eA, tok_fkf)
                    if limit < 2.6:
                        continue

                    for g in range(0 if SKIP_ST else 2):
                        h0 = g * 16
                        pr.wait("tensor", eA, tok_fkf)
                        for i in range(4):
                            wait_tok("tensor", pa_free[i])
                        for lp in range(8):
                            p_ = g * 8 + lp
                            mm(fkf_scr[:, p_ * P:(p_ + 1) * P],
                               fq_bd[:, p_ * 256:(p_ + 1) * 256],
                               pa[lp // 2][:, (lp % 2) * 256:(lp % 2 + 1) * 256],
                               start=True, stop=True, mark=1 if lp == 7 else 0)
                        tok_st = eP.n
                        if SKIP_MASK:
                            for i in range(4):
                                pa_free[i] = (eP, tok_st)
                            continue
                        pump(8)
                        pr.wait("vector", eP, tok_st)
                        for i in range(4):
                            v_op(lambda e, i=i: e.tensor_tensor(
                                stm_scr[:, i * 512:(i + 1) * 512], pa[i][:, :],
                                c_utri[:, :], ALU.mult), mark=1 if i == 3 else 0)
                        tok_stm = eV.n
                        for i in range(4):
                            pa_free[i] = (eV, tok_stm)
                        if limit < 2.7:
                            continue

                        if SKIP_Y:
                            continue
                        pr.wait("tensor", eV, tok_stm)
                        for i in range(2):
                            wait_tok("tensor", pb_free[i])
                        wait_tok("tensor", pa_free[0])
                        for i in range(16):
                            h = h0 + i
                            p_, s_ = h // 2, h % 2
                            fqb = fq_bd[:, p_ * 256 + s_ * 128: p_ * 256 + (s_ + 1) * 128]
                            mm(stm_scr[:, i * P:(i + 1) * P],
                               A_k[:, c * D + h * HD:c * D + (h + 1) * HD],
                               pb[i // 8][:, (i % 8) * HD:(i % 8 + 1) * HD],
                               start=True, stop=False)
                            mm(fqb, st_KVb[:, p_ * HD:(p_ + 1) * HD],
                               pb[i // 8][:, (i % 8) * HD:(i % 8 + 1) * HD],
                               start=False, stop=True)
                            mm(stm_scr[:, i * P:(i + 1) * P], c_ones[:, :],
                               pa[0][:, i:i + 1], start=True, stop=False)
                            mm(fqb, st_ke[:, p_:p_ + 1],
                               pa[0][:, i:i + 1], start=False, stop=True,
                               mark=1 if i == 15 else 0)
                        tok_y = eP.n
                        pump(8)
                        pr.wait("vector", eP, tok_y)
                        v_op(lambda e: e.reciprocal(s_rd[:, :], pa[0][:, 0:16]), mark=1)
                        pr.wait("vector", eV, eV.n)
                        for i in range(2):
                            v_op(lambda e, i=i: e.tensor_tensor(
                                ytmp[:, i * 512:(i + 1) * 512].rearrange(
                                    "p (a f) -> p a f", f=HD),
                                pb[i][:, :].rearrange("p (a f) -> p a f", f=HD),
                                s_rd[:, 8 * i:8 * i + 8].rearrange(
                                    "p (a o) -> p a o", o=1).broadcast_to([P, 8, HD]),
                                ALU.mult), mark=1 if i == 1 else 0)
                        tok_yd = eV.n
                        pa_free[0] = (eV, tok_yd)
                        pb_free[0] = (eV, tok_yd)
                        pb_free[1] = (eV, tok_yd)
                        if limit < 2.8:
                            continue

                        wait_tok("tensor", pt_free)
                        pr.wait("tensor", eV, tok_yd)
                        for q_ in range(8):
                            tr(ytmp[:, q_ * P:(q_ + 1) * P],
                               pt[:, (g * 8 + q_) * P:(g * 8 + q_ + 1) * P],
                               mark=1 if q_ == 7 else 0)
                        tok_yT = eP.n
                        pr.wait("scalar", eP, tok_yT)
                        act(A_xh[:, :].rearrange("p (d t) -> p d t", t=T)[
                                :, g * 8:(g + 1) * 8, c * P:(c + 1) * P],
                            pt[:, g * 1024:(g + 1) * 1024].rearrange("p (d q) -> p d q", q=P),
                            AF.Identity, mark=1)
                        if g == 1:
                            pt_free = (eA, eA.n)
                        ytok = eA.n
                        pump(10)

                    if limit < 2.85:
                        continue
                    for i in range(4):
                        wait_tok("tensor", pa_free[i])
                    for i in range(2):
                        wait_tok("tensor", pb_free[i])
                    for p in range(PAIR):
                        mm(fk_scr[slot][:, p * P:(p + 1) * P],
                           A_k[:, c * D + p * P:c * D + (p + 1) * P],
                           pa[p // 4][:, (p % 4) * P:(p % 4 + 1) * P],
                           start=True, stop=True)
                        mm(fk_scr[slot][:, p * P:(p + 1) * P], c_ones[:, :],
                           pb[p // 8][:, (p % 8):(p % 8) + 1],
                           start=True, stop=True, mark=1 if p == PAIR - 1 else 0)
                    tok_dmm = eP.n
                    fk_used[slot] = (eP, tok_dmm)
                    if limit < 2.9:
                        for i in range(4):
                            pa_free[i] = (eP, tok_dmm)
                        for i in range(2):
                            pb_free[i] = (eP, tok_dmm)
                        continue
                    pump(8)
                    pr.wait("vector", eP, tok_dmm)
                    state_add_ops(slot, masked_tile=3)
                    st_tok2 = eV.n
                    pr.wait("scalar", eV, st_tok2)
                    act(st_KVb[:, :], st_KV[:, :], AF.Identity)
                    pr.emit("scalar",
                            lambda e: e.activation(st_ke[:, :], st_k[:, :], AF.Identity,
                                                   bias=c_cvec[:, 1:2]),
                            [(eSt, 1)])
                    if limit >= 2.95 and c < STATE_CHUNKS:
                        kvb_tok = (eSt, eSt.n)
                    for i in range(4):
                        pa_free[i] = (eV, st_tok2)
                    for i in range(2):
                        pb_free[i] = (eV, st_tok2)
                    pump(16)
                tok_scan = ytok

        if limit < 4:
            _early_out()
            return nc
        R2 = sb("R2", [P, NCH * D], BF16)
        with ExitStack() as s_o:
            W_res = sb("W_res4", [P, DC * D], BF16, s_o)
            pr.wait("gpsimd", eP, eP.n)
            for dc in range(DC):
                for sc in range(4):
                    dma("gpsimd", sWR, W_res[:, dc * D + sc * 512:dc * D + (sc + 1) * 512],
                        woT[dc * P:(dc + 1) * P, sc * 512:(sc + 1) * 512])
            pr.wait("tensor", sWR)
            pr.wait("tensor", eA, tok_scan)

            x_user2 = list(x_user)
            pp = 0
            for tb in range(NCH):
                xw = x_win[tb % 1]
                sxw = sXW[tb % 1]
                wait_tok("sync", x_user2[tb % 1])
                dma("sync", sxw, xw[:, :], x_own[tb * P:(tb + 1) * P, :])
                for og in range(4):
                    bank = pa[pp]
                    wait_tok("tensor", pa_free[pp])
                    for dc in range(DC):
                        mm(A_xh[:, dc * T + tb * P: dc * T + (tb + 1) * P],
                           W_res[:, dc * D + og * 512: dc * D + (og + 1) * 512],
                           bank[:, :], start=(dc == 0), stop=(dc == DC - 1),
                           mark=1 if dc == DC - 1 else 0)
                    tok_mm = eP.n
                    pump(8)
                    pr.wait("vector", eP, tok_mm)
                    pr.wait("vector", sxw)
                    v_op(lambda e, tb=tb, og=og, bank=bank, xw=xw: e.scalar_tensor_tensor(
                        R2[:, tb * D + og * 512: tb * D + (og + 1) * 512],
                        xw[:, og * 512:(og + 1) * 512], 1.0, bank[:, :],
                        ALU.mult, ALU.add), mark=1)
                    pa_free[pp] = (eV, eV.n)
                    pp = (pp + 1) % 2
                x_user2[tb % 1] = (eV, eV.n)
            tok_r2 = eV.n

        with ExitStack() as s_ln2:
            ln_dump = sb("ln_dump2", [P, D], F32, s_ln2)
            tm_scr = sb("tm_scr2", [P, D], BF16, s_ln2)
            pr.wait("vector", eV, tok_r2)
            pr.wait("scalar", eV, tok_r2)
            for c in range(NCH):
                pump(6)
                act(ln_dump[:, :], R2[:, c * D:(c + 1) * D], AF.Identity, accum=s_sum[:, :],
                    mark=1)
                act(ln_dump[:, :], R2[:, c * D:(c + 1) * D], AF.Square, accum=s_sq[:, :],
                    mark=1)
                pr.wait("vector", eA, eA.n)
                v_op(lambda e: e.tensor_scalar(s_m[:, :], s_sum[:, :], 1.0 / D, None, ALU.mult),
                     mark=1)
                v_op(lambda e: e.tensor_scalar(s_t1[:, :], s_sq[:, :], 1.0 / D, None, ALU.mult),
                     mark=1)
                pr.wait("vector", eV, eV.n)
                v_op(lambda e: e.tensor_tensor(s_v[:, :], s_m[:, :], s_m[:, :], ALU.mult), mark=1)
                pr.wait("vector", eV, eV.n)
                v_op(lambda e: e.tensor_tensor(s_v[:, :], s_t1[:, :], s_v[:, :], ALU.subtract),
                     mark=1)
                pr.wait("scalar", eV, eV.n)
                act(s_t1[:, :], s_v[:, :], AF.Ln, bias=c_cvec[:, 0:1], mark=1)
                pr.wait("scalar", eA, eA.n)
                act(s_a[:, :], s_t1[:, :], AF.Exp, scale=-0.5, mark=1)
                pr.wait("vector", eA, eA.n)
                v_op(lambda e: e.scalar_tensor_tensor(
                    s_b[:, :], s_m[:, :], -1.0, s_a[:, :], ALU.mult, ALU.mult), mark=1)
                pr.wait("scalar", eV, eV.n)
                pr.wait("scalar", eA, eA.n)
                act(tm_scr[:, :], R2[:, c * D:(c + 1) * D], AF.Identity,
                    bias=s_b[:, :], scale=s_a[:, :], mark=1)
                tok_apply = eA.n
                wait_tok("tensor", pt_free)
                pr.wait("tensor", eA, tok_apply)
                for dc in range(DC):
                    tr(tm_scr[:, dc * P:(dc + 1) * P], pt[:, dc * P:(dc + 1) * P],
                       mark=1 if dc == DC - 1 else 0)
                pr.wait("scalar", eP, eP.n)
                act(A_xh[:, :].rearrange("p (d t) -> p d t", t=T)[:, :, c * P:(c + 1) * P],
                    pt[:, :].rearrange("p (d q) -> p d q", q=P), AF.Identity, mark=1)
                pt_free = (eA, eA.n)
            tok_xh2 = eA.n

        if limit < 5:
            _early_out()
            return nc
        s_att_psum.close()
        with ExitStack() as s_ffn:
            hstg = [sb(f"hstg{i}", [P, 512], BF16, s_ffn) for i in range(4)]
            ub_sb = [sb(f"ub{i}", [P, 512], BF16, s_ffn) for i in range(4)]
            silu_sb = [sb(f"silu{i}", [P, 512], BF16, s_ffn) for i in range(3)]
            wt = [sb(f"wt{i}", [P, P], BF16, s_ffn) for i in range(16)]
            wdt = [sb(f"wdt{i}", [P, 512], BF16, s_ffn) for i in range(8)]
            hmt = [sb(f"hmt{i}", [P, 512], BF16, s_ffn) for i in range(8)]
            otile = [sb(f"ot{i}", [P, 512], F32, s_ffn) for i in range(2)]
            pc = [s_ffn.enter_context(nc.psum_tensor(f"pc{i}", [P, 512], F32))
                  for i in range(8)]
            pc_free = [(eV, 0)] * 8

            def wait_mat(dst):
                while gather_mat_tok.get(id(dst)) is None and not _gg_done[0]:
                    pump(31)
                tok = gather_mat_tok.get(id(dst)) or (sGW[0].n, sGW[1].n)
                for eng in ("scalar", "sync", "tensor"):
                    pr.wait(eng, sGW[0], tok[0])
                    pr.wait(eng, sGW[1], tok[1])
            pr.wait("vector", eV, eV.n)
            pr.wait("scalar", eA, tok_xh2)
            pr.wait("tensor", eA, tok_xh2)
            pr.wait("tensor", eV, eV.n)

            wt_used = [(eP, 0)] * 16
            wd_used = [(eP, 0)] * 8
            hm_used = [(eP, 0)] * 8
            sHX = [sPL[0], sPL[1], sPS[0], sPS[1]]
            silu_free = [(eV, 0)] * 3
            ub_free = [(eA, 0)] * 4
            hstg_tok = [(sHM, 0)] * 4
            hm_tok = {}
            wcnt = 0

            def emit_mult(tp, iog, sl, bu, tok_u, tok_silu):
                st = iog % 4
                wait_tok("scalar", ub_free[st])
                pr.wait("scalar", eP, tok_u)
                act(ub_sb[st][:, :], pc[bu][:, :], AF.Identity, mark=1)
                tok_ub = eA.n
                pc_free[bu] = (eA, tok_ub)
                wait_tok("gpsimd", hstg_tok[st])
                pr.wait("gpsimd", eA, tok_ub)
                pr.emit("gpsimd", lambda e, st=st, sl=sl: e.tensor_tensor(
                    hstg[st][:, :], ub_sb[st][:, :], silu_sb[sl][:, :], ALU.mult),
                    [(eG, 1)])
                tok_htm = eG.n
                silu_free[sl] = (eG, tok_htm)
                ub_free[st] = (eG, tok_htm)
                pr.wait("sync", eG, tok_htm)
                pr.wait("sync", sHM)
                dma("sync", sHM, hm_d[tp, iog, :, :], hstg[st][:, :])
                hstg_tok[st] = (sHM, sHM.n)
                hm_tok[(tp, iog)] = sHM.n

            for tp in range(2):
                if tp == 0:
                    wait_mat(wg_s)
                    wait_mat(wu_s)
                for iog in range(IOC):
                    pump(26)
                    bg = (2 * iog) % 8
                    bu = (2 * iog + 1) % 8
                    sl = iog % 3
                    for dc in range(DC):
                        ws = wcnt % 16
                        wcnt += 1
                        wait_tok("sync", wt_used[ws ^ 8])
                        dma("sync", sWT[ws % 8], wt[ws][:, :],
                            wg_s[dc, :, iog * P:(iog + 1) * P])
                        if dc == 0:
                            wait_tok("tensor", pc_free[bg])
                        pr.wait("tensor", sWT[ws % 8])
                        mm(wt[ws][:, :],
                           A_xh[:, dc * T + tp * 512: dc * T + tp * 512 + 512],
                           pc[bg][:, :], start=(dc == 0), stop=(dc == DC - 1), mark=1)
                        wt_used[ws] = (eP, eP.n)
                    tok_g = eP.n
                    wait_tok("scalar", silu_free[sl])
                    pr.wait("scalar", eP, tok_g)
                    act(silu_sb[sl][:, :], pc[bg][:, :], AF.Silu, mark=1)
                    tok_silu = eA.n
                    pc_free[bg] = (eA, tok_silu)
                    for dc in range(DC):
                        ws = wcnt % 16
                        wcnt += 1
                        wait_tok("sync", wt_used[ws ^ 8])
                        dma("sync", sWT[ws % 8], wt[ws][:, :],
                            wu_s[dc, :, iog * P:(iog + 1) * P])
                        if dc == 0:
                            wait_tok("tensor", pc_free[bu])
                        pr.wait("tensor", sWT[ws % 8])
                        mm(wt[ws][:, :],
                           A_xh[:, dc * T + tp * 512: dc * T + tp * 512 + 512],
                           pc[bu][:, :], start=(dc == 0), stop=(dc == DC - 1), mark=1)
                        wt_used[ws] = (eP, eP.n)
                    tok_u = eP.n
                    emit_mult(tp, iog, sl, bu, tok_u, tok_silu)
            pr.wait("sync", eP, eP.n)
            pr.wait("scalar", eP, eP.n)

            for tp in range(2):
                for hg in range(4):
                    bs = (hg % 2) * 4
                    for k in range(IOC):
                        chase = tp == 0 and hg == 0
                        if chase:
                            while gather_tile_tok.get(k) is None and not _gg_done[0]:
                                pump(31)
                        else:
                            pump(4)
                        ws = k % 8
                        wait_tok("sync", wd_used[ws])
                        if chase:
                            tk = gather_tile_tok.get(k, (sGW[0].n, sGW[1].n))
                            pr.wait("sync", sGW[0], tk[0])
                            pr.wait("sync", sGW[1], tk[1])
                        dma("sync", sWT[ws], wdt[ws][:, :],
                            wd_s[k, :, hg * 512:(hg + 1) * 512])
                        wait_tok("scalar", hm_used[ws])
                        wait_tok("scalar", hm_used[(ws + 4) % 8])
                        pr.wait("scalar", sHM, hm_tok[(tp, k)])
                        dma("scalar", sHX[ws % 4], hmt[ws][:, :], hm_d[tp, k, :, :])
                        if k == 0:
                            for i in range(4):
                                wait_tok("tensor", pc_free[bs + i])
                        pr.wait("tensor", sWT[ws])
                        pr.wait("tensor", sHX[ws % 4])
                        for tb in range(4):
                            mm(hmt[ws][:, tb * P:(tb + 1) * P],
                               wdt[ws][:, :], pc[bs + tb][:, :],
                               start=(k == 0), stop=(k == IOC - 1),
                               mark=1 if tb == 3 else 0)
                        wd_used[ws] = (eP, eP.n)
                        hm_used[ws] = (eP, eP.n)
                    tok_down = eP.n
                    pr.wait("vector", eP, tok_down)
                    for tb in range(4):
                        os_ = tb % 2
                        tc = tp * 4 + tb
                        pr.wait("vector", sOT[os_])
                        v_op(lambda e, tc=tc, hg=hg, os_=os_, bs=bs, tb=tb:
                             e.scalar_tensor_tensor(
                                 otile[os_][:, :],
                                 R2[:, tc * D + hg * 512: tc * D + (hg + 1) * 512], 1.0,
                                 pc[bs + tb][:, :], ALU.mult, ALU.add), mark=1)
                        tok_ot = eV.n
                        pc_free[bs + tb] = (eV, tok_ot)
                        pr.wait("sync", eV, tok_ot)
                        dma("sync", sOT[os_], yout[tc * P:(tc + 1) * P, hg * 512:(hg + 1) * 512],
                            otile[os_][:, :])

            pump_all()
            pr.wait("sync", sOT[0])
            pr.wait("sync", sOT[1])

        block = ctx.enter_context(nc.Block())
        pr.install(block)

    return nc


_NC_CACHE = [None]


def _host_pack(inputs):
    f32 = np.float32
    x = np.asarray(inputs["L0_attn_hidden_states"], f32).reshape(2 * 4096, D)
    wq = np.ascontiguousarray(np.asarray(inputs["L0_attn_Wq"], f32).T)
    wk = np.ascontiguousarray(np.asarray(inputs["L0_attn_Wk"], f32).T)
    wv = np.ascontiguousarray(np.asarray(inputs["L0_attn_Wv"], f32).T)
    wo = np.ascontiguousarray(np.asarray(inputs["L0_attn_Wo"], f32).T)

    def blockdiag(wf):
        wf = np.asarray(wf, f32)
        bd = np.zeros((P, PAIR * P), f32)
        for p in range(PAIR):
            for s in range(2):
                bd[s * 64:(s + 1) * 64, p * P + s * 64: p * P + (s + 1) * 64] = wf[2 * p + s]
        return bd

    wfq_bd = blockdiag(inputs["L0_attn_Wfq"])
    wfk_bd = blockdiag(inputs["L0_attn_Wfk"])
    lut = np.asarray(inputs["L0_ffn_lut"], f32)
    lutbc = np.ascontiguousarray(np.broadcast_to(lut, (P, 16)))

    def idx8T(a):
        aT = np.ascontiguousarray(np.asarray(a, np.int32).T)
        return np.ascontiguousarray(aT.view(np.int8).reshape(aT.shape[0], aT.shape[1], 4)[:, :, 0])

    g8 = idx8T(inputs["L0_ffn_gate_idx"])
    u8 = idx8T(inputs["L0_ffn_up_idx"])
    d8 = idx8T(inputs["L0_ffn_down_idx"])

    utri = np.ascontiguousarray(np.tile(np.triu(np.ones((P, P), f32)), (1, 4)))
    ones_col = np.ones((P, 1), f32)
    cvec = np.zeros((P, 2), f32); cvec[:, 0] = LN_EPS; cvec[:, 1] = ATT_EPS
    ident = np.eye(P, dtype=f32)

    in_maps = []
    for core in range(8):
        seq, qpos = core // 4, core % 4
        pm = np.zeros((P, NTILE), f32)
        for t in range(3):
            if t < qpos:
                pm[:, t] = 1.0
        pm[:, 3] = 1.0
        in_maps.append({
            "x_own": np.ascontiguousarray(x[seq * 4096 + qpos * T: seq * 4096 + (qpos + 1) * T]),
            "x_pre": np.ascontiguousarray(x[seq * 4096: seq * 4096 + 3 * T]),
            "wqT": wq, "wkT": wk, "wvT": wv, "woT": wo,
            "wfq_bd": wfq_bd, "wfk_bd": wfk_bd, "lutbc": lutbc,
            "g8": g8, "u8": u8, "d8": d8,
            "utri": utri, "ones_in": ones_col, "ident_in": ident,
            "cvec_in": cvec,
            "pmask_in": pm,
        })
    return in_maps


def kernel(**inputs):
    if _NC_CACHE[0] is None:
        _NC_CACHE[0] = build_program()
    nc = _NC_CACHE[0]
    in_maps = _host_pack(inputs)
    res = run_bass_kernel_spmd(nc, in_maps, core_ids=list(range(8)))
    out = np.concatenate([np.asarray(r["yout"], np.float32) for r in res.results], axis=0)
    return out.reshape(2, 4096, D)


if __name__ == "__main__":
    nc = build_program()
    n = sum(1 for f in nc.m.functions for b in f.blocks for _ in b.instructions)
    print("built ok, instructions:", n)


# version 18
# speedup vs baseline: 1.1394x; 1.1394x over previous
import numpy as np
from contextlib import ExitStack

import concourse.bass as bass
import concourse.mybir as mybir
from concourse.bass_utils import run_bass_kernel_spmd

F32 = mybir.dt.float32
BF16 = mybir.dt.bfloat16
I8 = mybir.dt.int8
U16 = mybir.dt.uint16
AF = mybir.ActivationFunctionType
ALU = mybir.AluOpType
AX = mybir.AxisListType.X

P = 128
D = 2048
DC = D // P
T = 1024
NCH = T // P
H = 32
HD = 64
PAIR = 16
IO = 5504
IOC = IO // P
NTILE = 4
LN_EPS = 1e-5
STATE_CHUNKS = 8
SKIP_Y = False
SKIP_ST = False
SKIP_MASK = False
ST_HEADS = 3
DEBUG_TAPS = False
ATT_EPS = 1e-12

ENGINES = ("sync", "scalar", "vector", "tensor", "gpsimd")


class SemCtr:
    __slots__ = ("h", "n")

    def __init__(self, h):
        self.h = h
        self.n = 0


class Prog:
    def __init__(self):
        self.ops = {e: [] for e in ENGINES}

    def emit(self, eng, fn, incs=()):
        lowered = []
        for s, a in incs:
            s.n += a
            lowered.append((s.h, a))
        self.ops[eng].append((fn, tuple(lowered)))

    def wait(self, eng, s, val=None):
        v = s.n if val is None else val
        if v <= 0:
            return
        self.ops[eng].append((lambda e, h=s.h, vv=v: e.wait_ge(h, vv), ()))

    def install(self, block):
        def runner(ops):
            def body(e):
                for fn, incs in ops:
                    ins = fn(e)
                    for h, a in incs:
                        ins = ins.then_inc(h, a)
            return body
        block.sync(runner(self.ops["sync"]))
        block.scalar(runner(self.ops["scalar"]))
        block.vector(runner(self.ops["vector"]))
        block.tensor(runner(self.ops["tensor"]))
        block.gpsimd(runner(self.ops["gpsimd"]))


def build_program(limit=99, skip_gather=False):
    nc = bass.Bass("TRN2", target_bir_lowering=False)

    x_own = nc.dram_tensor("x_own", [T, D], F32, kind="ExternalInput").ap()
    x_pre = nc.dram_tensor("x_pre", [3 * T, D], F32, kind="ExternalInput").ap()
    wqT = nc.dram_tensor("wqT", [D, D], F32, kind="ExternalInput").ap()
    wkT = nc.dram_tensor("wkT", [D, D], F32, kind="ExternalInput").ap()
    wvT = nc.dram_tensor("wvT", [D, D], F32, kind="ExternalInput").ap()
    woT = nc.dram_tensor("woT", [D, D], F32, kind="ExternalInput").ap()
    wfq_in = nc.dram_tensor("wfq_bd", [P, PAIR * P], F32, kind="ExternalInput").ap()
    wfk_in = nc.dram_tensor("wfk_bd", [P, PAIR * P], F32, kind="ExternalInput").ap()
    lut_in = nc.dram_tensor("lutbc", [P, 16], F32, kind="ExternalInput").ap()
    g8 = nc.dram_tensor("g8", [D, IO], I8, kind="ExternalInput").ap()
    u8 = nc.dram_tensor("u8", [D, IO], I8, kind="ExternalInput").ap()
    d8 = nc.dram_tensor("d8", [IO, D], I8, kind="ExternalInput").ap()
    utri_in = nc.dram_tensor("utri", [P, 4 * P], F32, kind="ExternalInput").ap()
    ones_in = nc.dram_tensor("ones_in", [P, 1], F32, kind="ExternalInput").ap()
    cvec_in = nc.dram_tensor("cvec_in", [P, 2], F32, kind="ExternalInput").ap()
    ident_in = nc.dram_tensor("ident_in", [P, P], F32, kind="ExternalInput").ap()
    pmask_in = nc.dram_tensor("pmask_in", [P, NTILE], F32, kind="ExternalInput").ap()
    yout = nc.dram_tensor("yout", [T, D], F32, kind="ExternalOutput").ap()

    kd = "ExternalOutput" if DEBUG_TAPS else "Internal"
    wg_s = nc.dram_tensor("wg_s", [DC, P, IO], BF16, kind=kd).ap()
    wu_s = nc.dram_tensor("wu_s", [DC, P, IO], BF16, kind=kd).ap()
    wd_s = nc.dram_tensor("wd_s", [IOC, P, D], BF16, kind=kd).ap()
    xh_d = nc.dram_tensor("xh_d", [NTILE, P, DC * T], BF16, kind=kd).ap()
    fk_d = nc.dram_tensor("fk_d", [NTILE, NCH, P, D], BF16, kind=kd).ap()

    pr = Prog()

    with ExitStack() as ctx:
        def sb(name, shape, dtype, st=None):
            return (st or ctx).enter_context(nc.sbuf_tensor(name, shape, dtype))

        def psum(name, shape, dtype=F32):
            return ctx.enter_context(nc.psum_tensor(name, shape, dtype))

        def sem(name):
            return SemCtr(ctx.enter_context(nc.semaphore(name)))

        eP = sem("eP")
        eA = sem("eA")
        eV = sem("eV")
        sC = sem("sC")
        sC2 = sem("sC2")
        sXW = [sem("sXWa")]
        sXH = sem("sXH")
        sWR = sem("sWR")
        sFK = [sem("sFK0"), sem("sFK1")]
        sGB = [sem(f"sGB{i}") for i in range(4)]
        sGW = [sem("sGW0"), sem("sGW1")]
        sWT = [sem(f"sWT{i}") for i in range(4)]
        sOT = [sem("sOT0"), sem("sOT1")]
        eG = sem("eG")
        sPL = [sem("sPL0"), sem("sPL1")]
        sPS = [sem("sPS0"), sem("sPS1")]
        eSt = sem("eSt")

        c_utri = sb("c_utri", [P, 4 * P], BF16)
        c_wfq = sb("c_wfq", [P, PAIR * P], BF16)
        c_wfk = sb("c_wfk", [P, PAIR * P], BF16)
        c_ones = sb("c_ones", [P, 1], BF16)
        c_id = sb("c_id", [P, P], BF16)
        c_pmask = sb("c_pmask", [P, NTILE], F32)
        c_cvec = sb("c_cvec", [P, 2], F32)
        c_lutb = sb("c_lutb", [P, 16], BF16)
        c_lutf = sb("c_lutf", [P, 16], F32)

        st_KV = sb("st_KV", [P, PAIR * HD], F32)
        st_KVb = sb("st_KVb", [P, PAIR * HD], BF16)
        st_k = sb("st_k", [P, PAIR], F32)
        st_ke = sb("st_ke", [P, PAIR], BF16)

        x_win = [sb("x_win0", [P, D], F32)]
        GW = 1024
        gw_bf = [sb(f"gw_bf{i}", [P, GW], BF16) for i in range(4)]
        gw_t = sb("gw_t", [P, GW], BF16)
        gw_w = [sb(f"gw_w{i}", [P, GW], BF16) for i in range(2)]
        PW = 512
        A_xh = sb("A_xh", [P, DC * T], BF16)
        A_k = sb("A_k", [P, DC * T], BF16)
        fk_scr = [sb(f"fk_scr{i}", [P, D], BF16) for i in range(2)]

        s_sum = sb("s_sum", [P, 1], F32)
        s_sq = sb("s_sq", [P, 1], F32)
        s_m = sb("s_m", [P, 1], F32)
        s_v = sb("s_v", [P, 1], F32)
        s_t1 = sb("s_t1", [P, 1], F32)
        s_a = sb("s_a", [P, 1], F32)
        s_b = sb("s_b", [P, 1], F32)
        s_sk = sb("s_sk", [P, H], F32)
        s_rsk = sb("s_rsk", [P, H], F32)
        s_rd = sb("s_rd", [P, PAIR], F32)

        pa = [psum(f"pa{i}", [P, 512], F32) for i in range(4)]
        pb = [psum(f"pb{i}", [P, 512], F32) for i in range(2)]
        pt = psum("pt", [P, PAIR * P], BF16)

        def dma(q, s, out, in_):
            pr.emit(q, lambda e, o=out, i=in_: e.dma_start(out=o, in_=i), [(s, 16)])
            return s.n

        def mm(lhsT, rhs, out, start, stop, mark=0):
            incs = [(eP, mark)] if mark else ()
            pr.emit("tensor",
                    lambda e, a=lhsT, b=rhs, o=out, s1=start, s2=stop:
                    e.matmul(o, a, b, start=s1, stop=s2), incs)

        def tr(in_, out, mark=0):
            incs = [(eP, mark)] if mark else ()
            pr.emit("tensor",
                    lambda e, i=in_, o=out: e.transpose(o, i, c_id[:, :]), incs)

        def act(out, in_, func, bias=0.0, scale=1.0, accum=None, mark=0):
            incs = [(eA, mark)] if mark else ()
            pr.emit("scalar",
                    lambda e, o=out, i=in_, f=func, b=bias, s=scale, ao=accum:
                    e.activation(o, i, f, bias=b, scale=s, accum_out=ao), incs)

        def v_op(fn, mark=0):
            incs = [(eV, mark)] if mark else ()
            pr.emit("vector", fn, incs)

        def wait_tok(eng, tok):
            pr.wait(eng, tok[0], tok[1])

        pa_free = [(eV, 0)] * 4
        pb_free = [(eV, 0)] * 2
        pt_free = (eV, 0)

        def state_add_ops(slot, masked_tile=None):
            for i in range(4):
                for s in range(2):
                    lo = s * 64
                    src = pa[i][:, :].rearrange("p (a q) -> p a q", q=P)[lo:lo + 64, :, lo:lo + 64]
                    dst = st_KV[lo:lo + 64, 4 * i * HD:(4 * i + 4) * HD].rearrange(
                        "p (a f) -> p a f", f=HD)
                    if masked_tile is None:
                        v_op(lambda e, d=dst, s2=src: e.tensor_tensor(d, d, s2, ALU.add))
                    else:
                        tcol = c_pmask[:, masked_tile:masked_tile + 1][lo:lo + 64, :]
                        v_op(lambda e, d=dst, s2=src, tc=tcol: e.scalar_tensor_tensor(
                            d, s2, tc, d, ALU.mult, ALU.add))
            for i in range(2):
                dst = st_k[:, 8 * i:8 * i + 8]
                src = pb[i][:, 0:8]
                if masked_tile is None:
                    v_op(lambda e, d=dst, s2=src: e.tensor_tensor(d, d, s2, ALU.add),
                         mark=1 if i == 1 else 0)
                else:
                    tcol = c_pmask[:, masked_tile:masked_tile + 1]
                    v_op(lambda e, d=dst, s2=src, tc=tcol: e.scalar_tensor_tensor(
                        d, s2, tc, d, ALU.mult, ALU.add), mark=1 if i == 1 else 0)

        dma("gpsimd", sC, c_utri[:, :], utri_in)
        dma("gpsimd", sC, c_wfq[:, :], wfq_in)
        dma("gpsimd", sC, c_wfk[:, :], wfk_in)
        dma("gpsimd", sC, c_ones[:, :], ones_in)
        dma("gpsimd", sC, c_id[:, :], ident_in)
        dma("gpsimd", sC, c_lutb[:, :], lut_in)
        dma("sync", sC2, c_lutf[:, :], lut_in)
        dma("sync", sC2, c_cvec[:, :], cvec_in)
        dma("sync", sC2, c_pmask[:, :], pmask_in)

        pr.wait("vector", sC)
        pr.wait("vector", sC2)
        v_op(lambda e: e.memset(st_KV[:, :], 0.0))
        v_op(lambda e: e.memset(st_k[:, :], 0.0))
        v_op(lambda e: e.memset(st_KVb[:, :], 0.0))
        v_op(lambda e: e.memset(st_ke[:, :], float(ATT_EPS)), mark=1)
        pr.wait("tensor", sC)
        pr.wait("scalar", sC)
        pr.wait("scalar", sC2)

        POOL_DOWN_CHUNKS = 30
        gjobs = []
        pjobs = []
        for gsrc, gdst, nk, width in (() if skip_gather else ((g8, wg_s, DC, IO), (u8, wu_s, DC, IO), (d8, wd_s, IOC, D))):
            for k in range(nk):
                pool = False
                step = PW if pool else GW
                c0 = 0
                while c0 < width:
                    f = min(step, width - c0)
                    (pjobs if pool else gjobs).append((gsrc, gdst, k, c0, f))
                    c0 += f

        pr.wait("vector", sC)
        pr.wait("vector", sC2)
        pr.wait("gpsimd", sC2)

        def _gather_gen():
            NS = 4
            gw_rd_tok = [(eV, 0)] * NS

            def issue(n):
                gsrc_, gdst_, k_, c0_, f_ = gjobs[n]
                b_ = n % NS
                if n >= NS:
                    wait_tok("gpsimd", gw_rd_tok[b_])
                dma("gpsimd", sGB[b_], gw_bf[b_][:, 0:f_],
                    gsrc_[k_ * P:(k_ + 1) * P, c0_:c0_ + f_])

            for n in range(min(3, len(gjobs))):
                issue(n)
            for n, (gsrc, gdst, k, c0, f) in enumerate(gjobs):
                if n + 3 < len(gjobs):
                    issue(n + 3)
                b = n % NS
                w = n % 2
                pr.wait("vector", sGB[b])
                if n >= 2:
                    pr.wait("vector", sGW[w])
                for j in range(16):
                    if j == 0:
                        v_op(lambda e, b=b, w=w, f=f: e.tensor_scalar(
                            gw_w[w][:, 0:f], gw_bf[b][:, 0:f], 0.0, c_lutf[:, 0:1],
                            ALU.is_equal, ALU.mult))
                    else:
                        v_op(lambda e, b=b, f=f, j=j: e.tensor_scalar(
                            gw_t[:, 0:f], gw_bf[b][:, 0:f], float(j), c_lutf[:, j:j + 1],
                            ALU.is_equal, ALU.mult))
                        mk = 1 if j == 15 else 0
                        v_op(lambda e, w=w, f=f: e.tensor_tensor(
                            gw_w[w][:, 0:f], gw_w[w][:, 0:f], gw_t[:, 0:f], ALU.add),
                             mark=mk)
                    yield
                gw_rd_tok[b] = (eV, eV.n)
                pr.wait("sync", eV, eV.n)
                dma("sync", sGW[w], gdst[k, :, c0:c0 + f], gw_w[w][:, 0:f])
                if n == len(gjobs) - 1 or gjobs[n + 1][1] is not gdst:
                    gather_mat_tok[id(gdst)] = (sGW[0].n, sGW[1].n)
                if gdst is wd_s and (n == len(gjobs) - 1 or gjobs[n + 1][2] != k):
                    gather_tile_tok[k] = (sGW[0].n, sGW[1].n)
                yield

        def _pool_gen():
            for n, (gsrc, gdst, k, c0, f) in enumerate(pjobs):
                b = n % 2
                pr.wait("gpsimd", sPS[b])
                dma("gpsimd", sPL[b], pw_bf[b][:, 0:f], gsrc[k * P:(k + 1) * P, c0:c0 + f])
                pr.wait("gpsimd", sPL[b])
                for j in range(16):
                    if j == 0:
                        pr.emit("gpsimd", lambda e, b=b, f=f: e.tensor_scalar(
                            pw_w[b][:, 0:f], pw_bf[b][:, 0:f], 0.0, c_lutf[:, 0:1],
                            ALU.is_equal, ALU.mult))
                    else:
                        pr.emit("gpsimd", lambda e, b=b, f=f, j=j: e.tensor_scalar(
                            pw_t[:, 0:f], pw_bf[b][:, 0:f], float(j), c_lutf[:, j:j + 1],
                            ALU.is_equal, ALU.mult))
                        pr.emit("gpsimd", lambda e, b=b, f=f: e.tensor_tensor(
                            pw_w[b][:, 0:f], pw_w[b][:, 0:f], pw_t[:, 0:f], ALU.add))
                    yield
                dma("gpsimd", sPS[b], gdst[k, :, c0:c0 + f], pw_w[b][:, 0:f])
                yield

        _pg = _pool_gen()
        _pg_done = [False]

        def pump_pool(n):
            if _pg_done[0]:
                return
            try:
                for _ in range(n):
                    next(_pg)
            except StopIteration:
                _pg_done[0] = True

        def pump_pool_all():
            while not _pg_done[0]:
                pump_pool(1000)

        _gg = _gather_gen()
        _gg_done = [False]
        gather_mat_tok = {}
        gather_tile_tok = {}

        def pump(n):
            pump_pool(max(1, n // 3))
            if _gg_done[0]:
                return
            try:
                for _ in range(n):
                    next(_gg)
            except StopIteration:
                _gg_done[0] = True

        def pump_all():
            while not _gg_done[0]:
                pump(1000)
            pump_pool_all()



        def _early_out():
            pump_all()
            pump_pool_all()
            pr.wait("sync", eV, eV.n)
            pr.wait("sync", eA, eA.n)
            dma("sync", sOT[0], yout[0:P, :], x_win[0][:, :])
            for s_ in (sOT[0], sGW[0], sGW[1], sXH, sFK[0], sFK[1]):
                pr.wait("sync", s_)
            block = ctx.enter_context(nc.Block())
            pr.install(block)

        if limit < 1:
            _early_out()
            return nc

        def x_src(t, c):
            if t == 3:
                return x_own[c * P:(c + 1) * P, :]
            return x_pre[t * T + c * P:t * T + (c + 1) * P, :]

        fk_used = [(eP, 0), (eP, 0)]
        x_user = [(eA, 0)]
        tok_kproj = [0] * NTILE
        tok_zk_last = 0

        with ExitStack() as s_p1:
            ln_dump = sb("ln_dump", [P, D], F32, s_p1)
            tm_scr = sb("tm_scr", [P, D], BF16, s_p1)
            W_res = sb("W_res", [P, DC * D], BF16, s_p1)
            for dc in range(DC):
                for sc in range(4):
                    dma("gpsimd", sWR, W_res[:, dc * D + sc * 512:dc * D + (sc + 1) * 512],
                        wkT[dc * P:(dc + 1) * P, sc * 512:(sc + 1) * 512])

            for t in range(NTILE):
                for c in range(NCH):
                    xw = x_win[c % 1]
                    sxw = sXW[c % 1]
                    wait_tok("sync", x_user[c % 1])
                    if t > 0 and c == 0:
                        pr.wait("sync", eP, tok_kproj[t - 1])
                        pr.wait("sync", sXH)
                    dma("sync", sxw, xw[:, :], x_src(t, c))
                    pr.wait("scalar", sxw)
                    act(ln_dump[:, :], xw[:, :], AF.Identity, accum=s_sum[:, :], mark=1)
                    act(ln_dump[:, :], xw[:, :], AF.Square, accum=s_sq[:, :], mark=1)
                    pr.wait("vector", eA, eA.n)
                    v_op(lambda e: e.tensor_scalar(s_m[:, :], s_sum[:, :], 1.0 / D, None, ALU.mult),
                         mark=1)
                    v_op(lambda e: e.tensor_scalar(s_t1[:, :], s_sq[:, :], 1.0 / D, None, ALU.mult),
                         mark=1)
                    pr.wait("vector", eV, eV.n)
                    v_op(lambda e: e.tensor_tensor(s_v[:, :], s_m[:, :], s_m[:, :], ALU.mult), mark=1)
                    pr.wait("vector", eV, eV.n)
                    v_op(lambda e: e.tensor_tensor(s_v[:, :], s_t1[:, :], s_v[:, :], ALU.subtract),
                         mark=1)
                    pr.wait("scalar", eV, eV.n)
                    act(s_t1[:, :], s_v[:, :], AF.Ln, bias=c_cvec[:, 0:1], mark=1)
                    pr.wait("scalar", eA, eA.n)
                    act(s_a[:, :], s_t1[:, :], AF.Exp, scale=-0.5, mark=1)
                    pr.wait("vector", eA, eA.n)
                    v_op(lambda e: e.scalar_tensor_tensor(
                        s_b[:, :], s_m[:, :], -1.0, s_a[:, :], ALU.mult, ALU.mult), mark=1)
                    pr.wait("scalar", eV, eV.n)
                    pr.wait("scalar", eA, eA.n)
                    act(tm_scr[:, :], xw[:, :], AF.Identity, bias=s_b[:, :], scale=s_a[:, :],
                        mark=1)
                    x_user[c % 1] = (eA, eA.n)
                    tok_apply = eA.n
                    wait_tok("tensor", pt_free)
                    pr.wait("tensor", eA, tok_apply)
                    for dc in range(DC):
                        tr(tm_scr[:, dc * P:(dc + 1) * P], pt[:, dc * P:(dc + 1) * P],
                           mark=1 if dc == DC - 1 else 0)
                    pr.wait("scalar", eP, eP.n)
                    act(A_xh[:, :].rearrange("p (d t) -> p d t", t=T)[:, :, c * P:(c + 1) * P],
                        pt[:, :].rearrange("p (d q) -> p d q", q=P), AF.Identity, mark=1)
                    pt_free = (eA, eA.n)
                    pump(12)
                tok_xh = eA.n

                pr.wait("sync", eA, tok_xh)
                dma("sync", sXH, xh_d[t, :, :], A_xh[:, :])

                pr.wait("tensor", sWR)
                pr.wait("tensor", eA, tok_xh)
                if t > 0:
                    pr.wait("scalar", eP, tok_zk_last)
                pp = 0
                TH = T // 2
                for oc in range(DC):
                    for th in range(2):
                        bank = pa[pp]
                        wait_tok("tensor", pa_free[pp])
                        for dc in range(DC):
                            mm(W_res[:, dc * D + oc * P: dc * D + (oc + 1) * P],
                               A_xh[:, dc * T + th * TH: dc * T + (th + 1) * TH],
                               bank[:, 0:TH], start=(dc == 0), stop=(dc == DC - 1),
                               mark=1 if dc == DC - 1 else 0)
                        tok_mm = eP.n
                        pr.wait("scalar", eP, tok_mm)
                        act(A_k[:, oc * T + th * TH: oc * T + (th + 1) * TH], bank[:, 0:TH],
                            AF.Identity, mark=1)
                        pa_free[pp] = (eA, eA.n)
                        pp = (pp + 1) % 2
                        pump(6)
                tok_kproj[t] = eP.n
                tok_kfm = eA.n

                for c in range(NCH):
                    slot = c % 2
                    pr.wait("tensor", eA, tok_kfm)
                    for i in range(4):
                        wait_tok("tensor", pa_free[i])
                    for p in range(PAIR):
                        mm(A_k[:, p * T + c * P:p * T + (c + 1) * P],
                           c_wfk[:, p * P:(p + 1) * P],
                           pa[p // 4][:, (p % 4) * P:(p % 4 + 1) * P],
                           start=True, stop=True, mark=1 if p == PAIR - 1 else 0)
                    tok_zk = eP.n
                    tok_zk_last = tok_zk
                    pr.wait("scalar", eP, tok_zk)
                    pr.wait("scalar", sFK[slot])
                    wait_tok("scalar", fk_used[slot])
                    for i in range(4):
                        act(fk_scr[slot][:, i * 512:(i + 1) * 512], pa[i][:, :], AF.Exp,
                            mark=1 if i == 3 else 0)
                    tok_exp = eA.n
                    for i in range(4):
                        pa_free[i] = (eA, tok_exp)
                    pr.wait("vector", eA, tok_exp)
                    v_op(lambda e, s=slot: e.tensor_reduce(
                        s_sk[:, :], fk_scr[s][:, :].rearrange("p (h f) -> p h f", f=HD),
                        AX, ALU.add), mark=1)
                    pr.wait("vector", eV, eV.n)
                    v_op(lambda e: e.reciprocal(s_rsk[:, :], s_sk[:, :]), mark=1)
                    pr.wait("vector", eV, eV.n)
                    v_op(lambda e, t=t: e.tensor_scalar(
                        s_rsk[:, :], s_rsk[:, :], c_pmask[:, t:t + 1], None, ALU.mult), mark=1)
                    pr.wait("vector", eV, eV.n)
                    v_op(lambda e, s=slot: e.tensor_tensor(
                        fk_scr[s][:, :].rearrange("p (h f) -> p h f", f=HD),
                        fk_scr[s][:, :].rearrange("p (h f) -> p h f", f=HD),
                        s_rsk[:, :].rearrange("p (h o) -> p h o", o=1).broadcast_to([P, H, HD]),
                        ALU.mult), mark=1)
                    pr.wait("sync", eV, eV.n)
                    dma("sync", sFK[slot], fk_d[t, c, :, :], fk_scr[slot][:, :])
                    pump(12)

        if limit < 2:
            _early_out()
            return nc
        tok_vproj_last = 0
        with ExitStack() as s_p2:
            W_res = sb("W_res2", [P, DC * D], BF16, s_p2)
            pr.wait("gpsimd", eP, eP.n)
            for dc in range(DC):
                for sc in range(4):
                    dma("gpsimd", sWR, W_res[:, dc * D + sc * 512:dc * D + (sc + 1) * 512],
                        wvT[dc * P:(dc + 1) * P, sc * 512:(sc + 1) * 512])

            tok_zk_guard = tok_zk_last
            for t in range(NTILE):
                pr.wait("sync", eP, eP.n)
                pr.wait("sync", sXH)
                dma("sync", sXH, A_xh[:, :], xh_d[t, :, :])
                pr.wait("tensor", sXH)
                pr.wait("tensor", sWR)
                if t == 0:
                    pr.wait("scalar", eP, tok_zk_guard)
                pp = 0
                for tb in range(NCH):
                    for og in range(4):
                        bank = pa[pp]
                        wait_tok("tensor", pa_free[pp])
                        for dc in range(DC):
                            mm(A_xh[:, dc * T + tb * P: dc * T + (tb + 1) * P],
                               W_res[:, dc * D + og * 512: dc * D + (og + 1) * 512],
                               bank[:, :], start=(dc == 0), stop=(dc == DC - 1),
                               mark=1 if dc == DC - 1 else 0)
                        tok_mm = eP.n
                        pr.wait("scalar", eP, tok_mm)
                        act(A_k[:, tb * D + og * 512: tb * D + (og + 1) * 512], bank[:, :],
                            AF.Identity, mark=1)
                        pa_free[pp] = (eA, eA.n)
                        pp = (pp + 1) % 2
                        pump(4)
                tok_vproj_last = eP.n
                tok_vtm = eA.n

                if t == 3:
                    break

                for c in range(NCH):
                    slot = c % 2
                    wait_tok("sync", fk_used[slot])
                    pr.wait("sync", eV, eV.n)
                    dma("sync", sFK[slot], fk_scr[slot][:, :], fk_d[t, c, :, :])
                    pr.wait("tensor", sFK[slot])
                    pr.wait("tensor", eA, tok_vtm)
                    for i in range(4):
                        wait_tok("tensor", pa_free[i])
                    for i in range(2):
                        wait_tok("tensor", pb_free[i])
                    for p in range(PAIR):
                        mm(fk_scr[slot][:, p * P:(p + 1) * P],
                           A_k[:, c * D + p * P:c * D + (p + 1) * P],
                           pa[p // 4][:, (p % 4) * P:(p % 4 + 1) * P],
                           start=True, stop=True)
                        mm(fk_scr[slot][:, p * P:(p + 1) * P], c_ones[:, :],
                           pb[p // 8][:, (p % 8):(p % 8) + 1],
                           start=True, stop=True, mark=1 if p == PAIR - 1 else 0)
                    tok_dmm = eP.n
                    fk_used[slot] = (eP, tok_dmm)
                    pr.wait("vector", eP, tok_dmm)
                    state_add_ops(slot, masked_tile=t)
                    st_tok = eV.n
                    for i in range(4):
                        pa_free[i] = (eV, st_tok)
                    for i in range(2):
                        pb_free[i] = (eV, st_tok)
                    pump(12)

            v_op(lambda e: e.tensor_copy(st_KVb[:, :], st_KV[:, :]))
            v_op(lambda e: e.tensor_scalar(st_ke[:, :], st_k[:, :], float(ATT_EPS), None,
                                           ALU.add), mark=1)
            tok_state = eV.n

        if limit < 3:
            _early_out()
            return nc
        tok_scan = 0
        with ExitStack() as s_aq:
            A_q = sb("A_q", [P, DC * T], BF16, s_aq)

            with ExitStack() as s_q:
                W_res = sb("W_res3", [P, DC * D], BF16, s_q)
                pr.wait("gpsimd", eP, eP.n)
                for dc in range(DC):
                    for sc in range(4):
                        dma("gpsimd", sWR, W_res[:, dc * D + sc * 512:dc * D + (sc + 1) * 512],
                            wqT[dc * P:(dc + 1) * P, sc * 512:(sc + 1) * 512])
                pr.wait("sync", eP, eP.n)
                dma("sync", sXH, A_xh[:, :], xh_d[3, :, :])
                pr.wait("tensor", sXH)
                pr.wait("tensor", sWR)
                pp = 0
                TH = T // 2
                for oc in range(DC):
                    for th in range(2):
                        bank = pa[pp]
                        wait_tok("tensor", pa_free[pp])
                        for dc in range(DC):
                            mm(W_res[:, dc * D + oc * P: dc * D + (oc + 1) * P],
                               A_xh[:, dc * T + th * TH: dc * T + (th + 1) * TH],
                               bank[:, 0:TH], start=(dc == 0), stop=(dc == DC - 1),
                               mark=1 if dc == DC - 1 else 0)
                        tok_mm = eP.n
                        pr.wait("scalar", eP, tok_mm)
                        act(A_q[:, oc * T + th * TH: oc * T + (th + 1) * TH], bank[:, 0:TH],
                            AF.Identity, mark=1)
                        pa_free[pp] = (eA, eA.n)
                        pp = (pp + 1) % 2
                        pump(8)
                tok_qproj = eP.n
                tok_qfm = eA.n
            if limit < 2.4:
                _early_out()
                return nc

            with ExitStack() as s_scan:
                fq_bd = sb("fq_bd", [P, 2 * PAIR * P], BF16, s_scan)
                fkf_scr = sb("fkf_scr", [P, PAIR * P], BF16, s_scan)
                stm_scr = sb("stm_scr", [P, 16 * P], BF16, s_scan)
                ytmp = sb("ytmp", [P, 1024], BF16, s_scan)

                pr.wait("tensor", eA, tok_qfm)
                pr.wait("tensor", eV, tok_state)
                pr.wait("vector", eP, tok_qproj)
                v_op(lambda e: e.memset(fq_bd[:, :], 0.0), mark=1)
                pr.wait("scalar", eV, eV.n)
                kvb_tok = (eV, tok_state)
                ytok = 0
                for c in range(NCH):
                    slot = c % 2
                    wait_tok("sync", fk_used[slot])
                    pr.wait("sync", eV, eV.n)
                    dma("sync", sFK[slot], fk_scr[slot][:, :], fk_d[3, c, :, :])

                    wait_tok("tensor", kvb_tok)
                    for i in range(4):
                        wait_tok("tensor", pa_free[i])
                    for p in range(PAIR):
                        mm(c_wfq[:, p * P:(p + 1) * P],
                           A_q[:, p * T + c * P:p * T + (c + 1) * P],
                           pa[p // 4][:, (p % 4) * P:(p % 4 + 1) * P],
                           start=True, stop=True, mark=1 if p == PAIR - 1 else 0)
                    tok_zq = eP.n
                    pr.wait("scalar", eP, tok_zq)
                    for i in range(4):
                        act(fq_bd[0:64, i * 1024:(i + 1) * 1024].rearrange(
                                "p (a c) -> p a c", c=256)[:, :, 0:128],
                            pa[i][0:64, :].rearrange("p (a c) -> p a c", c=128),
                            AF.Exp)
                        act(fq_bd[64:128, i * 1024:(i + 1) * 1024].rearrange(
                                "p (a c) -> p a c", c=256)[:, :, 128:256],
                            pa[i][64:128, :].rearrange("p (a c) -> p a c", c=128),
                            AF.Exp, mark=1 if i == 3 else 0)
                    tok_fq = eA.n
                    for i in range(4):
                        pa_free[i] = (eA, tok_fq)
                    if limit < 2.5:
                        continue

                    wait_tok("tensor", pt_free)
                    pr.wait("tensor", sFK[slot])
                    for p in range(PAIR):
                        tr(fk_scr[slot][:, p * P:(p + 1) * P], pt[:, p * P:(p + 1) * P],
                           mark=1 if p == PAIR - 1 else 0)
                    tok_fkT = eP.n
                    pr.wait("scalar", eP, tok_fkT)
                    act(fkf_scr[:, 0:1024], pt[:, 0:1024], AF.Identity)
                    act(fkf_scr[:, 1024:2048], pt[:, 1024:2048], AF.Identity, mark=1)
                    tok_fkf = eA.n
                    pt_free = (eA, tok_fkf)
                    if limit < 2.6:
                        continue

                    for g in range(0 if SKIP_ST else 2):
                        h0 = g * 16
                        pr.wait("tensor", eA, tok_fkf)
                        for i in range(4):
                            wait_tok("tensor", pa_free[i])
                        for lp in range(8):
                            p_ = g * 8 + lp
                            mm(fkf_scr[:, p_ * P:(p_ + 1) * P],
                               fq_bd[:, p_ * 256:(p_ + 1) * 256],
                               pa[lp // 2][:, (lp % 2) * 256:(lp % 2 + 1) * 256],
                               start=True, stop=True, mark=1 if lp == 7 else 0)
                        tok_st = eP.n
                        if SKIP_MASK:
                            for i in range(4):
                                pa_free[i] = (eP, tok_st)
                            continue
                        pump(8)
                        pr.wait("vector", eP, tok_st)
                        for i in range(4):
                            v_op(lambda e, i=i: e.tensor_tensor(
                                stm_scr[:, i * 512:(i + 1) * 512], pa[i][:, :],
                                c_utri[:, :], ALU.mult), mark=1 if i == 3 else 0)
                        tok_stm = eV.n
                        for i in range(4):
                            pa_free[i] = (eV, tok_stm)
                        if limit < 2.7:
                            continue

                        if SKIP_Y:
                            continue
                        pr.wait("tensor", eV, tok_stm)
                        for i in range(2):
                            wait_tok("tensor", pb_free[i])
                        wait_tok("tensor", pa_free[0])
                        for i in range(16):
                            h = h0 + i
                            p_, s_ = h // 2, h % 2
                            fqb = fq_bd[:, p_ * 256 + s_ * 128: p_ * 256 + (s_ + 1) * 128]
                            mm(stm_scr[:, i * P:(i + 1) * P],
                               A_k[:, c * D + h * HD:c * D + (h + 1) * HD],
                               pb[i // 8][:, (i % 8) * HD:(i % 8 + 1) * HD],
                               start=True, stop=False)
                            mm(fqb, st_KVb[:, p_ * HD:(p_ + 1) * HD],
                               pb[i // 8][:, (i % 8) * HD:(i % 8 + 1) * HD],
                               start=False, stop=True)
                            mm(stm_scr[:, i * P:(i + 1) * P], c_ones[:, :],
                               pa[0][:, i:i + 1], start=True, stop=False)
                            mm(fqb, st_ke[:, p_:p_ + 1],
                               pa[0][:, i:i + 1], start=False, stop=True,
                               mark=1 if i == 15 else 0)
                        tok_y = eP.n
                        pump(8)
                        pr.wait("vector", eP, tok_y)
                        v_op(lambda e: e.reciprocal(s_rd[:, :], pa[0][:, 0:16]), mark=1)
                        pr.wait("vector", eV, eV.n)
                        for i in range(2):
                            v_op(lambda e, i=i: e.tensor_tensor(
                                ytmp[:, i * 512:(i + 1) * 512].rearrange(
                                    "p (a f) -> p a f", f=HD),
                                pb[i][:, :].rearrange("p (a f) -> p a f", f=HD),
                                s_rd[:, 8 * i:8 * i + 8].rearrange(
                                    "p (a o) -> p a o", o=1).broadcast_to([P, 8, HD]),
                                ALU.mult), mark=1 if i == 1 else 0)
                        tok_yd = eV.n
                        pa_free[0] = (eV, tok_yd)
                        pb_free[0] = (eV, tok_yd)
                        pb_free[1] = (eV, tok_yd)
                        if limit < 2.8:
                            continue

                        wait_tok("tensor", pt_free)
                        pr.wait("tensor", eV, tok_yd)
                        for q_ in range(8):
                            tr(ytmp[:, q_ * P:(q_ + 1) * P],
                               pt[:, (g * 8 + q_) * P:(g * 8 + q_ + 1) * P],
                               mark=1 if q_ == 7 else 0)
                        tok_yT = eP.n
                        pr.wait("scalar", eP, tok_yT)
                        act(A_xh[:, :].rearrange("p (d t) -> p d t", t=T)[
                                :, g * 8:(g + 1) * 8, c * P:(c + 1) * P],
                            pt[:, g * 1024:(g + 1) * 1024].rearrange("p (d q) -> p d q", q=P),
                            AF.Identity, mark=1)
                        if g == 1:
                            pt_free = (eA, eA.n)
                        ytok = eA.n
                        pump(10)

                    if limit < 2.85:
                        continue
                    for i in range(4):
                        wait_tok("tensor", pa_free[i])
                    for i in range(2):
                        wait_tok("tensor", pb_free[i])
                    for p in range(PAIR):
                        mm(fk_scr[slot][:, p * P:(p + 1) * P],
                           A_k[:, c * D + p * P:c * D + (p + 1) * P],
                           pa[p // 4][:, (p % 4) * P:(p % 4 + 1) * P],
                           start=True, stop=True)
                        mm(fk_scr[slot][:, p * P:(p + 1) * P], c_ones[:, :],
                           pb[p // 8][:, (p % 8):(p % 8) + 1],
                           start=True, stop=True, mark=1 if p == PAIR - 1 else 0)
                    tok_dmm = eP.n
                    fk_used[slot] = (eP, tok_dmm)
                    if limit < 2.9:
                        for i in range(4):
                            pa_free[i] = (eP, tok_dmm)
                        for i in range(2):
                            pb_free[i] = (eP, tok_dmm)
                        continue
                    pump(8)
                    pr.wait("vector", eP, tok_dmm)
                    state_add_ops(slot, masked_tile=3)
                    st_tok2 = eV.n
                    pr.wait("scalar", eV, st_tok2)
                    act(st_KVb[:, :], st_KV[:, :], AF.Identity)
                    pr.emit("scalar",
                            lambda e: e.activation(st_ke[:, :], st_k[:, :], AF.Identity,
                                                   bias=c_cvec[:, 1:2]),
                            [(eSt, 1)])
                    if limit >= 2.95 and c < STATE_CHUNKS:
                        kvb_tok = (eSt, eSt.n)
                    for i in range(4):
                        pa_free[i] = (eV, st_tok2)
                    for i in range(2):
                        pb_free[i] = (eV, st_tok2)
                    pump(16)
                tok_scan = ytok

        if limit < 4:
            _early_out()
            return nc
        R2 = sb("R2", [P, NCH * D], BF16)
        with ExitStack() as s_o:
            W_res = sb("W_res4", [P, DC * D], BF16, s_o)
            pr.wait("gpsimd", eP, eP.n)
            for dc in range(DC):
                for sc in range(4):
                    dma("gpsimd", sWR, W_res[:, dc * D + sc * 512:dc * D + (sc + 1) * 512],
                        woT[dc * P:(dc + 1) * P, sc * 512:(sc + 1) * 512])
            pr.wait("tensor", sWR)
            pr.wait("tensor", eA, tok_scan)

            x_user2 = list(x_user)
            pp = 0
            for tb in range(NCH):
                xw = x_win[tb % 1]
                sxw = sXW[tb % 1]
                wait_tok("sync", x_user2[tb % 1])
                dma("sync", sxw, xw[:, :], x_own[tb * P:(tb + 1) * P, :])
                for og in range(4):
                    bank = pa[pp]
                    wait_tok("tensor", pa_free[pp])
                    for dc in range(DC):
                        mm(A_xh[:, dc * T + tb * P: dc * T + (tb + 1) * P],
                           W_res[:, dc * D + og * 512: dc * D + (og + 1) * 512],
                           bank[:, :], start=(dc == 0), stop=(dc == DC - 1),
                           mark=1 if dc == DC - 1 else 0)
                    tok_mm = eP.n
                    pump(8)
                    pr.wait("vector", eP, tok_mm)
                    pr.wait("vector", sxw)
                    v_op(lambda e, tb=tb, og=og, bank=bank, xw=xw: e.scalar_tensor_tensor(
                        R2[:, tb * D + og * 512: tb * D + (og + 1) * 512],
                        xw[:, og * 512:(og + 1) * 512], 1.0, bank[:, :],
                        ALU.mult, ALU.add), mark=1)
                    pa_free[pp] = (eV, eV.n)
                    pp = (pp + 1) % 2
                x_user2[tb % 1] = (eV, eV.n)
            tok_r2 = eV.n

        with ExitStack() as s_ln2:
            ln_dump = sb("ln_dump2", [P, D], F32, s_ln2)
            tm_scr = sb("tm_scr2", [P, D], BF16, s_ln2)
            pr.wait("vector", eV, tok_r2)
            pr.wait("scalar", eV, tok_r2)
            for c in range(NCH):
                pump(6)
                act(ln_dump[:, :], R2[:, c * D:(c + 1) * D], AF.Identity, accum=s_sum[:, :],
                    mark=1)
                act(ln_dump[:, :], R2[:, c * D:(c + 1) * D], AF.Square, accum=s_sq[:, :],
                    mark=1)
                pr.wait("vector", eA, eA.n)
                v_op(lambda e: e.tensor_scalar(s_m[:, :], s_sum[:, :], 1.0 / D, None, ALU.mult),
                     mark=1)
                v_op(lambda e: e.tensor_scalar(s_t1[:, :], s_sq[:, :], 1.0 / D, None, ALU.mult),
                     mark=1)
                pr.wait("vector", eV, eV.n)
                v_op(lambda e: e.tensor_tensor(s_v[:, :], s_m[:, :], s_m[:, :], ALU.mult), mark=1)
                pr.wait("vector", eV, eV.n)
                v_op(lambda e: e.tensor_tensor(s_v[:, :], s_t1[:, :], s_v[:, :], ALU.subtract),
                     mark=1)
                pr.wait("scalar", eV, eV.n)
                act(s_t1[:, :], s_v[:, :], AF.Ln, bias=c_cvec[:, 0:1], mark=1)
                pr.wait("scalar", eA, eA.n)
                act(s_a[:, :], s_t1[:, :], AF.Exp, scale=-0.5, mark=1)
                pr.wait("vector", eA, eA.n)
                v_op(lambda e: e.scalar_tensor_tensor(
                    s_b[:, :], s_m[:, :], -1.0, s_a[:, :], ALU.mult, ALU.mult), mark=1)
                pr.wait("scalar", eV, eV.n)
                pr.wait("scalar", eA, eA.n)
                act(tm_scr[:, :], R2[:, c * D:(c + 1) * D], AF.Identity,
                    bias=s_b[:, :], scale=s_a[:, :], mark=1)
                tok_apply = eA.n
                wait_tok("tensor", pt_free)
                pr.wait("tensor", eA, tok_apply)
                for dc in range(DC):
                    tr(tm_scr[:, dc * P:(dc + 1) * P], pt[:, dc * P:(dc + 1) * P],
                       mark=1 if dc == DC - 1 else 0)
                pr.wait("scalar", eP, eP.n)
                act(A_xh[:, :].rearrange("p (d t) -> p d t", t=T)[:, :, c * P:(c + 1) * P],
                    pt[:, :].rearrange("p (d q) -> p d q", q=P), AF.Identity, mark=1)
                pt_free = (eA, eA.n)
            tok_xh2 = eA.n

        if limit < 5:
            _early_out()
            return nc
        with ExitStack() as s_ffn:
            h_mid = sb("h_mid", [P, IOC * 512], BF16, s_ffn)
            silu_sb = [sb(f"silu{i}", [P, 512], BF16, s_ffn) for i in range(4)]
            htm_sb = [sb(f"htm{i}", [P, 512], BF16, s_ffn) for i in range(4)]
            ub_sb = [sb(f"ub{i}", [P, 512], BF16, s_ffn) for i in range(4)]
            wt = [sb(f"wt{i}", [P, 512], BF16, s_ffn) for i in range(4)]
            otile = [sb(f"ot{i}", [P, 512], F32, s_ffn) for i in range(2)]

            def wait_mat(dst):
                while gather_mat_tok.get(id(dst)) is None and not _gg_done[0]:
                    pump(31)
                tok = gather_mat_tok.get(id(dst)) or (sGW[0].n, sGW[1].n)
                for eng in ("scalar", "tensor"):
                    pr.wait(eng, sGW[0], tok[0])
                    pr.wait(eng, sGW[1], tok[1])
            pr.wait("vector", eV, eV.n)
            pr.wait("scalar", eA, tok_xh2)
            pr.wait("tensor", eA, tok_xh2)

            wt_used = [(eP, 0)] * 4
            ub_free = [(eA, 0)] * 4
            silu_rd = (eG, 0)
            htm_rd = (eP, 0)

            IO_G = [(gi * 512, min(512, IO - gi * 512)) for gi in range((IO + 511) // 512)]

            for tp in range(2):
                if tp == 0:
                    wait_mat(wg_s)
                for gi, (c0, gw) in enumerate(IO_G):
                    pump(40)
                    for dc in range(DC):
                        ws = dc % 4
                        wait_tok("scalar", wt_used[ws])
                        dma("scalar", sWT[ws], wt[ws][:, 0:gw], wg_s[dc, :, c0:c0 + gw])
                        if dc == 0:
                            for i in range(4):
                                wait_tok("tensor", pa_free[i])
                        pr.wait("tensor", sWT[ws])
                        for tb in range(4):
                            mm(A_xh[:, dc * T + tp * 512 + tb * P:
                                    dc * T + tp * 512 + (tb + 1) * P],
                               wt[ws][:, 0:gw], pa[tb][:, 0:gw],
                               start=(dc == 0), stop=(dc == DC - 1),
                               mark=1 if tb == 3 else 0)
                        wt_used[ws] = (eP, eP.n)
                    tok_gate = eP.n
                    wait_tok("scalar", silu_rd)
                    pr.wait("scalar", eP, tok_gate)
                    for tb in range(4):
                        act(silu_sb[tb][:, 0:gw], pa[tb][:, 0:gw], AF.Silu,
                            mark=1 if tb == 3 else 0)
                    tok_silu = eA.n
                    if tp == 0 and gi == 0:
                        wait_mat(wu_s)
                    pump(40)
                    pr.wait("tensor", eA, tok_silu)
                    for dc in range(DC):
                        ws = dc % 4
                        wait_tok("scalar", wt_used[ws])
                        dma("scalar", sWT[ws], wt[ws][:, 0:gw], wu_s[dc, :, c0:c0 + gw])
                        pr.wait("tensor", sWT[ws])
                        for tb in range(4):
                            mm(A_xh[:, dc * T + tp * 512 + tb * P:
                                    dc * T + tp * 512 + (tb + 1) * P],
                               wt[ws][:, 0:gw], pa[tb][:, 0:gw],
                               start=(dc == 0), stop=(dc == DC - 1),
                               mark=1 if tb == 3 else 0)
                        wt_used[ws] = (eP, eP.n)
                    tok_up = eP.n
                    pr.wait("scalar", eP, tok_up)
                    for tb in range(4):
                        wait_tok("scalar", ub_free[tb])
                        act(ub_sb[tb][:, 0:gw], pa[tb][:, 0:gw], AF.Identity,
                            mark=1 if tb == 3 else 0)
                    tok_ub = eA.n
                    for i in range(4):
                        pa_free[i] = (eA, tok_ub)
                    wait_tok("gpsimd", htm_rd)
                    pr.wait("gpsimd", eA, tok_ub)
                    for tb in range(4):
                        pr.emit("gpsimd", lambda e, tb=tb, gw=gw: e.tensor_tensor(
                            htm_sb[tb][:, 0:gw], ub_sb[tb][:, 0:gw], silu_sb[tb][:, 0:gw],
                            ALU.mult), [(eG, 1)] if tb == 3 else ())
                    tok_htm = eG.n
                    silu_rd = (eG, tok_htm)
                    for tb in range(4):
                        ub_free[tb] = (eG, tok_htm)
                    wait_tok("tensor", pt_free)
                    pr.wait("tensor", eG, tok_htm)
                    nio = gw // P
                    for tb in range(4):
                        for s2 in range(nio):
                            tr(htm_sb[tb][:, s2 * P:(s2 + 1) * P],
                               pt[:, (tb * 4 + s2) * P:(tb * 4 + s2 + 1) * P],
                               mark=1 if (tb == 3 and s2 == nio - 1) else 0)
                    tok_hT = eP.n
                    htm_rd = (eP, tok_hT)
                    pr.wait("scalar", eP, tok_hT)
                    for tb in range(4):
                        for s2 in range(nio):
                            ioc = gi * 4 + s2
                            act(h_mid[:, ioc * 512 + tb * P: ioc * 512 + (tb + 1) * P],
                                pt[:, (tb * 4 + s2) * P:(tb * 4 + s2 + 1) * P],
                                AF.Identity, mark=1 if (tb == 3 and s2 == nio - 1) else 0)
                    pt_free = (eA, eA.n)
                tok_hmid = eA.n

                pr.wait("tensor", eA, tok_hmid)
                for hg in range(4):
                    pump(8)
                    for k in range(IOC):
                        if tp == 0:
                            while gather_tile_tok.get(k) is None and not _gg_done[0]:
                                pump(31)
                        elif k % 4 == 0:
                            pump(3)
                        ws = k % 4
                        wait_tok("scalar", wt_used[ws])
                        if tp == 0:
                            tk = gather_tile_tok.get(k, (sGW[0].n, sGW[1].n))
                            pr.wait("scalar", sGW[0], tk[0])
                            pr.wait("scalar", sGW[1], tk[1])
                        dma("scalar", sWT[ws], wt[ws][:, :], wd_s[k, :, hg * 512:(hg + 1) * 512])
                        if k == 0:
                            for i in range(4):
                                wait_tok("tensor", pa_free[i])
                        pr.wait("tensor", sWT[ws])
                        for tb in range(4):
                            mm(h_mid[:, k * 512 + tb * P: k * 512 + (tb + 1) * P],
                               wt[ws][:, :], pa[tb][:, :],
                               start=(k == 0), stop=(k == IOC - 1),
                               mark=1 if tb == 3 else 0)
                        wt_used[ws] = (eP, eP.n)
                    tok_down = eP.n
                    pr.wait("vector", eP, tok_down)
                    for tb in range(4):
                        os_ = tb % 2
                        tc = tp * 4 + tb
                        pr.wait("vector", sOT[os_])
                        v_op(lambda e, tc=tc, hg=hg, os_=os_, tb=tb: e.scalar_tensor_tensor(
                            otile[os_][:, :],
                            R2[:, tc * D + hg * 512: tc * D + (hg + 1) * 512], 1.0,
                            pa[tb][:, :], ALU.mult, ALU.add), mark=1)
                        tok_ot = eV.n
                        pr.wait("sync", eV, tok_ot)
                        dma("sync", sOT[os_], yout[tc * P:(tc + 1) * P, hg * 512:(hg + 1) * 512],
                            otile[os_][:, :])
                    for i in range(4):
                        pa_free[i] = (eV, eV.n)

            pr.wait("sync", sOT[0])
            pr.wait("sync", sOT[1])

        block = ctx.enter_context(nc.Block())
        pr.install(block)

    return nc


_NC_CACHE = [None]


def _host_pack(inputs):
    f32 = np.float32
    x = np.asarray(inputs["L0_attn_hidden_states"], f32).reshape(2 * 4096, D)
    wq = np.ascontiguousarray(np.asarray(inputs["L0_attn_Wq"], f32).T)
    wk = np.ascontiguousarray(np.asarray(inputs["L0_attn_Wk"], f32).T)
    wv = np.ascontiguousarray(np.asarray(inputs["L0_attn_Wv"], f32).T)
    wo = np.ascontiguousarray(np.asarray(inputs["L0_attn_Wo"], f32).T)

    def blockdiag(wf):
        wf = np.asarray(wf, f32)
        bd = np.zeros((P, PAIR * P), f32)
        for p in range(PAIR):
            for s in range(2):
                bd[s * 64:(s + 1) * 64, p * P + s * 64: p * P + (s + 1) * 64] = wf[2 * p + s]
        return bd

    wfq_bd = blockdiag(inputs["L0_attn_Wfq"])
    wfk_bd = blockdiag(inputs["L0_attn_Wfk"])
    lut = np.asarray(inputs["L0_ffn_lut"], f32)
    lutbc = np.ascontiguousarray(np.broadcast_to(lut, (P, 16)))

    def idx8T(a):
        aT = np.ascontiguousarray(np.asarray(a, np.int32).T)
        return np.ascontiguousarray(aT.view(np.int8).reshape(aT.shape[0], aT.shape[1], 4)[:, :, 0])

    g8 = idx8T(inputs["L0_ffn_gate_idx"])
    u8 = idx8T(inputs["L0_ffn_up_idx"])
    d8 = idx8T(inputs["L0_ffn_down_idx"])

    utri = np.ascontiguousarray(np.tile(np.triu(np.ones((P, P), f32)), (1, 4)))
    ones_col = np.ones((P, 1), f32)
    cvec = np.zeros((P, 2), f32); cvec[:, 0] = LN_EPS; cvec[:, 1] = ATT_EPS
    ident = np.eye(P, dtype=f32)

    in_maps = []
    for core in range(8):
        seq, qpos = core // 4, core % 4
        pm = np.zeros((P, NTILE), f32)
        for t in range(3):
            if t < qpos:
                pm[:, t] = 1.0
        pm[:, 3] = 1.0
        in_maps.append({
            "x_own": np.ascontiguousarray(x[seq * 4096 + qpos * T: seq * 4096 + (qpos + 1) * T]),
            "x_pre": np.ascontiguousarray(x[seq * 4096: seq * 4096 + 3 * T]),
            "wqT": wq, "wkT": wk, "wvT": wv, "woT": wo,
            "wfq_bd": wfq_bd, "wfk_bd": wfk_bd, "lutbc": lutbc,
            "g8": g8, "u8": u8, "d8": d8,
            "utri": utri, "ones_in": ones_col, "ident_in": ident,
            "cvec_in": cvec,
            "pmask_in": pm,
        })
    return in_maps


def kernel(**inputs):
    if _NC_CACHE[0] is None:
        _NC_CACHE[0] = build_program()
    nc = _NC_CACHE[0]
    in_maps = _host_pack(inputs)
    res = run_bass_kernel_spmd(nc, in_maps, core_ids=list(range(8)))
    out = np.concatenate([np.asarray(r["yout"], np.float32) for r in res.results], axis=0)
    return out.reshape(2, 4096, D)


if __name__ == "__main__":
    nc = build_program()
    n = sum(1 for f in nc.m.functions for b in f.blocks for _ in b.instructions)
    print("built ok, instructions:", n)


# version 19
# speedup vs baseline: 1.1561x; 1.0147x over previous
import numpy as np
from contextlib import ExitStack

import concourse.bass as bass
import concourse.mybir as mybir
from concourse.bass_utils import run_bass_kernel_spmd

F32 = mybir.dt.float32
BF16 = mybir.dt.bfloat16
I8 = mybir.dt.int8
U16 = mybir.dt.uint16
AF = mybir.ActivationFunctionType
ALU = mybir.AluOpType
AX = mybir.AxisListType.X

P = 128
D = 2048
DC = D // P
T = 1024
NCH = T // P
H = 32
HD = 64
PAIR = 16
IO = 5504
IOC = IO // P
NTILE = 4
LN_EPS = 1e-5
STATE_CHUNKS = 8
SKIP_Y = False
SKIP_ST = False
SKIP_MASK = False
ST_HEADS = 3
DEBUG_TAPS = False
ATT_EPS = 1e-12

ENGINES = ("sync", "scalar", "vector", "tensor", "gpsimd")


class SemCtr:
    __slots__ = ("h", "n")

    def __init__(self, h):
        self.h = h
        self.n = 0


class Prog:
    def __init__(self):
        self.ops = {e: [] for e in ENGINES}

    def emit(self, eng, fn, incs=()):
        lowered = []
        for s, a in incs:
            s.n += a
            lowered.append((s.h, a))
        self.ops[eng].append((fn, tuple(lowered)))

    def wait(self, eng, s, val=None):
        v = s.n if val is None else val
        if v <= 0:
            return
        self.ops[eng].append((lambda e, h=s.h, vv=v: e.wait_ge(h, vv), ()))

    def install(self, block):
        def runner(ops):
            def body(e):
                for fn, incs in ops:
                    ins = fn(e)
                    for h, a in incs:
                        ins = ins.then_inc(h, a)
            return body
        block.sync(runner(self.ops["sync"]))
        block.scalar(runner(self.ops["scalar"]))
        block.vector(runner(self.ops["vector"]))
        block.tensor(runner(self.ops["tensor"]))
        block.gpsimd(runner(self.ops["gpsimd"]))


def build_program(limit=99, skip_gather=False):
    nc = bass.Bass("TRN2", target_bir_lowering=False)

    x_own = nc.dram_tensor("x_own", [T, D], F32, kind="ExternalInput").ap()
    x_pre = nc.dram_tensor("x_pre", [3 * T, D], F32, kind="ExternalInput").ap()
    wqT = nc.dram_tensor("wqT", [D, D], F32, kind="ExternalInput").ap()
    wkT = nc.dram_tensor("wkT", [D, D], F32, kind="ExternalInput").ap()
    wvT = nc.dram_tensor("wvT", [D, D], F32, kind="ExternalInput").ap()
    woT = nc.dram_tensor("woT", [D, D], F32, kind="ExternalInput").ap()
    wfq_in = nc.dram_tensor("wfq_bd", [P, PAIR * P], F32, kind="ExternalInput").ap()
    wfk_in = nc.dram_tensor("wfk_bd", [P, PAIR * P], F32, kind="ExternalInput").ap()
    lut_in = nc.dram_tensor("lutbc", [P, 16], F32, kind="ExternalInput").ap()
    g8 = nc.dram_tensor("g8", [D, IO], I8, kind="ExternalInput").ap()
    u8 = nc.dram_tensor("u8", [D, IO], I8, kind="ExternalInput").ap()
    d8 = nc.dram_tensor("d8", [IO, D], I8, kind="ExternalInput").ap()
    utri_in = nc.dram_tensor("utri", [P, 4 * P], F32, kind="ExternalInput").ap()
    ones_in = nc.dram_tensor("ones_in", [P, 1], F32, kind="ExternalInput").ap()
    cvec_in = nc.dram_tensor("cvec_in", [P, 2], F32, kind="ExternalInput").ap()
    ident_in = nc.dram_tensor("ident_in", [P, P], F32, kind="ExternalInput").ap()
    pmask_in = nc.dram_tensor("pmask_in", [P, NTILE], F32, kind="ExternalInput").ap()
    yout = nc.dram_tensor("yout", [T, D], F32, kind="ExternalOutput").ap()

    kd = "ExternalOutput" if DEBUG_TAPS else "Internal"
    wg_s = nc.dram_tensor("wg_s", [DC, P, IO], BF16, kind=kd).ap()
    wu_s = nc.dram_tensor("wu_s", [DC, P, IO], BF16, kind=kd).ap()
    wd_s = nc.dram_tensor("wd_s", [IOC, P, D], BF16, kind=kd).ap()
    xh_d = nc.dram_tensor("xh_d", [NTILE, P, DC * T], BF16, kind=kd).ap()
    fk_d = nc.dram_tensor("fk_d", [NTILE, NCH, P, D], BF16, kind=kd).ap()

    pr = Prog()

    with ExitStack() as ctx:
        def sb(name, shape, dtype, st=None):
            return (st or ctx).enter_context(nc.sbuf_tensor(name, shape, dtype))

        def psum(name, shape, dtype=F32):
            return ctx.enter_context(nc.psum_tensor(name, shape, dtype))

        def sem(name):
            return SemCtr(ctx.enter_context(nc.semaphore(name)))

        eP = sem("eP")
        eA = sem("eA")
        eV = sem("eV")
        sC = sem("sC")
        sC2 = sem("sC2")
        sXW = [sem("sXWa")]
        sXH = sem("sXH")
        sWR = sem("sWR")
        sFK = [sem("sFK0"), sem("sFK1")]
        sGB = [sem(f"sGB{i}") for i in range(4)]
        sGW = [sem("sGW0"), sem("sGW1")]
        sWT = [sem(f"sWT{i}") for i in range(4)]
        sOT = [sem("sOT0"), sem("sOT1")]
        sPL = [sem("sPL0"), sem("sPL1")]
        sPS = [sem("sPS0"), sem("sPS1")]
        eSt = sem("eSt")

        c_utri = sb("c_utri", [P, 4 * P], BF16)
        c_wfq = sb("c_wfq", [P, PAIR * P], BF16)
        c_wfk = sb("c_wfk", [P, PAIR * P], BF16)
        c_ones = sb("c_ones", [P, 1], BF16)
        c_id = sb("c_id", [P, P], BF16)
        c_pmask = sb("c_pmask", [P, NTILE], F32)
        c_cvec = sb("c_cvec", [P, 2], F32)
        c_lutb = sb("c_lutb", [P, 16], BF16)
        c_lutf = sb("c_lutf", [P, 16], F32)

        st_KV = sb("st_KV", [P, PAIR * HD], F32)
        st_KVb = sb("st_KVb", [P, PAIR * HD], BF16)
        st_k = sb("st_k", [P, PAIR], F32)
        st_ke = sb("st_ke", [P, PAIR], BF16)

        x_win = [sb("x_win0", [P, D], F32)]
        GW = 1024
        gw_bf = [sb(f"gw_bf{i}", [P, GW], BF16) for i in range(4)]
        gw_t = sb("gw_t", [P, GW], BF16)
        gw_w = [sb(f"gw_w{i}", [P, GW], BF16) for i in range(2)]
        PW = 512
        A_xh = sb("A_xh", [P, DC * T], BF16)
        A_k = sb("A_k", [P, DC * T], BF16)
        fk_scr = [sb(f"fk_scr{i}", [P, D], BF16) for i in range(2)]

        s_sum = sb("s_sum", [P, 1], F32)
        s_sq = sb("s_sq", [P, 1], F32)
        s_m = sb("s_m", [P, 1], F32)
        s_v = sb("s_v", [P, 1], F32)
        s_t1 = sb("s_t1", [P, 1], F32)
        s_a = sb("s_a", [P, 1], F32)
        s_b = sb("s_b", [P, 1], F32)
        s_sk = sb("s_sk", [P, H], F32)
        s_rsk = sb("s_rsk", [P, H], F32)
        s_rd = sb("s_rd", [P, PAIR], F32)

        pa = [psum(f"pa{i}", [P, 512], F32) for i in range(4)]
        pb = [psum(f"pb{i}", [P, 512], F32) for i in range(2)]
        pt = psum("pt", [P, PAIR * P], BF16)

        def dma(q, s, out, in_):
            pr.emit(q, lambda e, o=out, i=in_: e.dma_start(out=o, in_=i), [(s, 16)])
            return s.n

        def mm(lhsT, rhs, out, start, stop, mark=0):
            incs = [(eP, mark)] if mark else ()
            pr.emit("tensor",
                    lambda e, a=lhsT, b=rhs, o=out, s1=start, s2=stop:
                    e.matmul(o, a, b, start=s1, stop=s2), incs)

        def tr(in_, out, mark=0):
            incs = [(eP, mark)] if mark else ()
            pr.emit("tensor",
                    lambda e, i=in_, o=out: e.transpose(o, i, c_id[:, :]), incs)

        def act(out, in_, func, bias=0.0, scale=1.0, accum=None, mark=0):
            incs = [(eA, mark)] if mark else ()
            pr.emit("scalar",
                    lambda e, o=out, i=in_, f=func, b=bias, s=scale, ao=accum:
                    e.activation(o, i, f, bias=b, scale=s, accum_out=ao), incs)

        def v_op(fn, mark=0):
            incs = [(eV, mark)] if mark else ()
            pr.emit("vector", fn, incs)

        def wait_tok(eng, tok):
            pr.wait(eng, tok[0], tok[1])

        pa_free = [(eV, 0)] * 4
        pb_free = [(eV, 0)] * 2
        pt_free = (eV, 0)

        def state_add_ops(slot, masked_tile=None):
            for i in range(4):
                for s in range(2):
                    lo = s * 64
                    src = pa[i][:, :].rearrange("p (a q) -> p a q", q=P)[lo:lo + 64, :, lo:lo + 64]
                    dst = st_KV[lo:lo + 64, 4 * i * HD:(4 * i + 4) * HD].rearrange(
                        "p (a f) -> p a f", f=HD)
                    if masked_tile is None:
                        v_op(lambda e, d=dst, s2=src: e.tensor_tensor(d, d, s2, ALU.add))
                    else:
                        tcol = c_pmask[:, masked_tile:masked_tile + 1][lo:lo + 64, :]
                        v_op(lambda e, d=dst, s2=src, tc=tcol: e.scalar_tensor_tensor(
                            d, s2, tc, d, ALU.mult, ALU.add))
            for i in range(2):
                dst = st_k[:, 8 * i:8 * i + 8]
                src = pb[i][:, 0:8]
                if masked_tile is None:
                    v_op(lambda e, d=dst, s2=src: e.tensor_tensor(d, d, s2, ALU.add),
                         mark=1 if i == 1 else 0)
                else:
                    tcol = c_pmask[:, masked_tile:masked_tile + 1]
                    v_op(lambda e, d=dst, s2=src, tc=tcol: e.scalar_tensor_tensor(
                        d, s2, tc, d, ALU.mult, ALU.add), mark=1 if i == 1 else 0)

        dma("gpsimd", sC, c_utri[:, :], utri_in)
        dma("gpsimd", sC, c_wfq[:, :], wfq_in)
        dma("gpsimd", sC, c_wfk[:, :], wfk_in)
        dma("gpsimd", sC, c_ones[:, :], ones_in)
        dma("gpsimd", sC, c_id[:, :], ident_in)
        dma("gpsimd", sC, c_lutb[:, :], lut_in)
        dma("sync", sC2, c_lutf[:, :], lut_in)
        dma("sync", sC2, c_cvec[:, :], cvec_in)
        dma("sync", sC2, c_pmask[:, :], pmask_in)

        pr.wait("vector", sC)
        pr.wait("vector", sC2)
        v_op(lambda e: e.memset(st_KV[:, :], 0.0))
        v_op(lambda e: e.memset(st_k[:, :], 0.0))
        v_op(lambda e: e.memset(st_KVb[:, :], 0.0))
        v_op(lambda e: e.memset(st_ke[:, :], float(ATT_EPS)), mark=1)
        pr.wait("tensor", sC)
        pr.wait("scalar", sC)
        pr.wait("scalar", sC2)

        POOL_DOWN_CHUNKS = 30
        gjobs = []
        pjobs = []
        for gsrc, gdst, nk, width in (() if skip_gather else ((g8, wg_s, DC, IO), (u8, wu_s, DC, IO), (d8, wd_s, IOC, D))):
            for k in range(nk):
                pool = False
                step = PW if pool else GW
                c0 = 0
                while c0 < width:
                    f = min(step, width - c0)
                    (pjobs if pool else gjobs).append((gsrc, gdst, k, c0, f))
                    c0 += f

        pr.wait("vector", sC)
        pr.wait("vector", sC2)
        pr.wait("gpsimd", sC2)

        def _gather_gen():
            NS = 4
            gw_rd_tok = [(eV, 0)] * NS

            def issue(n):
                gsrc_, gdst_, k_, c0_, f_ = gjobs[n]
                b_ = n % NS
                if n >= NS:
                    wait_tok("gpsimd", gw_rd_tok[b_])
                dma("gpsimd", sGB[b_], gw_bf[b_][:, 0:f_],
                    gsrc_[k_ * P:(k_ + 1) * P, c0_:c0_ + f_])

            for n in range(min(3, len(gjobs))):
                issue(n)
            for n, (gsrc, gdst, k, c0, f) in enumerate(gjobs):
                if n + 3 < len(gjobs):
                    issue(n + 3)
                b = n % NS
                w = n % 2
                pr.wait("vector", sGB[b])
                if n >= 2:
                    pr.wait("vector", sGW[w])
                for j in range(16):
                    if j == 0:
                        v_op(lambda e, b=b, w=w, f=f: e.tensor_scalar(
                            gw_w[w][:, 0:f], gw_bf[b][:, 0:f], 0.0, c_lutf[:, 0:1],
                            ALU.is_equal, ALU.mult))
                    else:
                        v_op(lambda e, b=b, f=f, j=j: e.tensor_scalar(
                            gw_t[:, 0:f], gw_bf[b][:, 0:f], float(j), c_lutf[:, j:j + 1],
                            ALU.is_equal, ALU.mult))
                        mk = 1 if j == 15 else 0
                        v_op(lambda e, w=w, f=f: e.tensor_tensor(
                            gw_w[w][:, 0:f], gw_w[w][:, 0:f], gw_t[:, 0:f], ALU.add),
                             mark=mk)
                    yield
                gw_rd_tok[b] = (eV, eV.n)
                pr.wait("sync", eV, eV.n)
                dma("sync", sGW[w], gdst[k, :, c0:c0 + f], gw_w[w][:, 0:f])
                if n == len(gjobs) - 1 or gjobs[n + 1][1] is not gdst:
                    gather_mat_tok[id(gdst)] = (sGW[0].n, sGW[1].n)
                if gdst is wd_s and (n == len(gjobs) - 1 or gjobs[n + 1][2] != k):
                    gather_tile_tok[k] = (sGW[0].n, sGW[1].n)
                yield

        def _pool_gen():
            for n, (gsrc, gdst, k, c0, f) in enumerate(pjobs):
                b = n % 2
                pr.wait("gpsimd", sPS[b])
                dma("gpsimd", sPL[b], pw_bf[b][:, 0:f], gsrc[k * P:(k + 1) * P, c0:c0 + f])
                pr.wait("gpsimd", sPL[b])
                for j in range(16):
                    if j == 0:
                        pr.emit("gpsimd", lambda e, b=b, f=f: e.tensor_scalar(
                            pw_w[b][:, 0:f], pw_bf[b][:, 0:f], 0.0, c_lutf[:, 0:1],
                            ALU.is_equal, ALU.mult))
                    else:
                        pr.emit("gpsimd", lambda e, b=b, f=f, j=j: e.tensor_scalar(
                            pw_t[:, 0:f], pw_bf[b][:, 0:f], float(j), c_lutf[:, j:j + 1],
                            ALU.is_equal, ALU.mult))
                        pr.emit("gpsimd", lambda e, b=b, f=f: e.tensor_tensor(
                            pw_w[b][:, 0:f], pw_w[b][:, 0:f], pw_t[:, 0:f], ALU.add))
                    yield
                dma("gpsimd", sPS[b], gdst[k, :, c0:c0 + f], pw_w[b][:, 0:f])
                yield

        _pg = _pool_gen()
        _pg_done = [False]

        def pump_pool(n):
            if _pg_done[0]:
                return
            try:
                for _ in range(n):
                    next(_pg)
            except StopIteration:
                _pg_done[0] = True

        def pump_pool_all():
            while not _pg_done[0]:
                pump_pool(1000)

        _gg = _gather_gen()
        _gg_done = [False]
        gather_mat_tok = {}
        gather_tile_tok = {}

        def pump(n):
            pump_pool(max(1, n // 3))
            if _gg_done[0]:
                return
            try:
                for _ in range(n):
                    next(_gg)
            except StopIteration:
                _gg_done[0] = True

        def pump_all():
            while not _gg_done[0]:
                pump(1000)
            pump_pool_all()



        def _early_out():
            pump_all()
            pump_pool_all()
            pr.wait("sync", eV, eV.n)
            pr.wait("sync", eA, eA.n)
            dma("sync", sOT[0], yout[0:P, :], x_win[0][:, :])
            for s_ in (sOT[0], sGW[0], sGW[1], sXH, sFK[0], sFK[1]):
                pr.wait("sync", s_)
            block = ctx.enter_context(nc.Block())
            pr.install(block)

        if limit < 1:
            _early_out()
            return nc

        def x_src(t, c):
            if t == 3:
                return x_own[c * P:(c + 1) * P, :]
            return x_pre[t * T + c * P:t * T + (c + 1) * P, :]

        fk_used = [(eP, 0), (eP, 0)]
        x_user = [(eA, 0)]
        tok_kproj = [0] * NTILE
        tok_zk_last = 0

        with ExitStack() as s_p1:
            ln_dump = sb("ln_dump", [P, D], F32, s_p1)
            tm_scr = sb("tm_scr", [P, D], BF16, s_p1)
            W_res = sb("W_res", [P, DC * D], BF16, s_p1)
            for dc in range(DC):
                for sc in range(4):
                    dma("gpsimd", sWR, W_res[:, dc * D + sc * 512:dc * D + (sc + 1) * 512],
                        wkT[dc * P:(dc + 1) * P, sc * 512:(sc + 1) * 512])

            for t in range(NTILE):
                for c in range(NCH):
                    xw = x_win[c % 1]
                    sxw = sXW[c % 1]
                    wait_tok("sync", x_user[c % 1])
                    if t > 0 and c == 0:
                        pr.wait("sync", eP, tok_kproj[t - 1])
                        pr.wait("sync", sXH)
                    dma("sync", sxw, xw[:, :], x_src(t, c))
                    pr.wait("scalar", sxw)
                    act(ln_dump[:, :], xw[:, :], AF.Identity, accum=s_sum[:, :], mark=1)
                    act(ln_dump[:, :], xw[:, :], AF.Square, accum=s_sq[:, :], mark=1)
                    pr.wait("vector", eA, eA.n)
                    v_op(lambda e: e.tensor_scalar(s_m[:, :], s_sum[:, :], 1.0 / D, None, ALU.mult),
                         mark=1)
                    v_op(lambda e: e.tensor_scalar(s_t1[:, :], s_sq[:, :], 1.0 / D, None, ALU.mult),
                         mark=1)
                    pr.wait("vector", eV, eV.n)
                    v_op(lambda e: e.tensor_tensor(s_v[:, :], s_m[:, :], s_m[:, :], ALU.mult), mark=1)
                    pr.wait("vector", eV, eV.n)
                    v_op(lambda e: e.tensor_tensor(s_v[:, :], s_t1[:, :], s_v[:, :], ALU.subtract),
                         mark=1)
                    pr.wait("scalar", eV, eV.n)
                    act(s_t1[:, :], s_v[:, :], AF.Ln, bias=c_cvec[:, 0:1], mark=1)
                    pr.wait("scalar", eA, eA.n)
                    act(s_a[:, :], s_t1[:, :], AF.Exp, scale=-0.5, mark=1)
                    pr.wait("vector", eA, eA.n)
                    v_op(lambda e: e.scalar_tensor_tensor(
                        s_b[:, :], s_m[:, :], -1.0, s_a[:, :], ALU.mult, ALU.mult), mark=1)
                    pr.wait("scalar", eV, eV.n)
                    pr.wait("scalar", eA, eA.n)
                    act(tm_scr[:, :], xw[:, :], AF.Identity, bias=s_b[:, :], scale=s_a[:, :],
                        mark=1)
                    x_user[c % 1] = (eA, eA.n)
                    tok_apply = eA.n
                    wait_tok("tensor", pt_free)
                    pr.wait("tensor", eA, tok_apply)
                    for dc in range(DC):
                        tr(tm_scr[:, dc * P:(dc + 1) * P], pt[:, dc * P:(dc + 1) * P],
                           mark=1 if dc == DC - 1 else 0)
                    pr.wait("scalar", eP, eP.n)
                    act(A_xh[:, :].rearrange("p (d t) -> p d t", t=T)[:, :, c * P:(c + 1) * P],
                        pt[:, :].rearrange("p (d q) -> p d q", q=P), AF.Identity, mark=1)
                    pt_free = (eA, eA.n)
                    pump(12)
                tok_xh = eA.n

                pr.wait("sync", eA, tok_xh)
                dma("sync", sXH, xh_d[t, :, :], A_xh[:, :])

                pr.wait("tensor", sWR)
                pr.wait("tensor", eA, tok_xh)
                if t > 0:
                    pr.wait("scalar", eP, tok_zk_last)
                pp = 0
                TH = T // 2
                for oc in range(DC):
                    for th in range(2):
                        bank = pa[pp]
                        wait_tok("tensor", pa_free[pp])
                        for dc in range(DC):
                            mm(W_res[:, dc * D + oc * P: dc * D + (oc + 1) * P],
                               A_xh[:, dc * T + th * TH: dc * T + (th + 1) * TH],
                               bank[:, 0:TH], start=(dc == 0), stop=(dc == DC - 1),
                               mark=1 if dc == DC - 1 else 0)
                        tok_mm = eP.n
                        pr.wait("scalar", eP, tok_mm)
                        act(A_k[:, oc * T + th * TH: oc * T + (th + 1) * TH], bank[:, 0:TH],
                            AF.Identity, mark=1)
                        pa_free[pp] = (eA, eA.n)
                        pp = (pp + 1) % 2
                        pump(6)
                tok_kproj[t] = eP.n
                tok_kfm = eA.n

                for c in range(NCH):
                    slot = c % 2
                    pr.wait("tensor", eA, tok_kfm)
                    for i in range(4):
                        wait_tok("tensor", pa_free[i])
                    for p in range(PAIR):
                        mm(A_k[:, p * T + c * P:p * T + (c + 1) * P],
                           c_wfk[:, p * P:(p + 1) * P],
                           pa[p // 4][:, (p % 4) * P:(p % 4 + 1) * P],
                           start=True, stop=True, mark=1 if p == PAIR - 1 else 0)
                    tok_zk = eP.n
                    tok_zk_last = tok_zk
                    pr.wait("scalar", eP, tok_zk)
                    pr.wait("scalar", sFK[slot])
                    wait_tok("scalar", fk_used[slot])
                    for i in range(4):
                        act(fk_scr[slot][:, i * 512:(i + 1) * 512], pa[i][:, :], AF.Exp,
                            mark=1 if i == 3 else 0)
                    tok_exp = eA.n
                    for i in range(4):
                        pa_free[i] = (eA, tok_exp)
                    pr.wait("vector", eA, tok_exp)
                    v_op(lambda e, s=slot: e.tensor_reduce(
                        s_sk[:, :], fk_scr[s][:, :].rearrange("p (h f) -> p h f", f=HD),
                        AX, ALU.add), mark=1)
                    pr.wait("vector", eV, eV.n)
                    v_op(lambda e: e.reciprocal(s_rsk[:, :], s_sk[:, :]), mark=1)
                    pr.wait("vector", eV, eV.n)
                    v_op(lambda e, t=t: e.tensor_scalar(
                        s_rsk[:, :], s_rsk[:, :], c_pmask[:, t:t + 1], None, ALU.mult), mark=1)
                    pr.wait("vector", eV, eV.n)
                    v_op(lambda e, s=slot: e.tensor_tensor(
                        fk_scr[s][:, :].rearrange("p (h f) -> p h f", f=HD),
                        fk_scr[s][:, :].rearrange("p (h f) -> p h f", f=HD),
                        s_rsk[:, :].rearrange("p (h o) -> p h o", o=1).broadcast_to([P, H, HD]),
                        ALU.mult), mark=1)
                    pr.wait("sync", eV, eV.n)
                    dma("sync", sFK[slot], fk_d[t, c, :, :], fk_scr[slot][:, :])
                    pump(12)

        if limit < 2:
            _early_out()
            return nc
        tok_vproj_last = 0
        with ExitStack() as s_p2:
            W_res = sb("W_res2", [P, DC * D], BF16, s_p2)
            pr.wait("gpsimd", eP, eP.n)
            for dc in range(DC):
                for sc in range(4):
                    dma("gpsimd", sWR, W_res[:, dc * D + sc * 512:dc * D + (sc + 1) * 512],
                        wvT[dc * P:(dc + 1) * P, sc * 512:(sc + 1) * 512])

            tok_zk_guard = tok_zk_last
            for t in range(NTILE):
                pr.wait("sync", eP, eP.n)
                pr.wait("sync", sXH)
                dma("sync", sXH, A_xh[:, :], xh_d[t, :, :])
                pr.wait("tensor", sXH)
                pr.wait("tensor", sWR)
                if t == 0:
                    pr.wait("scalar", eP, tok_zk_guard)
                pp = 0
                for tb in range(NCH):
                    for og in range(4):
                        bank = pa[pp]
                        wait_tok("tensor", pa_free[pp])
                        for dc in range(DC):
                            mm(A_xh[:, dc * T + tb * P: dc * T + (tb + 1) * P],
                               W_res[:, dc * D + og * 512: dc * D + (og + 1) * 512],
                               bank[:, :], start=(dc == 0), stop=(dc == DC - 1),
                               mark=1 if dc == DC - 1 else 0)
                        tok_mm = eP.n
                        pr.wait("scalar", eP, tok_mm)
                        act(A_k[:, tb * D + og * 512: tb * D + (og + 1) * 512], bank[:, :],
                            AF.Identity, mark=1)
                        pa_free[pp] = (eA, eA.n)
                        pp = (pp + 1) % 2
                        pump(4)
                tok_vproj_last = eP.n
                tok_vtm = eA.n

                if t == 3:
                    break

                for c in range(NCH):
                    slot = c % 2
                    wait_tok("sync", fk_used[slot])
                    pr.wait("sync", eV, eV.n)
                    dma("sync", sFK[slot], fk_scr[slot][:, :], fk_d[t, c, :, :])
                    pr.wait("tensor", sFK[slot])
                    pr.wait("tensor", eA, tok_vtm)
                    for i in range(4):
                        wait_tok("tensor", pa_free[i])
                    for i in range(2):
                        wait_tok("tensor", pb_free[i])
                    for p in range(PAIR):
                        mm(fk_scr[slot][:, p * P:(p + 1) * P],
                           A_k[:, c * D + p * P:c * D + (p + 1) * P],
                           pa[p // 4][:, (p % 4) * P:(p % 4 + 1) * P],
                           start=True, stop=True)
                        mm(fk_scr[slot][:, p * P:(p + 1) * P], c_ones[:, :],
                           pb[p // 8][:, (p % 8):(p % 8) + 1],
                           start=True, stop=True, mark=1 if p == PAIR - 1 else 0)
                    tok_dmm = eP.n
                    fk_used[slot] = (eP, tok_dmm)
                    pr.wait("vector", eP, tok_dmm)
                    state_add_ops(slot, masked_tile=t)
                    st_tok = eV.n
                    for i in range(4):
                        pa_free[i] = (eV, st_tok)
                    for i in range(2):
                        pb_free[i] = (eV, st_tok)
                    pump(12)

            v_op(lambda e: e.tensor_copy(st_KVb[:, :], st_KV[:, :]))
            v_op(lambda e: e.tensor_scalar(st_ke[:, :], st_k[:, :], float(ATT_EPS), None,
                                           ALU.add), mark=1)
            tok_state = eV.n

        if limit < 3:
            _early_out()
            return nc
        tok_scan = 0
        with ExitStack() as s_aq:
            A_q = sb("A_q", [P, DC * T], BF16, s_aq)

            with ExitStack() as s_q:
                W_res = sb("W_res3", [P, DC * D], BF16, s_q)
                pr.wait("gpsimd", eP, eP.n)
                for dc in range(DC):
                    for sc in range(4):
                        dma("gpsimd", sWR, W_res[:, dc * D + sc * 512:dc * D + (sc + 1) * 512],
                            wqT[dc * P:(dc + 1) * P, sc * 512:(sc + 1) * 512])
                pr.wait("sync", eP, eP.n)
                dma("sync", sXH, A_xh[:, :], xh_d[3, :, :])
                pr.wait("tensor", sXH)
                pr.wait("tensor", sWR)
                pp = 0
                TH = T // 2
                for oc in range(DC):
                    for th in range(2):
                        bank = pa[pp]
                        wait_tok("tensor", pa_free[pp])
                        for dc in range(DC):
                            mm(W_res[:, dc * D + oc * P: dc * D + (oc + 1) * P],
                               A_xh[:, dc * T + th * TH: dc * T + (th + 1) * TH],
                               bank[:, 0:TH], start=(dc == 0), stop=(dc == DC - 1),
                               mark=1 if dc == DC - 1 else 0)
                        tok_mm = eP.n
                        pr.wait("scalar", eP, tok_mm)
                        act(A_q[:, oc * T + th * TH: oc * T + (th + 1) * TH], bank[:, 0:TH],
                            AF.Identity, mark=1)
                        pa_free[pp] = (eA, eA.n)
                        pp = (pp + 1) % 2
                        pump(8)
                tok_qproj = eP.n
                tok_qfm = eA.n
            if limit < 2.4:
                _early_out()
                return nc

            with ExitStack() as s_scan:
                fq_bd = sb("fq_bd", [P, 2 * PAIR * P], BF16, s_scan)
                fkf_scr = sb("fkf_scr", [P, PAIR * P], BF16, s_scan)
                stm_scr = sb("stm_scr", [P, 16 * P], BF16, s_scan)
                ytmp = sb("ytmp", [P, 1024], BF16, s_scan)

                pr.wait("tensor", eA, tok_qfm)
                pr.wait("tensor", eV, tok_state)
                pr.wait("vector", eP, tok_qproj)
                v_op(lambda e: e.memset(fq_bd[:, :], 0.0), mark=1)
                pr.wait("scalar", eV, eV.n)
                kvb_tok = (eV, tok_state)
                ytok = 0
                for c in range(NCH):
                    slot = c % 2
                    wait_tok("sync", fk_used[slot])
                    pr.wait("sync", eV, eV.n)
                    dma("sync", sFK[slot], fk_scr[slot][:, :], fk_d[3, c, :, :])

                    wait_tok("tensor", kvb_tok)
                    for i in range(4):
                        wait_tok("tensor", pa_free[i])
                    for p in range(PAIR):
                        mm(c_wfq[:, p * P:(p + 1) * P],
                           A_q[:, p * T + c * P:p * T + (c + 1) * P],
                           pa[p // 4][:, (p % 4) * P:(p % 4 + 1) * P],
                           start=True, stop=True, mark=1 if p == PAIR - 1 else 0)
                    tok_zq = eP.n
                    pr.wait("scalar", eP, tok_zq)
                    for i in range(4):
                        act(fq_bd[0:64, i * 1024:(i + 1) * 1024].rearrange(
                                "p (a c) -> p a c", c=256)[:, :, 0:128],
                            pa[i][0:64, :].rearrange("p (a c) -> p a c", c=128),
                            AF.Exp)
                        act(fq_bd[64:128, i * 1024:(i + 1) * 1024].rearrange(
                                "p (a c) -> p a c", c=256)[:, :, 128:256],
                            pa[i][64:128, :].rearrange("p (a c) -> p a c", c=128),
                            AF.Exp, mark=1 if i == 3 else 0)
                    tok_fq = eA.n
                    for i in range(4):
                        pa_free[i] = (eA, tok_fq)
                    if limit < 2.5:
                        continue

                    wait_tok("tensor", pt_free)
                    pr.wait("tensor", sFK[slot])
                    for p in range(PAIR):
                        tr(fk_scr[slot][:, p * P:(p + 1) * P], pt[:, p * P:(p + 1) * P],
                           mark=1 if p == PAIR - 1 else 0)
                    tok_fkT = eP.n
                    pr.wait("scalar", eP, tok_fkT)
                    act(fkf_scr[:, 0:1024], pt[:, 0:1024], AF.Identity)
                    act(fkf_scr[:, 1024:2048], pt[:, 1024:2048], AF.Identity, mark=1)
                    tok_fkf = eA.n
                    pt_free = (eA, tok_fkf)
                    if limit < 2.6:
                        continue

                    for g in range(0 if SKIP_ST else 2):
                        h0 = g * 16
                        pr.wait("tensor", eA, tok_fkf)
                        for i in range(4):
                            wait_tok("tensor", pa_free[i])
                        for lp in range(8):
                            p_ = g * 8 + lp
                            mm(fkf_scr[:, p_ * P:(p_ + 1) * P],
                               fq_bd[:, p_ * 256:(p_ + 1) * 256],
                               pa[lp // 2][:, (lp % 2) * 256:(lp % 2 + 1) * 256],
                               start=True, stop=True, mark=1 if lp == 7 else 0)
                        tok_st = eP.n
                        if SKIP_MASK:
                            for i in range(4):
                                pa_free[i] = (eP, tok_st)
                            continue
                        pump(8)
                        pr.wait("vector", eP, tok_st)
                        for i in range(4):
                            v_op(lambda e, i=i: e.tensor_tensor(
                                stm_scr[:, i * 512:(i + 1) * 512], pa[i][:, :],
                                c_utri[:, :], ALU.mult), mark=1 if i == 3 else 0)
                        tok_stm = eV.n
                        for i in range(4):
                            pa_free[i] = (eV, tok_stm)
                        if limit < 2.7:
                            continue

                        if SKIP_Y:
                            continue
                        pr.wait("tensor", eV, tok_stm)
                        for i in range(2):
                            wait_tok("tensor", pb_free[i])
                        wait_tok("tensor", pa_free[0])
                        for i in range(16):
                            h = h0 + i
                            p_, s_ = h // 2, h % 2
                            fqb = fq_bd[:, p_ * 256 + s_ * 128: p_ * 256 + (s_ + 1) * 128]
                            mm(stm_scr[:, i * P:(i + 1) * P],
                               A_k[:, c * D + h * HD:c * D + (h + 1) * HD],
                               pb[i // 8][:, (i % 8) * HD:(i % 8 + 1) * HD],
                               start=True, stop=False)
                            mm(fqb, st_KVb[:, p_ * HD:(p_ + 1) * HD],
                               pb[i // 8][:, (i % 8) * HD:(i % 8 + 1) * HD],
                               start=False, stop=True)
                            mm(stm_scr[:, i * P:(i + 1) * P], c_ones[:, :],
                               pa[0][:, i:i + 1], start=True, stop=False)
                            mm(fqb, st_ke[:, p_:p_ + 1],
                               pa[0][:, i:i + 1], start=False, stop=True,
                               mark=1 if i == 15 else 0)
                        tok_y = eP.n
                        pump(8)
                        pr.wait("vector", eP, tok_y)
                        v_op(lambda e: e.reciprocal(s_rd[:, :], pa[0][:, 0:16]), mark=1)
                        pr.wait("vector", eV, eV.n)
                        for i in range(2):
                            v_op(lambda e, i=i: e.tensor_tensor(
                                ytmp[:, i * 512:(i + 1) * 512].rearrange(
                                    "p (a f) -> p a f", f=HD),
                                pb[i][:, :].rearrange("p (a f) -> p a f", f=HD),
                                s_rd[:, 8 * i:8 * i + 8].rearrange(
                                    "p (a o) -> p a o", o=1).broadcast_to([P, 8, HD]),
                                ALU.mult), mark=1 if i == 1 else 0)
                        tok_yd = eV.n
                        pa_free[0] = (eV, tok_yd)
                        pb_free[0] = (eV, tok_yd)
                        pb_free[1] = (eV, tok_yd)
                        if limit < 2.8:
                            continue

                        wait_tok("tensor", pt_free)
                        pr.wait("tensor", eV, tok_yd)
                        for q_ in range(8):
                            tr(ytmp[:, q_ * P:(q_ + 1) * P],
                               pt[:, (g * 8 + q_) * P:(g * 8 + q_ + 1) * P],
                               mark=1 if q_ == 7 else 0)
                        tok_yT = eP.n
                        pr.wait("scalar", eP, tok_yT)
                        act(A_xh[:, :].rearrange("p (d t) -> p d t", t=T)[
                                :, g * 8:(g + 1) * 8, c * P:(c + 1) * P],
                            pt[:, g * 1024:(g + 1) * 1024].rearrange("p (d q) -> p d q", q=P),
                            AF.Identity, mark=1)
                        if g == 1:
                            pt_free = (eA, eA.n)
                        ytok = eA.n
                        pump(10)

                    if limit < 2.85:
                        continue
                    for i in range(4):
                        wait_tok("tensor", pa_free[i])
                    for i in range(2):
                        wait_tok("tensor", pb_free[i])
                    for p in range(PAIR):
                        mm(fk_scr[slot][:, p * P:(p + 1) * P],
                           A_k[:, c * D + p * P:c * D + (p + 1) * P],
                           pa[p // 4][:, (p % 4) * P:(p % 4 + 1) * P],
                           start=True, stop=True)
                        mm(fk_scr[slot][:, p * P:(p + 1) * P], c_ones[:, :],
                           pb[p // 8][:, (p % 8):(p % 8) + 1],
                           start=True, stop=True, mark=1 if p == PAIR - 1 else 0)
                    tok_dmm = eP.n
                    fk_used[slot] = (eP, tok_dmm)
                    if limit < 2.9:
                        for i in range(4):
                            pa_free[i] = (eP, tok_dmm)
                        for i in range(2):
                            pb_free[i] = (eP, tok_dmm)
                        continue
                    pump(8)
                    pr.wait("vector", eP, tok_dmm)
                    state_add_ops(slot, masked_tile=3)
                    st_tok2 = eV.n
                    pr.wait("scalar", eV, st_tok2)
                    act(st_KVb[:, :], st_KV[:, :], AF.Identity)
                    pr.emit("scalar",
                            lambda e: e.activation(st_ke[:, :], st_k[:, :], AF.Identity,
                                                   bias=c_cvec[:, 1:2]),
                            [(eSt, 1)])
                    if limit >= 2.95 and c < STATE_CHUNKS:
                        kvb_tok = (eSt, eSt.n)
                    for i in range(4):
                        pa_free[i] = (eV, st_tok2)
                    for i in range(2):
                        pb_free[i] = (eV, st_tok2)
                    pump(16)
                tok_scan = ytok

        if limit < 4:
            _early_out()
            return nc
        R2 = sb("R2", [P, NCH * D], BF16)
        with ExitStack() as s_o:
            W_res = sb("W_res4", [P, DC * D], BF16, s_o)
            pr.wait("gpsimd", eP, eP.n)
            for dc in range(DC):
                for sc in range(4):
                    dma("gpsimd", sWR, W_res[:, dc * D + sc * 512:dc * D + (sc + 1) * 512],
                        woT[dc * P:(dc + 1) * P, sc * 512:(sc + 1) * 512])
            pr.wait("tensor", sWR)
            pr.wait("tensor", eA, tok_scan)

            x_user2 = list(x_user)
            pp = 0
            for tb in range(NCH):
                xw = x_win[tb % 1]
                sxw = sXW[tb % 1]
                wait_tok("sync", x_user2[tb % 1])
                dma("sync", sxw, xw[:, :], x_own[tb * P:(tb + 1) * P, :])
                for og in range(4):
                    bank = pa[pp]
                    wait_tok("tensor", pa_free[pp])
                    for dc in range(DC):
                        mm(A_xh[:, dc * T + tb * P: dc * T + (tb + 1) * P],
                           W_res[:, dc * D + og * 512: dc * D + (og + 1) * 512],
                           bank[:, :], start=(dc == 0), stop=(dc == DC - 1),
                           mark=1 if dc == DC - 1 else 0)
                    tok_mm = eP.n
                    pump(8)
                    pr.wait("vector", eP, tok_mm)
                    pr.wait("vector", sxw)
                    v_op(lambda e, tb=tb, og=og, bank=bank, xw=xw: e.scalar_tensor_tensor(
                        R2[:, tb * D + og * 512: tb * D + (og + 1) * 512],
                        xw[:, og * 512:(og + 1) * 512], 1.0, bank[:, :],
                        ALU.mult, ALU.add), mark=1)
                    pa_free[pp] = (eV, eV.n)
                    pp = (pp + 1) % 2
                x_user2[tb % 1] = (eV, eV.n)
            tok_r2 = eV.n

        with ExitStack() as s_ln2:
            ln_dump = sb("ln_dump2", [P, D], F32, s_ln2)
            tm_scr = sb("tm_scr2", [P, D], BF16, s_ln2)
            pr.wait("vector", eV, tok_r2)
            pr.wait("scalar", eV, tok_r2)
            for c in range(NCH):
                pump(6)
                act(ln_dump[:, :], R2[:, c * D:(c + 1) * D], AF.Identity, accum=s_sum[:, :],
                    mark=1)
                act(ln_dump[:, :], R2[:, c * D:(c + 1) * D], AF.Square, accum=s_sq[:, :],
                    mark=1)
                pr.wait("vector", eA, eA.n)
                v_op(lambda e: e.tensor_scalar(s_m[:, :], s_sum[:, :], 1.0 / D, None, ALU.mult),
                     mark=1)
                v_op(lambda e: e.tensor_scalar(s_t1[:, :], s_sq[:, :], 1.0 / D, None, ALU.mult),
                     mark=1)
                pr.wait("vector", eV, eV.n)
                v_op(lambda e: e.tensor_tensor(s_v[:, :], s_m[:, :], s_m[:, :], ALU.mult), mark=1)
                pr.wait("vector", eV, eV.n)
                v_op(lambda e: e.tensor_tensor(s_v[:, :], s_t1[:, :], s_v[:, :], ALU.subtract),
                     mark=1)
                pr.wait("scalar", eV, eV.n)
                act(s_t1[:, :], s_v[:, :], AF.Ln, bias=c_cvec[:, 0:1], mark=1)
                pr.wait("scalar", eA, eA.n)
                act(s_a[:, :], s_t1[:, :], AF.Exp, scale=-0.5, mark=1)
                pr.wait("vector", eA, eA.n)
                v_op(lambda e: e.scalar_tensor_tensor(
                    s_b[:, :], s_m[:, :], -1.0, s_a[:, :], ALU.mult, ALU.mult), mark=1)
                pr.wait("scalar", eV, eV.n)
                pr.wait("scalar", eA, eA.n)
                act(tm_scr[:, :], R2[:, c * D:(c + 1) * D], AF.Identity,
                    bias=s_b[:, :], scale=s_a[:, :], mark=1)
                tok_apply = eA.n
                wait_tok("tensor", pt_free)
                pr.wait("tensor", eA, tok_apply)
                for dc in range(DC):
                    tr(tm_scr[:, dc * P:(dc + 1) * P], pt[:, dc * P:(dc + 1) * P],
                       mark=1 if dc == DC - 1 else 0)
                pr.wait("scalar", eP, eP.n)
                act(A_xh[:, :].rearrange("p (d t) -> p d t", t=T)[:, :, c * P:(c + 1) * P],
                    pt[:, :].rearrange("p (d q) -> p d q", q=P), AF.Identity, mark=1)
                pt_free = (eA, eA.n)
            tok_xh2 = eA.n

        if limit < 5:
            _early_out()
            return nc
        with ExitStack() as s_ffn:
            h_mid = sb("h_mid", [P, IOC * 512], BF16, s_ffn)
            silu_sb = [sb(f"silu{i}", [P, 512], BF16, s_ffn) for i in range(4)]
            htm_sb = [sb(f"htm{i}", [P, 512], BF16, s_ffn) for i in range(4)]
            wt = [sb(f"wt{i}", [P, 512], BF16, s_ffn) for i in range(4)]
            otile = [sb(f"ot{i}", [P, 512], F32, s_ffn) for i in range(2)]

            def wait_mat(dst):
                while gather_mat_tok.get(id(dst)) is None and not _gg_done[0]:
                    pump(31)
                tok = gather_mat_tok.get(id(dst)) or (sGW[0].n, sGW[1].n)
                for eng in ("scalar", "tensor"):
                    pr.wait(eng, sGW[0], tok[0])
                    pr.wait(eng, sGW[1], tok[1])
            pr.wait("vector", eV, eV.n)
            pr.wait("scalar", eA, tok_xh2)
            pr.wait("tensor", eA, tok_xh2)

            wt_used = [(eP, 0)] * 4

            IO_G = [(gi * 512, min(512, IO - gi * 512)) for gi in range((IO + 511) // 512)]

            for tp in range(2):
                if tp == 0:
                    wait_mat(wg_s)
                for gi, (c0, gw) in enumerate(IO_G):
                    pump(40)
                    for dc in range(DC):
                        ws = dc % 4
                        wait_tok("scalar", wt_used[ws])
                        dma("scalar", sWT[ws], wt[ws][:, 0:gw], wg_s[dc, :, c0:c0 + gw])
                        if dc == 0:
                            for i in range(4):
                                wait_tok("tensor", pa_free[i])
                        pr.wait("tensor", sWT[ws])
                        for tb in range(4):
                            mm(A_xh[:, dc * T + tp * 512 + tb * P:
                                    dc * T + tp * 512 + (tb + 1) * P],
                               wt[ws][:, 0:gw], pa[tb][:, 0:gw],
                               start=(dc == 0), stop=(dc == DC - 1),
                               mark=1 if tb == 3 else 0)
                        wt_used[ws] = (eP, eP.n)
                    tok_gate = eP.n
                    pr.wait("scalar", eP, tok_gate)
                    for tb in range(4):
                        act(silu_sb[tb][:, 0:gw], pa[tb][:, 0:gw], AF.Silu,
                            mark=1 if tb == 3 else 0)
                    tok_silu = eA.n
                    if tp == 0 and gi == 0:
                        wait_mat(wu_s)
                    pump(40)
                    pr.wait("tensor", eA, tok_silu)
                    for dc in range(DC):
                        ws = dc % 4
                        wait_tok("scalar", wt_used[ws])
                        dma("scalar", sWT[ws], wt[ws][:, 0:gw], wu_s[dc, :, c0:c0 + gw])
                        pr.wait("tensor", sWT[ws])
                        for tb in range(4):
                            mm(A_xh[:, dc * T + tp * 512 + tb * P:
                                    dc * T + tp * 512 + (tb + 1) * P],
                               wt[ws][:, 0:gw], pa[tb][:, 0:gw],
                               start=(dc == 0), stop=(dc == DC - 1),
                               mark=1 if tb == 3 else 0)
                        wt_used[ws] = (eP, eP.n)
                    tok_up = eP.n
                    pr.wait("vector", eP, tok_up)
                    pr.wait("vector", eA, tok_silu)
                    for tb in range(4):
                        v_op(lambda e, tb=tb, gw=gw: e.tensor_tensor(
                            htm_sb[tb][:, 0:gw], pa[tb][:, 0:gw], silu_sb[tb][:, 0:gw],
                            ALU.mult), mark=1 if tb == 3 else 0)
                    tok_htm = eV.n
                    for i in range(4):
                        pa_free[i] = (eV, tok_htm)
                    wait_tok("tensor", pt_free)
                    pr.wait("tensor", eV, tok_htm)
                    nio = gw // P
                    for tb in range(4):
                        for s2 in range(nio):
                            tr(htm_sb[tb][:, s2 * P:(s2 + 1) * P],
                               pt[:, (tb * 4 + s2) * P:(tb * 4 + s2 + 1) * P],
                               mark=1 if (tb == 3 and s2 == nio - 1) else 0)
                    tok_hT = eP.n
                    pr.wait("scalar", eP, tok_hT)
                    for tb in range(4):
                        for s2 in range(nio):
                            ioc = gi * 4 + s2
                            act(h_mid[:, ioc * 512 + tb * P: ioc * 512 + (tb + 1) * P],
                                pt[:, (tb * 4 + s2) * P:(tb * 4 + s2 + 1) * P],
                                AF.Identity, mark=1 if (tb == 3 and s2 == nio - 1) else 0)
                    pt_free = (eA, eA.n)
                tok_hmid = eA.n

                pr.wait("tensor", eA, tok_hmid)
                for hg in range(4):
                    pump(8)
                    for k in range(IOC):
                        if tp == 0:
                            while gather_tile_tok.get(k) is None and not _gg_done[0]:
                                pump(31)
                        elif k % 4 == 0:
                            pump(3)
                        ws = k % 4
                        wait_tok("scalar", wt_used[ws])
                        if tp == 0:
                            tk = gather_tile_tok.get(k, (sGW[0].n, sGW[1].n))
                            pr.wait("scalar", sGW[0], tk[0])
                            pr.wait("scalar", sGW[1], tk[1])
                        dma("scalar", sWT[ws], wt[ws][:, :], wd_s[k, :, hg * 512:(hg + 1) * 512])
                        if k == 0:
                            for i in range(4):
                                wait_tok("tensor", pa_free[i])
                        pr.wait("tensor", sWT[ws])
                        for tb in range(4):
                            mm(h_mid[:, k * 512 + tb * P: k * 512 + (tb + 1) * P],
                               wt[ws][:, :], pa[tb][:, :],
                               start=(k == 0), stop=(k == IOC - 1),
                               mark=1 if tb == 3 else 0)
                        wt_used[ws] = (eP, eP.n)
                    tok_down = eP.n
                    pr.wait("vector", eP, tok_down)
                    for tb in range(4):
                        os_ = tb % 2
                        tc = tp * 4 + tb
                        pr.wait("vector", sOT[os_])
                        v_op(lambda e, tc=tc, hg=hg, os_=os_, tb=tb: e.scalar_tensor_tensor(
                            otile[os_][:, :],
                            R2[:, tc * D + hg * 512: tc * D + (hg + 1) * 512], 1.0,
                            pa[tb][:, :], ALU.mult, ALU.add), mark=1)
                        tok_ot = eV.n
                        pr.wait("sync", eV, tok_ot)
                        dma("sync", sOT[os_], yout[tc * P:(tc + 1) * P, hg * 512:(hg + 1) * 512],
                            otile[os_][:, :])
                    for i in range(4):
                        pa_free[i] = (eV, eV.n)

            pr.wait("sync", sOT[0])
            pr.wait("sync", sOT[1])

        block = ctx.enter_context(nc.Block())
        pr.install(block)

    return nc


_NC_CACHE = [None]


def _host_pack(inputs):
    f32 = np.float32
    x = np.asarray(inputs["L0_attn_hidden_states"], f32).reshape(2 * 4096, D)
    wq = np.ascontiguousarray(np.asarray(inputs["L0_attn_Wq"], f32).T)
    wk = np.ascontiguousarray(np.asarray(inputs["L0_attn_Wk"], f32).T)
    wv = np.ascontiguousarray(np.asarray(inputs["L0_attn_Wv"], f32).T)
    wo = np.ascontiguousarray(np.asarray(inputs["L0_attn_Wo"], f32).T)

    def blockdiag(wf):
        wf = np.asarray(wf, f32)
        bd = np.zeros((P, PAIR * P), f32)
        for p in range(PAIR):
            for s in range(2):
                bd[s * 64:(s + 1) * 64, p * P + s * 64: p * P + (s + 1) * 64] = wf[2 * p + s]
        return bd

    wfq_bd = blockdiag(inputs["L0_attn_Wfq"])
    wfk_bd = blockdiag(inputs["L0_attn_Wfk"])
    lut = np.asarray(inputs["L0_ffn_lut"], f32)
    lutbc = np.ascontiguousarray(np.broadcast_to(lut, (P, 16)))

    def idx8T(a):
        aT = np.ascontiguousarray(np.asarray(a, np.int32).T)
        return np.ascontiguousarray(aT.view(np.int8).reshape(aT.shape[0], aT.shape[1], 4)[:, :, 0])

    g8 = idx8T(inputs["L0_ffn_gate_idx"])
    u8 = idx8T(inputs["L0_ffn_up_idx"])
    d8 = idx8T(inputs["L0_ffn_down_idx"])

    utri = np.ascontiguousarray(np.tile(np.triu(np.ones((P, P), f32)), (1, 4)))
    ones_col = np.ones((P, 1), f32)
    cvec = np.zeros((P, 2), f32); cvec[:, 0] = LN_EPS; cvec[:, 1] = ATT_EPS
    ident = np.eye(P, dtype=f32)

    in_maps = []
    for core in range(8):
        seq, qpos = core // 4, core % 4
        pm = np.zeros((P, NTILE), f32)
        for t in range(3):
            if t < qpos:
                pm[:, t] = 1.0
        pm[:, 3] = 1.0
        in_maps.append({
            "x_own": np.ascontiguousarray(x[seq * 4096 + qpos * T: seq * 4096 + (qpos + 1) * T]),
            "x_pre": np.ascontiguousarray(x[seq * 4096: seq * 4096 + 3 * T]),
            "wqT": wq, "wkT": wk, "wvT": wv, "woT": wo,
            "wfq_bd": wfq_bd, "wfk_bd": wfk_bd, "lutbc": lutbc,
            "g8": g8, "u8": u8, "d8": d8,
            "utri": utri, "ones_in": ones_col, "ident_in": ident,
            "cvec_in": cvec,
            "pmask_in": pm,
        })
    return in_maps


def kernel(**inputs):
    if _NC_CACHE[0] is None:
        _NC_CACHE[0] = build_program()
    nc = _NC_CACHE[0]
    in_maps = _host_pack(inputs)
    res = run_bass_kernel_spmd(nc, in_maps, core_ids=list(range(8)))
    out = np.concatenate([np.asarray(r["yout"], np.float32) for r in res.results], axis=0)
    return out.reshape(2, 4096, D)


if __name__ == "__main__":
    nc = build_program()
    n = sum(1 for f in nc.m.functions for b in f.blocks for _ in b.instructions)
    print("built ok, instructions:", n)


# version 20 (final)
# speedup vs baseline: 1.1948x; 1.0334x over previous
import numpy as np
from contextlib import ExitStack

import concourse.bass as bass
import concourse.mybir as mybir
from concourse.bass_utils import run_bass_kernel_spmd

F32 = mybir.dt.float32
BF16 = mybir.dt.bfloat16
I8 = mybir.dt.int8
U16 = mybir.dt.uint16
AF = mybir.ActivationFunctionType
ALU = mybir.AluOpType
AX = mybir.AxisListType.X

P = 128
D = 2048
DC = D // P
T = 1024
NCH = T // P
H = 32
HD = 64
PAIR = 16
IO = 5504
IOC = IO // P
NTILE = 4
LN_EPS = 1e-5
STATE_CHUNKS = 8
SKIP_Y = False
SKIP_ST = False
SKIP_MASK = False
ST_HEADS = 3
DEBUG_TAPS = False
ATT_EPS = 1e-12

ENGINES = ("sync", "scalar", "vector", "tensor", "gpsimd")


class SemCtr:
    __slots__ = ("h", "n")

    def __init__(self, h):
        self.h = h
        self.n = 0


class Prog:
    def __init__(self):
        self.ops = {e: [] for e in ENGINES}

    def emit(self, eng, fn, incs=()):
        lowered = []
        for s, a in incs:
            s.n += a
            lowered.append((s.h, a))
        self.ops[eng].append((fn, tuple(lowered)))

    def wait(self, eng, s, val=None):
        v = s.n if val is None else val
        if v <= 0:
            return
        self.ops[eng].append((lambda e, h=s.h, vv=v: e.wait_ge(h, vv), ()))

    def install(self, block):
        def runner(ops):
            def body(e):
                for fn, incs in ops:
                    ins = fn(e)
                    for h, a in incs:
                        ins = ins.then_inc(h, a)
            return body
        block.sync(runner(self.ops["sync"]))
        block.scalar(runner(self.ops["scalar"]))
        block.vector(runner(self.ops["vector"]))
        block.tensor(runner(self.ops["tensor"]))
        block.gpsimd(runner(self.ops["gpsimd"]))


def build_program(limit=99, skip_gather=False):
    nc = bass.Bass("TRN2", target_bir_lowering=False)

    x_own = nc.dram_tensor("x_own", [T, D], F32, kind="ExternalInput").ap()
    x_pre = nc.dram_tensor("x_pre", [3 * T, D], F32, kind="ExternalInput").ap()
    wqT = nc.dram_tensor("wqT", [D, D], F32, kind="ExternalInput").ap()
    wkT = nc.dram_tensor("wkT", [D, D], F32, kind="ExternalInput").ap()
    wvT = nc.dram_tensor("wvT", [D, D], F32, kind="ExternalInput").ap()
    woT = nc.dram_tensor("woT", [D, D], F32, kind="ExternalInput").ap()
    wfq_in = nc.dram_tensor("wfq_bd", [P, PAIR * P], F32, kind="ExternalInput").ap()
    wfk_in = nc.dram_tensor("wfk_bd", [P, PAIR * P], F32, kind="ExternalInput").ap()
    lut_in = nc.dram_tensor("lutbc", [P, 16], F32, kind="ExternalInput").ap()
    g8 = nc.dram_tensor("g8", [D, IO], I8, kind="ExternalInput").ap()
    u8 = nc.dram_tensor("u8", [D, IO], I8, kind="ExternalInput").ap()
    d8 = nc.dram_tensor("d8", [IO, D], I8, kind="ExternalInput").ap()
    utri_in = nc.dram_tensor("utri", [P, 4 * P], F32, kind="ExternalInput").ap()
    ones_in = nc.dram_tensor("ones_in", [P, 1], F32, kind="ExternalInput").ap()
    cvec_in = nc.dram_tensor("cvec_in", [P, 2], F32, kind="ExternalInput").ap()
    ident_in = nc.dram_tensor("ident_in", [P, P], F32, kind="ExternalInput").ap()
    pmask_in = nc.dram_tensor("pmask_in", [P, NTILE], F32, kind="ExternalInput").ap()
    yout = nc.dram_tensor("yout", [T, D], F32, kind="ExternalOutput").ap()

    kd = "ExternalOutput" if DEBUG_TAPS else "Internal"
    wg_s = nc.dram_tensor("wg_s", [DC, P, IO], BF16, kind=kd).ap()
    wu_s = nc.dram_tensor("wu_s", [DC, P, IO], BF16, kind=kd).ap()
    wd_s = nc.dram_tensor("wd_s", [IOC, P, D], BF16, kind=kd).ap()
    xh_d = nc.dram_tensor("xh_d", [NTILE, P, DC * T], BF16, kind=kd).ap()
    fk_d = nc.dram_tensor("fk_d", [NTILE, NCH, P, D], BF16, kind=kd).ap()
    hm_d0 = nc.dram_tensor("hm_d0", [IOC, P, 512], BF16, kind="Internal").ap()

    pr = Prog()

    with ExitStack() as ctx:
        def sb(name, shape, dtype, st=None):
            return (st or ctx).enter_context(nc.sbuf_tensor(name, shape, dtype))

        def psum(name, shape, dtype=F32):
            return ctx.enter_context(nc.psum_tensor(name, shape, dtype))

        def sem(name):
            return SemCtr(ctx.enter_context(nc.semaphore(name)))

        eP = sem("eP")
        eA = sem("eA")
        eV = sem("eV")
        sC = sem("sC")
        sC2 = sem("sC2")
        sXW = [sem("sXWa")]
        sXH = sem("sXH")
        sWR = sem("sWR")
        sFK = [sem("sFK0"), sem("sFK1")]
        sGB = [sem(f"sGB{i}") for i in range(4)]
        sGW = [sem("sGW0"), sem("sGW1")]
        sWT = [sem(f"sWT{i}") for i in range(8)]
        sOT = [sem("sOT0"), sem("sOT1")]
        sHM = sem("sHM")
        sPL = [sem("sPL0"), sem("sPL1")]
        sPS = [sem("sPS0"), sem("sPS1")]
        eSt = sem("eSt")

        c_utri = sb("c_utri", [P, 4 * P], BF16)
        c_wfq = sb("c_wfq", [P, PAIR * P], BF16)
        c_wfk = sb("c_wfk", [P, PAIR * P], BF16)
        c_ones = sb("c_ones", [P, 1], BF16)
        c_id = sb("c_id", [P, P], BF16)
        c_pmask = sb("c_pmask", [P, NTILE], F32)
        c_cvec = sb("c_cvec", [P, 2], F32)
        c_lutb = sb("c_lutb", [P, 16], BF16)
        c_lutf = sb("c_lutf", [P, 16], F32)

        st_KV = sb("st_KV", [P, PAIR * HD], F32)
        st_KVb = sb("st_KVb", [P, PAIR * HD], BF16)
        st_k = sb("st_k", [P, PAIR], F32)
        st_ke = sb("st_ke", [P, PAIR], BF16)

        x_win = [sb("x_win0", [P, D], F32)]
        GW = 1024
        gw_bf = [sb(f"gw_bf{i}", [P, GW], BF16) for i in range(4)]
        gw_t = sb("gw_t", [P, GW], BF16)
        gw_w = [sb(f"gw_w{i}", [P, GW], BF16) for i in range(2)]
        PW = 512
        A_xh = sb("A_xh", [P, DC * T], BF16)
        A_k = sb("A_k", [P, DC * T], BF16)
        fk_scr = [sb(f"fk_scr{i}", [P, D], BF16) for i in range(2)]

        s_sum = sb("s_sum", [P, 1], F32)
        s_sq = sb("s_sq", [P, 1], F32)
        s_m = sb("s_m", [P, 1], F32)
        s_v = sb("s_v", [P, 1], F32)
        s_t1 = sb("s_t1", [P, 1], F32)
        s_a = sb("s_a", [P, 1], F32)
        s_b = sb("s_b", [P, 1], F32)
        s_sk = sb("s_sk", [P, H], F32)
        s_rsk = sb("s_rsk", [P, H], F32)
        s_rd = sb("s_rd", [P, PAIR], F32)

        pa = [psum(f"pa{i}", [P, 512], F32) for i in range(4)]
        pb = [psum(f"pb{i}", [P, 512], F32) for i in range(2)]
        pt = psum("pt", [P, PAIR * P], BF16)

        def dma(q, s, out, in_):
            pr.emit(q, lambda e, o=out, i=in_: e.dma_start(out=o, in_=i), [(s, 16)])
            return s.n

        def mm(lhsT, rhs, out, start, stop, mark=0):
            incs = [(eP, mark)] if mark else ()
            pr.emit("tensor",
                    lambda e, a=lhsT, b=rhs, o=out, s1=start, s2=stop:
                    e.matmul(o, a, b, start=s1, stop=s2), incs)

        def tr(in_, out, mark=0):
            incs = [(eP, mark)] if mark else ()
            pr.emit("tensor",
                    lambda e, i=in_, o=out: e.transpose(o, i, c_id[:, :]), incs)

        def act(out, in_, func, bias=0.0, scale=1.0, accum=None, mark=0):
            incs = [(eA, mark)] if mark else ()
            pr.emit("scalar",
                    lambda e, o=out, i=in_, f=func, b=bias, s=scale, ao=accum:
                    e.activation(o, i, f, bias=b, scale=s, accum_out=ao), incs)

        def v_op(fn, mark=0):
            incs = [(eV, mark)] if mark else ()
            pr.emit("vector", fn, incs)

        def wait_tok(eng, tok):
            pr.wait(eng, tok[0], tok[1])

        pa_free = [(eV, 0)] * 4
        pb_free = [(eV, 0)] * 2
        pt_free = (eV, 0)

        def state_add_ops(slot, masked_tile=None):
            for i in range(4):
                for s in range(2):
                    lo = s * 64
                    src = pa[i][:, :].rearrange("p (a q) -> p a q", q=P)[lo:lo + 64, :, lo:lo + 64]
                    dst = st_KV[lo:lo + 64, 4 * i * HD:(4 * i + 4) * HD].rearrange(
                        "p (a f) -> p a f", f=HD)
                    if masked_tile is None:
                        v_op(lambda e, d=dst, s2=src: e.tensor_tensor(d, d, s2, ALU.add))
                    else:
                        tcol = c_pmask[:, masked_tile:masked_tile + 1][lo:lo + 64, :]
                        v_op(lambda e, d=dst, s2=src, tc=tcol: e.scalar_tensor_tensor(
                            d, s2, tc, d, ALU.mult, ALU.add))
            for i in range(2):
                dst = st_k[:, 8 * i:8 * i + 8]
                src = pb[i][:, 0:8]
                if masked_tile is None:
                    v_op(lambda e, d=dst, s2=src: e.tensor_tensor(d, d, s2, ALU.add),
                         mark=1 if i == 1 else 0)
                else:
                    tcol = c_pmask[:, masked_tile:masked_tile + 1]
                    v_op(lambda e, d=dst, s2=src, tc=tcol: e.scalar_tensor_tensor(
                        d, s2, tc, d, ALU.mult, ALU.add), mark=1 if i == 1 else 0)

        dma("gpsimd", sC, c_utri[:, :], utri_in)
        dma("gpsimd", sC, c_wfq[:, :], wfq_in)
        dma("gpsimd", sC, c_wfk[:, :], wfk_in)
        dma("gpsimd", sC, c_ones[:, :], ones_in)
        dma("gpsimd", sC, c_id[:, :], ident_in)
        dma("gpsimd", sC, c_lutb[:, :], lut_in)
        dma("sync", sC2, c_lutf[:, :], lut_in)
        dma("sync", sC2, c_cvec[:, :], cvec_in)
        dma("sync", sC2, c_pmask[:, :], pmask_in)

        pr.wait("vector", sC)
        pr.wait("vector", sC2)
        v_op(lambda e: e.memset(st_KV[:, :], 0.0))
        v_op(lambda e: e.memset(st_k[:, :], 0.0))
        v_op(lambda e: e.memset(st_KVb[:, :], 0.0))
        v_op(lambda e: e.memset(st_ke[:, :], float(ATT_EPS)), mark=1)
        pr.wait("tensor", sC)
        pr.wait("scalar", sC)
        pr.wait("scalar", sC2)

        POOL_DOWN_CHUNKS = 30
        gjobs = []
        pjobs = []
        for gsrc, gdst, nk, width in (() if skip_gather else ((g8, wg_s, DC, IO), (u8, wu_s, DC, IO), (d8, wd_s, IOC, D))):
            for k in range(nk):
                pool = False
                step = PW if pool else GW
                c0 = 0
                while c0 < width:
                    f = min(step, width - c0)
                    (pjobs if pool else gjobs).append((gsrc, gdst, k, c0, f))
                    c0 += f

        pr.wait("vector", sC)
        pr.wait("vector", sC2)
        pr.wait("gpsimd", sC2)

        def _gather_gen():
            NS = 4
            gw_rd_tok = [(eV, 0)] * NS

            def issue(n):
                gsrc_, gdst_, k_, c0_, f_ = gjobs[n]
                b_ = n % NS
                if n >= NS:
                    wait_tok("gpsimd", gw_rd_tok[b_])
                dma("gpsimd", sGB[b_], gw_bf[b_][:, 0:f_],
                    gsrc_[k_ * P:(k_ + 1) * P, c0_:c0_ + f_])

            for n in range(min(3, len(gjobs))):
                issue(n)
            for n, (gsrc, gdst, k, c0, f) in enumerate(gjobs):
                if n + 3 < len(gjobs):
                    issue(n + 3)
                b = n % NS
                w = n % 2
                pr.wait("vector", sGB[b])
                if n >= 2:
                    pr.wait("vector", sGW[w])
                for j in range(16):
                    if j == 0:
                        v_op(lambda e, b=b, w=w, f=f: e.tensor_scalar(
                            gw_w[w][:, 0:f], gw_bf[b][:, 0:f], 0.0, c_lutf[:, 0:1],
                            ALU.is_equal, ALU.mult))
                    else:
                        v_op(lambda e, b=b, f=f, j=j: e.tensor_scalar(
                            gw_t[:, 0:f], gw_bf[b][:, 0:f], float(j), c_lutf[:, j:j + 1],
                            ALU.is_equal, ALU.mult))
                        mk = 1 if j == 15 else 0
                        v_op(lambda e, w=w, f=f: e.tensor_tensor(
                            gw_w[w][:, 0:f], gw_w[w][:, 0:f], gw_t[:, 0:f], ALU.add),
                             mark=mk)
                    yield
                gw_rd_tok[b] = (eV, eV.n)
                pr.wait("sync", eV, eV.n)
                dma("sync", sGW[w], gdst[k, :, c0:c0 + f], gw_w[w][:, 0:f])
                if n == len(gjobs) - 1 or gjobs[n + 1][1] is not gdst:
                    gather_mat_tok[id(gdst)] = (sGW[0].n, sGW[1].n)
                if gdst is wd_s and (n == len(gjobs) - 1 or gjobs[n + 1][2] != k):
                    gather_tile_tok[k] = (sGW[0].n, sGW[1].n)
                yield

        def _pool_gen():
            for n, (gsrc, gdst, k, c0, f) in enumerate(pjobs):
                b = n % 2
                pr.wait("gpsimd", sPS[b])
                dma("gpsimd", sPL[b], pw_bf[b][:, 0:f], gsrc[k * P:(k + 1) * P, c0:c0 + f])
                pr.wait("gpsimd", sPL[b])
                for j in range(16):
                    if j == 0:
                        pr.emit("gpsimd", lambda e, b=b, f=f: e.tensor_scalar(
                            pw_w[b][:, 0:f], pw_bf[b][:, 0:f], 0.0, c_lutf[:, 0:1],
                            ALU.is_equal, ALU.mult))
                    else:
                        pr.emit("gpsimd", lambda e, b=b, f=f, j=j: e.tensor_scalar(
                            pw_t[:, 0:f], pw_bf[b][:, 0:f], float(j), c_lutf[:, j:j + 1],
                            ALU.is_equal, ALU.mult))
                        pr.emit("gpsimd", lambda e, b=b, f=f: e.tensor_tensor(
                            pw_w[b][:, 0:f], pw_w[b][:, 0:f], pw_t[:, 0:f], ALU.add))
                    yield
                dma("gpsimd", sPS[b], gdst[k, :, c0:c0 + f], pw_w[b][:, 0:f])
                yield

        _pg = _pool_gen()
        _pg_done = [False]

        def pump_pool(n):
            if _pg_done[0]:
                return
            try:
                for _ in range(n):
                    next(_pg)
            except StopIteration:
                _pg_done[0] = True

        def pump_pool_all():
            while not _pg_done[0]:
                pump_pool(1000)

        _gg = _gather_gen()
        _gg_done = [False]
        gather_mat_tok = {}
        gather_tile_tok = {}

        def pump(n):
            pump_pool(max(1, n // 3))
            if _gg_done[0]:
                return
            try:
                for _ in range(n):
                    next(_gg)
            except StopIteration:
                _gg_done[0] = True

        def pump_all():
            while not _gg_done[0]:
                pump(1000)
            pump_pool_all()



        def _early_out():
            pump_all()
            pump_pool_all()
            pr.wait("sync", eV, eV.n)
            pr.wait("sync", eA, eA.n)
            dma("sync", sOT[0], yout[0:P, :], x_win[0][:, :])
            for s_ in (sOT[0], sGW[0], sGW[1], sXH, sFK[0], sFK[1]):
                pr.wait("sync", s_)
            block = ctx.enter_context(nc.Block())
            pr.install(block)

        if limit < 1:
            _early_out()
            return nc

        def x_src(t, c):
            if t == 3:
                return x_own[c * P:(c + 1) * P, :]
            return x_pre[t * T + c * P:t * T + (c + 1) * P, :]

        fk_used = [(eP, 0), (eP, 0)]
        x_user = [(eA, 0)]
        tok_kproj = [0] * NTILE
        tok_zk_last = 0

        with ExitStack() as s_p1:
            ln_dump = sb("ln_dump", [P, D], F32, s_p1)
            tm_scr = sb("tm_scr", [P, D], BF16, s_p1)
            W_res = sb("W_res", [P, DC * D], BF16, s_p1)
            for dc in range(DC):
                for sc in range(4):
                    dma("gpsimd", sWR, W_res[:, dc * D + sc * 512:dc * D + (sc + 1) * 512],
                        wkT[dc * P:(dc + 1) * P, sc * 512:(sc + 1) * 512])

            for t in range(NTILE):
                for c in range(NCH):
                    xw = x_win[c % 1]
                    sxw = sXW[c % 1]
                    wait_tok("sync", x_user[c % 1])
                    if t > 0 and c == 0:
                        pr.wait("sync", eP, tok_kproj[t - 1])
                        pr.wait("sync", sXH)
                    dma("sync", sxw, xw[:, :], x_src(t, c))
                    pr.wait("scalar", sxw)
                    act(ln_dump[:, :], xw[:, :], AF.Identity, accum=s_sum[:, :], mark=1)
                    act(ln_dump[:, :], xw[:, :], AF.Square, accum=s_sq[:, :], mark=1)
                    pr.wait("vector", eA, eA.n)
                    v_op(lambda e: e.tensor_scalar(s_m[:, :], s_sum[:, :], 1.0 / D, None, ALU.mult),
                         mark=1)
                    v_op(lambda e: e.tensor_scalar(s_t1[:, :], s_sq[:, :], 1.0 / D, None, ALU.mult),
                         mark=1)
                    pr.wait("vector", eV, eV.n)
                    v_op(lambda e: e.tensor_tensor(s_v[:, :], s_m[:, :], s_m[:, :], ALU.mult), mark=1)
                    pr.wait("vector", eV, eV.n)
                    v_op(lambda e: e.tensor_tensor(s_v[:, :], s_t1[:, :], s_v[:, :], ALU.subtract),
                         mark=1)
                    pr.wait("scalar", eV, eV.n)
                    act(s_t1[:, :], s_v[:, :], AF.Ln, bias=c_cvec[:, 0:1], mark=1)
                    pr.wait("scalar", eA, eA.n)
                    act(s_a[:, :], s_t1[:, :], AF.Exp, scale=-0.5, mark=1)
                    pr.wait("vector", eA, eA.n)
                    v_op(lambda e: e.scalar_tensor_tensor(
                        s_b[:, :], s_m[:, :], -1.0, s_a[:, :], ALU.mult, ALU.mult), mark=1)
                    pr.wait("scalar", eV, eV.n)
                    pr.wait("scalar", eA, eA.n)
                    act(tm_scr[:, :], xw[:, :], AF.Identity, bias=s_b[:, :], scale=s_a[:, :],
                        mark=1)
                    x_user[c % 1] = (eA, eA.n)
                    tok_apply = eA.n
                    wait_tok("tensor", pt_free)
                    pr.wait("tensor", eA, tok_apply)
                    for dc in range(DC):
                        tr(tm_scr[:, dc * P:(dc + 1) * P], pt[:, dc * P:(dc + 1) * P],
                           mark=1 if dc == DC - 1 else 0)
                    pr.wait("scalar", eP, eP.n)
                    act(A_xh[:, :].rearrange("p (d t) -> p d t", t=T)[:, :, c * P:(c + 1) * P],
                        pt[:, :].rearrange("p (d q) -> p d q", q=P), AF.Identity, mark=1)
                    pt_free = (eA, eA.n)
                    pump(12)
                tok_xh = eA.n

                pr.wait("sync", eA, tok_xh)
                dma("sync", sXH, xh_d[t, :, :], A_xh[:, :])

                pr.wait("tensor", sWR)
                pr.wait("tensor", eA, tok_xh)
                if t > 0:
                    pr.wait("scalar", eP, tok_zk_last)
                pp = 0
                TH = T // 2
                for oc in range(DC):
                    for th in range(2):
                        bank = pa[pp]
                        wait_tok("tensor", pa_free[pp])
                        for dc in range(DC):
                            mm(W_res[:, dc * D + oc * P: dc * D + (oc + 1) * P],
                               A_xh[:, dc * T + th * TH: dc * T + (th + 1) * TH],
                               bank[:, 0:TH], start=(dc == 0), stop=(dc == DC - 1),
                               mark=1 if dc == DC - 1 else 0)
                        tok_mm = eP.n
                        pr.wait("scalar", eP, tok_mm)
                        act(A_k[:, oc * T + th * TH: oc * T + (th + 1) * TH], bank[:, 0:TH],
                            AF.Identity, mark=1)
                        pa_free[pp] = (eA, eA.n)
                        pp = (pp + 1) % 2
                        pump(6)
                tok_kproj[t] = eP.n
                tok_kfm = eA.n

                for c in range(NCH):
                    slot = c % 2
                    pr.wait("tensor", eA, tok_kfm)
                    for i in range(4):
                        wait_tok("tensor", pa_free[i])
                    for p in range(PAIR):
                        mm(A_k[:, p * T + c * P:p * T + (c + 1) * P],
                           c_wfk[:, p * P:(p + 1) * P],
                           pa[p // 4][:, (p % 4) * P:(p % 4 + 1) * P],
                           start=True, stop=True, mark=1 if p == PAIR - 1 else 0)
                    tok_zk = eP.n
                    tok_zk_last = tok_zk
                    pr.wait("scalar", eP, tok_zk)
                    pr.wait("scalar", sFK[slot])
                    wait_tok("scalar", fk_used[slot])
                    for i in range(4):
                        act(fk_scr[slot][:, i * 512:(i + 1) * 512], pa[i][:, :], AF.Exp,
                            mark=1 if i == 3 else 0)
                    tok_exp = eA.n
                    for i in range(4):
                        pa_free[i] = (eA, tok_exp)
                    pr.wait("vector", eA, tok_exp)
                    v_op(lambda e, s=slot: e.tensor_reduce(
                        s_sk[:, :], fk_scr[s][:, :].rearrange("p (h f) -> p h f", f=HD),
                        AX, ALU.add), mark=1)
                    pr.wait("vector", eV, eV.n)
                    v_op(lambda e: e.reciprocal(s_rsk[:, :], s_sk[:, :]), mark=1)
                    pr.wait("vector", eV, eV.n)
                    v_op(lambda e, t=t: e.tensor_scalar(
                        s_rsk[:, :], s_rsk[:, :], c_pmask[:, t:t + 1], None, ALU.mult), mark=1)
                    pr.wait("vector", eV, eV.n)
                    v_op(lambda e, s=slot: e.tensor_tensor(
                        fk_scr[s][:, :].rearrange("p (h f) -> p h f", f=HD),
                        fk_scr[s][:, :].rearrange("p (h f) -> p h f", f=HD),
                        s_rsk[:, :].rearrange("p (h o) -> p h o", o=1).broadcast_to([P, H, HD]),
                        ALU.mult), mark=1)
                    pr.wait("sync", eV, eV.n)
                    dma("sync", sFK[slot], fk_d[t, c, :, :], fk_scr[slot][:, :])
                    pump(12)

        if limit < 2:
            _early_out()
            return nc
        tok_vproj_last = 0
        with ExitStack() as s_p2:
            W_res = sb("W_res2", [P, DC * D], BF16, s_p2)
            pr.wait("gpsimd", eP, eP.n)
            for dc in range(DC):
                for sc in range(4):
                    dma("gpsimd", sWR, W_res[:, dc * D + sc * 512:dc * D + (sc + 1) * 512],
                        wvT[dc * P:(dc + 1) * P, sc * 512:(sc + 1) * 512])

            tok_zk_guard = tok_zk_last
            for t in range(NTILE):
                pr.wait("sync", eP, eP.n)
                pr.wait("sync", sXH)
                dma("sync", sXH, A_xh[:, :], xh_d[t, :, :])
                pr.wait("tensor", sXH)
                pr.wait("tensor", sWR)
                if t == 0:
                    pr.wait("scalar", eP, tok_zk_guard)
                pp = 0
                for tb in range(NCH):
                    for og in range(4):
                        bank = pa[pp]
                        wait_tok("tensor", pa_free[pp])
                        for dc in range(DC):
                            mm(A_xh[:, dc * T + tb * P: dc * T + (tb + 1) * P],
                               W_res[:, dc * D + og * 512: dc * D + (og + 1) * 512],
                               bank[:, :], start=(dc == 0), stop=(dc == DC - 1),
                               mark=1 if dc == DC - 1 else 0)
                        tok_mm = eP.n
                        pr.wait("scalar", eP, tok_mm)
                        act(A_k[:, tb * D + og * 512: tb * D + (og + 1) * 512], bank[:, :],
                            AF.Identity, mark=1)
                        pa_free[pp] = (eA, eA.n)
                        pp = (pp + 1) % 2
                        pump(4)
                tok_vproj_last = eP.n
                tok_vtm = eA.n

                if t == 3:
                    break

                for c in range(NCH):
                    slot = c % 2
                    wait_tok("sync", fk_used[slot])
                    pr.wait("sync", eV, eV.n)
                    dma("sync", sFK[slot], fk_scr[slot][:, :], fk_d[t, c, :, :])
                    pr.wait("tensor", sFK[slot])
                    pr.wait("tensor", eA, tok_vtm)
                    for i in range(4):
                        wait_tok("tensor", pa_free[i])
                    for i in range(2):
                        wait_tok("tensor", pb_free[i])
                    for p in range(PAIR):
                        mm(fk_scr[slot][:, p * P:(p + 1) * P],
                           A_k[:, c * D + p * P:c * D + (p + 1) * P],
                           pa[p // 4][:, (p % 4) * P:(p % 4 + 1) * P],
                           start=True, stop=True)
                        mm(fk_scr[slot][:, p * P:(p + 1) * P], c_ones[:, :],
                           pb[p // 8][:, (p % 8):(p % 8) + 1],
                           start=True, stop=True, mark=1 if p == PAIR - 1 else 0)
                    tok_dmm = eP.n
                    fk_used[slot] = (eP, tok_dmm)
                    pr.wait("vector", eP, tok_dmm)
                    state_add_ops(slot, masked_tile=t)
                    st_tok = eV.n
                    for i in range(4):
                        pa_free[i] = (eV, st_tok)
                    for i in range(2):
                        pb_free[i] = (eV, st_tok)
                    pump(12)

            v_op(lambda e: e.tensor_copy(st_KVb[:, :], st_KV[:, :]))
            v_op(lambda e: e.tensor_scalar(st_ke[:, :], st_k[:, :], float(ATT_EPS), None,
                                           ALU.add), mark=1)
            tok_state = eV.n

        if limit < 3:
            _early_out()
            return nc
        tok_scan = 0
        with ExitStack() as s_aq:
            A_q = sb("A_q", [P, DC * T], BF16, s_aq)

            with ExitStack() as s_q:
                W_res = sb("W_res3", [P, DC * D], BF16, s_q)
                pr.wait("gpsimd", eP, eP.n)
                for dc in range(DC):
                    for sc in range(4):
                        dma("gpsimd", sWR, W_res[:, dc * D + sc * 512:dc * D + (sc + 1) * 512],
                            wqT[dc * P:(dc + 1) * P, sc * 512:(sc + 1) * 512])
                pr.wait("sync", eP, eP.n)
                dma("sync", sXH, A_xh[:, :], xh_d[3, :, :])
                pr.wait("tensor", sXH)
                pr.wait("tensor", sWR)
                pp = 0
                TH = T // 2
                for oc in range(DC):
                    for th in range(2):
                        bank = pa[pp]
                        wait_tok("tensor", pa_free[pp])
                        for dc in range(DC):
                            mm(W_res[:, dc * D + oc * P: dc * D + (oc + 1) * P],
                               A_xh[:, dc * T + th * TH: dc * T + (th + 1) * TH],
                               bank[:, 0:TH], start=(dc == 0), stop=(dc == DC - 1),
                               mark=1 if dc == DC - 1 else 0)
                        tok_mm = eP.n
                        pr.wait("scalar", eP, tok_mm)
                        act(A_q[:, oc * T + th * TH: oc * T + (th + 1) * TH], bank[:, 0:TH],
                            AF.Identity, mark=1)
                        pa_free[pp] = (eA, eA.n)
                        pp = (pp + 1) % 2
                        pump(8)
                tok_qproj = eP.n
                tok_qfm = eA.n
            if limit < 2.4:
                _early_out()
                return nc

            with ExitStack() as s_scan:
                fq_bd = sb("fq_bd", [P, 2 * PAIR * P], BF16, s_scan)
                fkf_scr = sb("fkf_scr", [P, PAIR * P], BF16, s_scan)
                stm_scr = sb("stm_scr", [P, 16 * P], BF16, s_scan)
                ytmp = sb("ytmp", [P, 1024], BF16, s_scan)

                pr.wait("tensor", eA, tok_qfm)
                pr.wait("tensor", eV, tok_state)
                pr.wait("vector", eP, tok_qproj)
                v_op(lambda e: e.memset(fq_bd[:, :], 0.0), mark=1)
                pr.wait("scalar", eV, eV.n)
                kvb_tok = (eV, tok_state)
                ytok = 0
                for c in range(NCH):
                    slot = c % 2
                    wait_tok("sync", fk_used[slot])
                    pr.wait("sync", eV, eV.n)
                    dma("sync", sFK[slot], fk_scr[slot][:, :], fk_d[3, c, :, :])

                    wait_tok("tensor", kvb_tok)
                    for i in range(4):
                        wait_tok("tensor", pa_free[i])
                    for p in range(PAIR):
                        mm(c_wfq[:, p * P:(p + 1) * P],
                           A_q[:, p * T + c * P:p * T + (c + 1) * P],
                           pa[p // 4][:, (p % 4) * P:(p % 4 + 1) * P],
                           start=True, stop=True, mark=1 if p == PAIR - 1 else 0)
                    tok_zq = eP.n
                    pr.wait("scalar", eP, tok_zq)
                    for i in range(4):
                        act(fq_bd[0:64, i * 1024:(i + 1) * 1024].rearrange(
                                "p (a c) -> p a c", c=256)[:, :, 0:128],
                            pa[i][0:64, :].rearrange("p (a c) -> p a c", c=128),
                            AF.Exp)
                        act(fq_bd[64:128, i * 1024:(i + 1) * 1024].rearrange(
                                "p (a c) -> p a c", c=256)[:, :, 128:256],
                            pa[i][64:128, :].rearrange("p (a c) -> p a c", c=128),
                            AF.Exp, mark=1 if i == 3 else 0)
                    tok_fq = eA.n
                    for i in range(4):
                        pa_free[i] = (eA, tok_fq)
                    if limit < 2.5:
                        continue

                    wait_tok("tensor", pt_free)
                    pr.wait("tensor", sFK[slot])
                    for p in range(PAIR):
                        tr(fk_scr[slot][:, p * P:(p + 1) * P], pt[:, p * P:(p + 1) * P],
                           mark=1 if p == PAIR - 1 else 0)
                    tok_fkT = eP.n
                    pr.wait("scalar", eP, tok_fkT)
                    act(fkf_scr[:, 0:1024], pt[:, 0:1024], AF.Identity)
                    act(fkf_scr[:, 1024:2048], pt[:, 1024:2048], AF.Identity, mark=1)
                    tok_fkf = eA.n
                    pt_free = (eA, tok_fkf)
                    if limit < 2.6:
                        continue

                    for g in range(0 if SKIP_ST else 2):
                        h0 = g * 16
                        pr.wait("tensor", eA, tok_fkf)
                        for i in range(4):
                            wait_tok("tensor", pa_free[i])
                        for lp in range(8):
                            p_ = g * 8 + lp
                            mm(fkf_scr[:, p_ * P:(p_ + 1) * P],
                               fq_bd[:, p_ * 256:(p_ + 1) * 256],
                               pa[lp // 2][:, (lp % 2) * 256:(lp % 2 + 1) * 256],
                               start=True, stop=True, mark=1 if lp == 7 else 0)
                        tok_st = eP.n
                        if SKIP_MASK:
                            for i in range(4):
                                pa_free[i] = (eP, tok_st)
                            continue
                        pump(8)
                        pr.wait("vector", eP, tok_st)
                        for i in range(4):
                            v_op(lambda e, i=i: e.tensor_tensor(
                                stm_scr[:, i * 512:(i + 1) * 512], pa[i][:, :],
                                c_utri[:, :], ALU.mult), mark=1 if i == 3 else 0)
                        tok_stm = eV.n
                        for i in range(4):
                            pa_free[i] = (eV, tok_stm)
                        if limit < 2.7:
                            continue

                        if SKIP_Y:
                            continue
                        pr.wait("tensor", eV, tok_stm)
                        for i in range(2):
                            wait_tok("tensor", pb_free[i])
                        wait_tok("tensor", pa_free[0])
                        for i in range(16):
                            h = h0 + i
                            p_, s_ = h // 2, h % 2
                            fqb = fq_bd[:, p_ * 256 + s_ * 128: p_ * 256 + (s_ + 1) * 128]
                            mm(stm_scr[:, i * P:(i + 1) * P],
                               A_k[:, c * D + h * HD:c * D + (h + 1) * HD],
                               pb[i // 8][:, (i % 8) * HD:(i % 8 + 1) * HD],
                               start=True, stop=False)
                            mm(fqb, st_KVb[:, p_ * HD:(p_ + 1) * HD],
                               pb[i // 8][:, (i % 8) * HD:(i % 8 + 1) * HD],
                               start=False, stop=True)
                            mm(stm_scr[:, i * P:(i + 1) * P], c_ones[:, :],
                               pa[0][:, i:i + 1], start=True, stop=False)
                            mm(fqb, st_ke[:, p_:p_ + 1],
                               pa[0][:, i:i + 1], start=False, stop=True,
                               mark=1 if i == 15 else 0)
                        tok_y = eP.n
                        pump(8)
                        pr.wait("vector", eP, tok_y)
                        v_op(lambda e: e.reciprocal(s_rd[:, :], pa[0][:, 0:16]), mark=1)
                        pr.wait("vector", eV, eV.n)
                        for i in range(2):
                            v_op(lambda e, i=i: e.tensor_tensor(
                                ytmp[:, i * 512:(i + 1) * 512].rearrange(
                                    "p (a f) -> p a f", f=HD),
                                pb[i][:, :].rearrange("p (a f) -> p a f", f=HD),
                                s_rd[:, 8 * i:8 * i + 8].rearrange(
                                    "p (a o) -> p a o", o=1).broadcast_to([P, 8, HD]),
                                ALU.mult), mark=1 if i == 1 else 0)
                        tok_yd = eV.n
                        pa_free[0] = (eV, tok_yd)
                        pb_free[0] = (eV, tok_yd)
                        pb_free[1] = (eV, tok_yd)
                        if limit < 2.8:
                            continue

                        wait_tok("tensor", pt_free)
                        pr.wait("tensor", eV, tok_yd)
                        for q_ in range(8):
                            tr(ytmp[:, q_ * P:(q_ + 1) * P],
                               pt[:, (g * 8 + q_) * P:(g * 8 + q_ + 1) * P],
                               mark=1 if q_ == 7 else 0)
                        tok_yT = eP.n
                        pr.wait("scalar", eP, tok_yT)
                        act(A_xh[:, :].rearrange("p (d t) -> p d t", t=T)[
                                :, g * 8:(g + 1) * 8, c * P:(c + 1) * P],
                            pt[:, g * 1024:(g + 1) * 1024].rearrange("p (d q) -> p d q", q=P),
                            AF.Identity, mark=1)
                        if g == 1:
                            pt_free = (eA, eA.n)
                        ytok = eA.n
                        pump(10)

                    if limit < 2.85:
                        continue
                    for i in range(4):
                        wait_tok("tensor", pa_free[i])
                    for i in range(2):
                        wait_tok("tensor", pb_free[i])
                    for p in range(PAIR):
                        mm(fk_scr[slot][:, p * P:(p + 1) * P],
                           A_k[:, c * D + p * P:c * D + (p + 1) * P],
                           pa[p // 4][:, (p % 4) * P:(p % 4 + 1) * P],
                           start=True, stop=True)
                        mm(fk_scr[slot][:, p * P:(p + 1) * P], c_ones[:, :],
                           pb[p // 8][:, (p % 8):(p % 8) + 1],
                           start=True, stop=True, mark=1 if p == PAIR - 1 else 0)
                    tok_dmm = eP.n
                    fk_used[slot] = (eP, tok_dmm)
                    if limit < 2.9:
                        for i in range(4):
                            pa_free[i] = (eP, tok_dmm)
                        for i in range(2):
                            pb_free[i] = (eP, tok_dmm)
                        continue
                    pump(8)
                    pr.wait("vector", eP, tok_dmm)
                    state_add_ops(slot, masked_tile=3)
                    st_tok2 = eV.n
                    pr.wait("scalar", eV, st_tok2)
                    act(st_KVb[:, :], st_KV[:, :], AF.Identity)
                    pr.emit("scalar",
                            lambda e: e.activation(st_ke[:, :], st_k[:, :], AF.Identity,
                                                   bias=c_cvec[:, 1:2]),
                            [(eSt, 1)])
                    if limit >= 2.95 and c < STATE_CHUNKS:
                        kvb_tok = (eSt, eSt.n)
                    for i in range(4):
                        pa_free[i] = (eV, st_tok2)
                    for i in range(2):
                        pb_free[i] = (eV, st_tok2)
                    pump(16)
                tok_scan = ytok

        if limit < 4:
            _early_out()
            return nc
        R2 = sb("R2", [P, NCH * D], BF16)
        with ExitStack() as s_o:
            W_res = sb("W_res4", [P, DC * D], BF16, s_o)
            pr.wait("gpsimd", eP, eP.n)
            for dc in range(DC):
                for sc in range(4):
                    dma("gpsimd", sWR, W_res[:, dc * D + sc * 512:dc * D + (sc + 1) * 512],
                        woT[dc * P:(dc + 1) * P, sc * 512:(sc + 1) * 512])
            pr.wait("tensor", sWR)
            pr.wait("tensor", eA, tok_scan)

            x_user2 = list(x_user)
            pp = 0
            for tb in range(NCH):
                xw = x_win[tb % 1]
                sxw = sXW[tb % 1]
                wait_tok("sync", x_user2[tb % 1])
                dma("sync", sxw, xw[:, :], x_own[tb * P:(tb + 1) * P, :])
                for og in range(4):
                    bank = pa[pp]
                    wait_tok("tensor", pa_free[pp])
                    for dc in range(DC):
                        mm(A_xh[:, dc * T + tb * P: dc * T + (tb + 1) * P],
                           W_res[:, dc * D + og * 512: dc * D + (og + 1) * 512],
                           bank[:, :], start=(dc == 0), stop=(dc == DC - 1),
                           mark=1 if dc == DC - 1 else 0)
                    tok_mm = eP.n
                    pump(8)
                    pr.wait("vector", eP, tok_mm)
                    pr.wait("vector", sxw)
                    v_op(lambda e, tb=tb, og=og, bank=bank, xw=xw: e.scalar_tensor_tensor(
                        R2[:, tb * D + og * 512: tb * D + (og + 1) * 512],
                        xw[:, og * 512:(og + 1) * 512], 1.0, bank[:, :],
                        ALU.mult, ALU.add), mark=1)
                    pa_free[pp] = (eV, eV.n)
                    pp = (pp + 1) % 2
                x_user2[tb % 1] = (eV, eV.n)
            tok_r2 = eV.n

        with ExitStack() as s_ln2:
            ln_dump = sb("ln_dump2", [P, D], F32, s_ln2)
            tm_scr = sb("tm_scr2", [P, D], BF16, s_ln2)
            pr.wait("vector", eV, tok_r2)
            pr.wait("scalar", eV, tok_r2)
            for c in range(NCH):
                pump(6)
                act(ln_dump[:, :], R2[:, c * D:(c + 1) * D], AF.Identity, accum=s_sum[:, :],
                    mark=1)
                act(ln_dump[:, :], R2[:, c * D:(c + 1) * D], AF.Square, accum=s_sq[:, :],
                    mark=1)
                pr.wait("vector", eA, eA.n)
                v_op(lambda e: e.tensor_scalar(s_m[:, :], s_sum[:, :], 1.0 / D, None, ALU.mult),
                     mark=1)
                v_op(lambda e: e.tensor_scalar(s_t1[:, :], s_sq[:, :], 1.0 / D, None, ALU.mult),
                     mark=1)
                pr.wait("vector", eV, eV.n)
                v_op(lambda e: e.tensor_tensor(s_v[:, :], s_m[:, :], s_m[:, :], ALU.mult), mark=1)
                pr.wait("vector", eV, eV.n)
                v_op(lambda e: e.tensor_tensor(s_v[:, :], s_t1[:, :], s_v[:, :], ALU.subtract),
                     mark=1)
                pr.wait("scalar", eV, eV.n)
                act(s_t1[:, :], s_v[:, :], AF.Ln, bias=c_cvec[:, 0:1], mark=1)
                pr.wait("scalar", eA, eA.n)
                act(s_a[:, :], s_t1[:, :], AF.Exp, scale=-0.5, mark=1)
                pr.wait("vector", eA, eA.n)
                v_op(lambda e: e.scalar_tensor_tensor(
                    s_b[:, :], s_m[:, :], -1.0, s_a[:, :], ALU.mult, ALU.mult), mark=1)
                pr.wait("scalar", eV, eV.n)
                pr.wait("scalar", eA, eA.n)
                act(tm_scr[:, :], R2[:, c * D:(c + 1) * D], AF.Identity,
                    bias=s_b[:, :], scale=s_a[:, :], mark=1)
                tok_apply = eA.n
                wait_tok("tensor", pt_free)
                pr.wait("tensor", eA, tok_apply)
                for dc in range(DC):
                    tr(tm_scr[:, dc * P:(dc + 1) * P], pt[:, dc * P:(dc + 1) * P],
                       mark=1 if dc == DC - 1 else 0)
                pr.wait("scalar", eP, eP.n)
                act(A_xh[:, :].rearrange("p (d t) -> p d t", t=T)[:, :, c * P:(c + 1) * P],
                    pt[:, :].rearrange("p (d q) -> p d q", q=P), AF.Identity, mark=1)
                pt_free = (eA, eA.n)
            tok_xh2 = eA.n

        if limit < 5:
            _early_out()
            return nc
        with ExitStack() as s_ffn:
            h_mid = sb("h_mid", [P, IOC * 512], BF16, s_ffn)
            silu_sb = [sb(f"silu{i}", [P, 512], BF16, s_ffn) for i in range(4)]
            htm_sb = [sb(f"htm{i}", [P, 512], BF16, s_ffn) for i in range(4)]
            wt = [sb(f"wt{i}", [P, 512], BF16, s_ffn) for i in range(6)]
            hmt = [sb(f"hmt{i}", [P, 512], BF16, s_ffn) for i in range(4)]
            otile = [sb(f"ot{i}", [P, 512], F32, s_ffn) for i in range(2)]

            def wait_mat(dst):
                while gather_mat_tok.get(id(dst)) is None and not _gg_done[0]:
                    pump(31)
                tok = gather_mat_tok.get(id(dst)) or (sGW[0].n, sGW[1].n)
                for eng in ("scalar", "tensor"):
                    pr.wait(eng, sGW[0], tok[0])
                    pr.wait(eng, sGW[1], tok[1])
            pr.wait("vector", eV, eV.n)
            pr.wait("scalar", eA, tok_xh2)
            pr.wait("tensor", eA, tok_xh2)

            wt_used = [(eP, 0)] * 6
            hm_used = [(eP, 0)] * 4
            sHX = [sPL[0], sPL[1], sPS[0], sPS[1]]
            hm_tok = {}
            tok_hmid_tp = [0, 0]

            IO_G = [(gi * 512, min(512, IO - gi * 512)) for gi in range((IO + 511) // 512)]

            for tp in range(2):
                if tp == 0:
                    wait_mat(wg_s)
                for gi, (c0, gw) in enumerate(IO_G):
                    pump(40)
                    for dc in range(DC):
                        ws = dc % 6
                        wait_tok("scalar", wt_used[ws])
                        dma("scalar", sWT[ws], wt[ws][:, 0:gw], wg_s[dc, :, c0:c0 + gw])
                        if dc == 0:
                            for i in range(4):
                                wait_tok("tensor", pa_free[i])
                        pr.wait("tensor", sWT[ws])
                        for tb in range(4):
                            mm(A_xh[:, dc * T + tp * 512 + tb * P:
                                    dc * T + tp * 512 + (tb + 1) * P],
                               wt[ws][:, 0:gw], pa[tb][:, 0:gw],
                               start=(dc == 0), stop=(dc == DC - 1),
                               mark=1 if tb == 3 else 0)
                        wt_used[ws] = (eP, eP.n)
                    tok_gate = eP.n
                    pr.wait("scalar", eP, tok_gate)
                    for tb in range(4):
                        act(silu_sb[tb][:, 0:gw], pa[tb][:, 0:gw], AF.Silu,
                            mark=1 if tb == 3 else 0)
                    tok_silu = eA.n
                    if tp == 0 and gi == 0:
                        wait_mat(wu_s)
                    pump(40)
                    pr.wait("tensor", eA, tok_silu)
                    for dc in range(DC):
                        ws = dc % 6
                        wait_tok("scalar", wt_used[ws])
                        dma("scalar", sWT[ws], wt[ws][:, 0:gw], wu_s[dc, :, c0:c0 + gw])
                        pr.wait("tensor", sWT[ws])
                        for tb in range(4):
                            mm(A_xh[:, dc * T + tp * 512 + tb * P:
                                    dc * T + tp * 512 + (tb + 1) * P],
                               wt[ws][:, 0:gw], pa[tb][:, 0:gw],
                               start=(dc == 0), stop=(dc == DC - 1),
                               mark=1 if tb == 3 else 0)
                        wt_used[ws] = (eP, eP.n)
                    tok_up = eP.n
                    pr.wait("vector", eP, tok_up)
                    pr.wait("vector", eA, tok_silu)
                    for tb in range(4):
                        v_op(lambda e, tb=tb, gw=gw: e.tensor_tensor(
                            htm_sb[tb][:, 0:gw], pa[tb][:, 0:gw], silu_sb[tb][:, 0:gw],
                            ALU.mult), mark=1 if tb == 3 else 0)
                    tok_htm = eV.n
                    for i in range(4):
                        pa_free[i] = (eV, tok_htm)
                    wait_tok("tensor", pt_free)
                    pr.wait("tensor", eV, tok_htm)
                    nio = gw // P
                    for tb in range(4):
                        for s2 in range(nio):
                            tr(htm_sb[tb][:, s2 * P:(s2 + 1) * P],
                               pt[:, (tb * 4 + s2) * P:(tb * 4 + s2 + 1) * P],
                               mark=1 if (tb == 3 and s2 == nio - 1) else 0)
                    tok_hT = eP.n
                    if tp == 1:
                        pr.wait("scalar", sHM, hm_tok[gi])
                    pr.wait("scalar", eP, tok_hT)
                    for tb in range(4):
                        for s2 in range(nio):
                            ioc = gi * 4 + s2
                            act(h_mid[:, ioc * 512 + tb * P: ioc * 512 + (tb + 1) * P],
                                pt[:, (tb * 4 + s2) * P:(tb * 4 + s2 + 1) * P],
                                AF.Identity, mark=1 if (tb == 3 and s2 == nio - 1) else 0)
                    pt_free = (eA, eA.n)
                    if tp == 0:
                        pr.wait("sync", eA, eA.n)
                        for s2 in range(nio):
                            pr.wait("sync", sHM)
                            dma("sync", sHM, hm_d0[gi * 4 + s2, :, :],
                                h_mid[:, (gi * 4 + s2) * 512: (gi * 4 + s2 + 1) * 512])
                        hm_tok[gi] = sHM.n
                tok_hmid_tp[tp] = eA.n

            pr.wait("sync", sHM)
            for tp in range(2):
                pr.wait("tensor", eA, tok_hmid_tp[tp])
                for hg in range(4):
                    pump(8)
                    for k in range(IOC):
                        if tp == 0:
                            while gather_tile_tok.get(k) is None and not _gg_done[0]:
                                pump(31)
                        elif k % 4 == 0:
                            pump(3)
                        ws = k % 6
                        wait_tok("scalar", wt_used[ws])
                        if tp == 0:
                            tk = gather_tile_tok.get(k, (sGW[0].n, sGW[1].n))
                            pr.wait("scalar", sGW[0], tk[0])
                            pr.wait("scalar", sGW[1], tk[1])
                        dma("scalar", sWT[ws], wt[ws][:, :], wd_s[k, :, hg * 512:(hg + 1) * 512])
                        if tp == 0:
                            w4 = k % 4
                            wait_tok("sync", hm_used[w4])
                            dma("sync", sHX[w4], hmt[w4][:, :], hm_d0[k, :, :])
                        if k == 0:
                            for i in range(4):
                                wait_tok("tensor", pa_free[i])
                        pr.wait("tensor", sWT[ws])
                        if tp == 0:
                            pr.wait("tensor", sHX[w4])
                        for tb in range(4):
                            mm(hmt[w4][:, tb * P:(tb + 1) * P] if tp == 0 else
                               h_mid[:, k * 512 + tb * P: k * 512 + (tb + 1) * P],
                               wt[ws][:, :], pa[tb][:, :],
                               start=(k == 0), stop=(k == IOC - 1),
                               mark=1 if tb == 3 else 0)
                        wt_used[ws] = (eP, eP.n)
                        if tp == 0:
                            hm_used[w4] = (eP, eP.n)
                    tok_down = eP.n
                    pr.wait("vector", eP, tok_down)
                    for tb in range(4):
                        os_ = tb % 2
                        tc = tp * 4 + tb
                        pr.wait("vector", sOT[os_])
                        v_op(lambda e, tc=tc, hg=hg, os_=os_, tb=tb: e.scalar_tensor_tensor(
                            otile[os_][:, :],
                            R2[:, tc * D + hg * 512: tc * D + (hg + 1) * 512], 1.0,
                            pa[tb][:, :], ALU.mult, ALU.add), mark=1)
                        tok_ot = eV.n
                        pr.wait("sync", eV, tok_ot)
                        dma("sync", sOT[os_], yout[tc * P:(tc + 1) * P, hg * 512:(hg + 1) * 512],
                            otile[os_][:, :])
                    for i in range(4):
                        pa_free[i] = (eV, eV.n)

            pr.wait("sync", sOT[0])
            pr.wait("sync", sOT[1])

        block = ctx.enter_context(nc.Block())
        pr.install(block)

    return nc


_NC_CACHE = [None]


def _host_pack(inputs):
    f32 = np.float32
    x = np.asarray(inputs["L0_attn_hidden_states"], f32).reshape(2 * 4096, D)
    wq = np.ascontiguousarray(np.asarray(inputs["L0_attn_Wq"], f32).T)
    wk = np.ascontiguousarray(np.asarray(inputs["L0_attn_Wk"], f32).T)
    wv = np.ascontiguousarray(np.asarray(inputs["L0_attn_Wv"], f32).T)
    wo = np.ascontiguousarray(np.asarray(inputs["L0_attn_Wo"], f32).T)

    def blockdiag(wf):
        wf = np.asarray(wf, f32)
        bd = np.zeros((P, PAIR * P), f32)
        for p in range(PAIR):
            for s in range(2):
                bd[s * 64:(s + 1) * 64, p * P + s * 64: p * P + (s + 1) * 64] = wf[2 * p + s]
        return bd

    wfq_bd = blockdiag(inputs["L0_attn_Wfq"])
    wfk_bd = blockdiag(inputs["L0_attn_Wfk"])
    lut = np.asarray(inputs["L0_ffn_lut"], f32)
    lutbc = np.ascontiguousarray(np.broadcast_to(lut, (P, 16)))

    def idx8T(a):
        aT = np.ascontiguousarray(np.asarray(a, np.int32).T)
        return np.ascontiguousarray(aT.view(np.int8).reshape(aT.shape[0], aT.shape[1], 4)[:, :, 0])

    g8 = idx8T(inputs["L0_ffn_gate_idx"])
    u8 = idx8T(inputs["L0_ffn_up_idx"])
    d8 = idx8T(inputs["L0_ffn_down_idx"])

    utri = np.ascontiguousarray(np.tile(np.triu(np.ones((P, P), f32)), (1, 4)))
    ones_col = np.ones((P, 1), f32)
    cvec = np.zeros((P, 2), f32); cvec[:, 0] = LN_EPS; cvec[:, 1] = ATT_EPS
    ident = np.eye(P, dtype=f32)

    in_maps = []
    for core in range(8):
        seq, qpos = core // 4, core % 4
        pm = np.zeros((P, NTILE), f32)
        for t in range(3):
            if t < qpos:
                pm[:, t] = 1.0
        pm[:, 3] = 1.0
        in_maps.append({
            "x_own": np.ascontiguousarray(x[seq * 4096 + qpos * T: seq * 4096 + (qpos + 1) * T]),
            "x_pre": np.ascontiguousarray(x[seq * 4096: seq * 4096 + 3 * T]),
            "wqT": wq, "wkT": wk, "wvT": wv, "woT": wo,
            "wfq_bd": wfq_bd, "wfk_bd": wfk_bd, "lutbc": lutbc,
            "g8": g8, "u8": u8, "d8": d8,
            "utri": utri, "ones_in": ones_col, "ident_in": ident,
            "cvec_in": cvec,
            "pmask_in": pm,
        })
    return in_maps


def kernel(**inputs):
    if _NC_CACHE[0] is None:
        _NC_CACHE[0] = build_program()
    nc = _NC_CACHE[0]
    in_maps = _host_pack(inputs)
    res = run_bass_kernel_spmd(nc, in_maps, core_ids=list(range(8)))
    out = np.concatenate([np.asarray(r["yout"], np.float32) for r in res.results], axis=0)
    return out.reshape(2, 4096, D)


if __name__ == "__main__":
    nc = build_program()
    n = sum(1 for f in nc.m.functions for b in f.blocks for _ in b.instructions)
    print("built ok, instructions:", n)
